# version 16
# baseline (speedup 1.0000x reference)
import numpy as np
import ml_dtypes
from contextlib import ExitStack
import concourse.bass as bass
import concourse.mybir as mybir
from concourse.bass_utils import run_bass_kernel_spmd

F32 = mybir.dt.float32
BF16 = mybir.dt.bfloat16
AF = mybir.ActivationFunctionType
ALU = mybir.AluOpType
AX = mybir.AxisListType

D = 1024
KC = 8
NUM_NOD = 6

ENGINES = ("tensor", "scalar", "vector", "gpsimd", "sync")


class Prog:
    def __init__(self):
        self.q = {e: [] for e in ENGINES}
        self.sems = []
        self.cnt = {}
        self.hw = {}
        self.tag = "init"
        self.pe_tags = []

    def _reg(self, name):
        if name not in self.cnt:
            self.cnt[name] = 0
            self.sems.append(name)

    def op(self, eng, fn, waits=(), inc=None, inc_n=None):
        if eng == "tensor":
            self.pe_tags.append(self.tag)
        byname = {}
        for t in waits:
            if t is None:
                continue
            s, v = t
            if v <= 0:
                continue
            if byname.get(s, 0) < v:
                byname[s] = v
        for s, v in byname.items():
            key = (eng, s)
            if self.hw.get(key, 0) >= v:
                continue
            self.hw[key] = v
            self.q[eng].append(("w", s, v))
        tok = None
        if inc is not None:
            self._reg(inc)
            n = inc_n if inc_n is not None else 1
            self.cnt[inc] += n
            tok = (inc, self.cnt[inc])
            self.q[eng].append(("o", fn, inc, n))
        else:
            self.q[eng].append(("o", fn, None, 0))
        return tok

    def finalize(self, nc, block, stack):
        sems = {s: stack.enter_context(nc.semaphore(s)) for s in self.sems}
        for eng_name in ENGINES:
            items = list(self.q[eng_name])

            def make(items):
                def emit(eng):
                    for it in items:
                        if it[0] == "w":
                            eng.wait_ge(sems[it[1]], it[2])
                        else:
                            ins = it[1](eng)
                            if it[2] is not None:
                                ins.then_inc(sems[it[2]], it[3])
                return emit

            getattr(block, eng_name)(make(items))


def ffn_dag_meta():
    meta = []
    lind = 0
    for c in range(NUM_NOD):
        nsrc = min(c + 1, 5)
        snode = c - nsrc
        meta.append(dict(c=c, eq=lind + nsrc - 1, ek=lind, ev=lind + nsrc // 2,
                         in_q=c - 1, in_k=snode, in_v=snode + nsrc // 2))
        lind += nsrc
    return meta


def build_nc(T=2048, debug=False):
    TC = T // 128
    PW = T // 2
    NQ = T // PW
    SEG = min(512, T)
    NSEG_T = T // SEG
    PSEG = min(512, PW)
    NSEG_P = PW // PSEG
    NH = min(1024, T)
    NHA = T // NH

    nc = bass.Bass("TRN2", target_bir_lowering=False)

    x_in = nc.dram_tensor("x", [T, D], F32, kind="ExternalInput")
    out_t = nc.dram_tensor("out", [T, D], F32, kind="ExternalOutput")
    W = {}
    for nm, shp in [
        ("aWq", [D, D]), ("aWk", [D, D]), ("aWv", [D, D]), ("aWo", [D, D]),
        ("abq", [D]), ("abk", [D]), ("abv", [D]), ("abo", [D]),
        ("agamma", [D]), ("abeta", [D]), ("few", [6, 3]), ("fact", [6]),
        ("fWe", [20, D, D]), ("fbe", [20, D]), ("fWq", [6, D, D]),
        ("fWk", [6, D, D]), ("fWv", [6, D, D]), ("fWo", [6, D, D]),
        ("fln_g", [D]), ("fln_b", [D]),
    ]:
        W[nm] = nc.dram_tensor(nm, shp, F32, kind="ExternalInput")

    xn_sp = nc.dram_tensor("xn_sp", [T, D], BF16, kind="Internal")
    n_sp = {c: nc.dram_tensor(f"n{c}_sp", [D, T], BF16, kind="Internal")
            for c in (0, 1, 2)}
    n5_sp = nc.dram_tensor("n5_sp", [D, T], BF16, kind="Internal")
    y_sp = nc.dram_tensor("y_sp", [D, T], BF16, kind="Internal")
    dbg = {}
    if debug:
        for c in (0, 1):
            dbg[f"q{c}"] = nc.dram_tensor(f"dbg_q{c}", [128, T], BF16,
                                          kind="ExternalOutput")
            dbg[f"k{c}"] = nc.dram_tensor(f"dbg_k{c}", [128, T], BF16,
                                          kind="ExternalOutput")
            dbg[f"ctx{c}"] = nc.dram_tensor(f"dbg_ctx{c}", [128, T], BF16,
                                            kind="ExternalOutput")
            dbg[f"w{c}"] = nc.dram_tensor(f"dbg_w{c}", [1, 1024], F32,
                                          kind="ExternalOutput")
            dbg[f"vv{c}"] = nc.dram_tensor(f"dbg_vv{c}", [128, 1024], BF16,
                                           kind="ExternalOutput")

    P = Prog()
    stack = ExitStack()
    e = stack.enter_context

    slot = [e(nc.sbuf_tensor(f"slot{i}", [128, KC * T], BF16)) for i in range(4)]
    wbuf = [e(nc.sbuf_tensor(f"wbuf{i}", [128, KC * 1024], BF16)) for i in range(2)]
    wstage = [e(nc.sbuf_tensor(f"wstage{i}", [128, 1024], F32)) for i in range(2)]
    qbuf = e(nc.sbuf_tensor("qbuf", [128, T], BF16))
    kbuf = e(nc.sbuf_tensor("kbuf", [128, T], BF16))
    attb = [e(nc.sbuf_tensor(f"attb{i}", [128, max(PW, 1024)], BF16))
            for i in range(4)]
    sq_scr = attb[2]
    wb_sb = e(nc.sbuf_tensor("wb_sb", [128, PW], BF16))
    fio = [e(nc.sbuf_tensor(f"fio{i}", [128, 1024], F32)) for i in range(3)]
    ones_bf = e(nc.sbuf_tensor("ones_bf", [128, 1], BF16))
    ones_bfr = e(nc.sbuf_tensor("ones_bfr", [1, 128], BF16))
    ones_f32r = e(nc.sbuf_tensor("ones_f32r", [1, 128], F32))
    ident_f32 = e(nc.sbuf_tensor("ident_f32", [128, 128], F32))
    iscaled = e(nc.sbuf_tensor("iscaled", [128, 128], BF16))
    t128 = e(nc.sbuf_tensor("t128", [128, 128], BF16))
    beT = e(nc.sbuf_tensor("beT", [128, 160], F32))
    vecT = e(nc.sbuf_tensor("vecT", [128, 64], F32))
    vec_stage = e(nc.sbuf_tensor("vec_stage", [128, 128], F32))
    ewrow = e(nc.sbuf_tensor("ewrow", [1, 32], F32))
    scal = e(nc.sbuf_tensor("scal", [128, 32], F32))
    tiny = e(nc.sbuf_tensor("tiny", [1, 16], F32))
    lrow = fio[2]
    stat = e(nc.sbuf_tensor("stat", [128, 8], F32))
    gamT = e(nc.sbuf_tensor("gamT", [128, 2 * 1024], BF16))
    negM = e(nc.sbuf_tensor("negM", [128, 16], F32))

    bvrow = lrow.ap().bitcast(BF16)

    ps = [e(nc.psum_tensor(f"ps{i}", [128, 1024], F32)) for i in range(4)]
    ps_last = {0: None, 1: None, 2: None, 3: None}

    slot_w = [[None] * max(KC, TC) for _ in range(4)]
    slot_r = [[] for _ in range(4)]

    def slot_written(i, ch, tok):
        slot_w[i][ch] = tok
        slot_r[i] = []

    def slot_read(i, tok):
        slot_r[i].append(tok)

    def wtile(b, kc, c0, w):
        return wbuf[b][:, kc * 1024 + c0: kc * 1024 + c0 + w]

    def fchunk(s, c, t0=0, tw=None):
        tw = T if tw is None else tw
        return slot[s][:, c * T + t0: c * T + t0 + tw]

    def tchunk(s, tc, d0=0, dw=1024):
        return slot[s][:, tc * 1024 + d0: tc * 1024 + d0 + dw]

    dge, pln, trq = "gpsimd", "sync", "sync"

    def pser():
        P._reg("P")
        return [("P", P.cnt["P"])]

    t_ew = P.op(trq, lambda g: g.dma_start(
        out=ewrow[0:1, 0:18],
        in_=W["few"].ap().rearrange("a b -> (a b)")[None, :]),
        waits=pser(), inc="P", inc_n=16)
    t_aw = P.op(trq, lambda g: g.dma_start(
        out=ewrow[0:1, 18:24], in_=W["fact"].ap()[None, :]),
        waits=pser(), inc="P", inc_n=16)

    t_io1 = P.op("gpsimd", lambda g: g.iota(
        fio[0][:, 0:128], [[1, 128]], channel_multiplier=-1,
        allow_small_or_imprecise_dtypes=True), inc="tPOOL")
    t_idf = P.op("vector", lambda v: v.tensor_scalar(
        ident_f32[:, :], fio[0][:, 0:128], 0.0, None, ALU.is_equal),
        waits=[t_io1], inc="tDVE")
    t_t128 = P.op("vector", lambda v: v.tensor_scalar(
        t128[:, :], fio[0][:, 0:128], 0.0, None, ALU.is_ge),
        waits=[t_io1], inc="tDVE")
    t_on1 = P.op("gpsimd", lambda g: g.memset(ones_bf[:, :], 1.0), inc="tPOOL")
    t_on2 = P.op("gpsimd", lambda g: g.memset(ones_f32r[:, :], 1.0), inc="tPOOL")
    t_on3 = P.op("gpsimd", lambda g: g.memset(ones_bfr[:, :], 1.0), inc="tPOOL")

    t_be1 = P.op(trq, lambda g: g.dma_start(
        out=vec_stage[:, 0:128],
        in_=W["fbe"].ap().rearrange("a (c p) -> (a c) p", p=128)[0:128, :]),
        waits=pser(), inc="P", inc_n=16)
    tk = P.op("tensor", lambda t: t.transpose(
        ps[0][:, 0:128], vec_stage[:, 0:128], ident_f32[:, :]),
        waits=[t_be1, t_idf], inc="tPE")
    t_ev1 = P.op("vector", lambda v: v.tensor_copy(beT[:, 0:128], ps[0][:, 0:128]),
                 waits=[tk], inc="tDVE")
    ps_last[0] = t_ev1
    t_be2 = P.op(trq, lambda g: g.dma_start(
        out=vec_stage[0:32, 0:128],
        in_=W["fbe"].ap().rearrange("a (c p) -> (a c) p", p=128)[128:160, :]),
        waits=pser() + [tk], inc="P", inc_n=16)
    tk = P.op("tensor", lambda t: t.transpose(
        ps[1][:, 0:32], vec_stage[0:32, 0:128], ident_f32[0:32, 0:32]),
        waits=[t_be2], inc="tPE")
    t_ev2 = P.op("vector", lambda v: v.tensor_copy(beT[:, 128:160], ps[1][:, 0:32]),
                 waits=[tk], inc="tDVE")
    ps_last[1] = t_ev2

    vps = ["abq", "abk", "abv", "abo", "agamma", "abeta", "fln_g", "fln_b"]
    vtoks = []
    for i, nm in enumerate(vps):
        vtoks.append(P.op(trq, lambda g, nm=nm, i=i: g.dma_start(
            out=vec_stage[8 * i:8 * i + 8, 0:128],
            in_=W[nm].ap().rearrange("(c p) -> c p", p=128)),
            waits=pser() + ([t_ev2] if i == 0 else []), inc="P", inc_n=16))
    tk = P.op("tensor", lambda t: t.transpose(
        ps[0][:, 0:64], vec_stage[0:64, 0:128], ident_f32[0:64, 0:64]),
        waits=vtoks + [ps_last[0]], inc="tPE")
    t_vec = P.op("vector", lambda v: v.tensor_copy(vecT[:, 0:64], ps[0][:, 0:64]),
                 waits=[tk], inc="tDVE")
    ps_last[0] = t_vec

    def vcol(name, c):
        i = vps.index(name)
        return vecT[:, i * 8 + c: i * 8 + c + 1]

    tks = []
    for j in range(24):
        tks.append(P.op("tensor", lambda t, j=j: t.matmul(
            ps[1][:, j:j + 1], ones_f32r[0:1, :], ewrow[0:1, j:j + 1],
            start=True, stop=True),
            waits=[t_ew, t_aw, t_on2, ps_last[1]] if j == 0 else [], inc="tPE"))
    t_scal = P.op("vector", lambda v: v.tensor_copy(scal[:, 0:24], ps[1][:, 0:24]),
                  waits=[tks[-1]], inc="tDVE")
    ps_last[1] = t_scal

    xn_wtoks = []
    lrow_gate = []
    fio_free = [t_t128, None, None]
    attb_free = [[], [], [], []]
    for i in range(TC):
        b = i % 2
        tx = P.op(pln, lambda g, i=i, b=b: g.dma_start(
            out=fio[b][:, :], in_=x_in.ap()[i * 128:(i + 1) * 128, :]),
            waits=[fio_free[b]], inc=f"X{b}", inc_n=16)
        t1 = P.op("vector", lambda v, b=b: v.reduce_sum(
            stat[:, 0:1], fio[b][:, :], axis=AX.X), waits=[tx], inc="tDVE")
        t2 = P.op("vector", lambda v: v.tensor_scalar(
            stat[:, 1:2], stat[:, 0:1], 1.0 / D, None, ALU.mult),
            waits=[t1], inc="tDVE")
        t3 = P.op("vector", lambda v, b=b: v.tensor_scalar(
            fio[2][:, :], fio[b][:, :], stat[:, 1:2], None, ALU.subtract),
            waits=[t2, fio_free[2]], inc="tDVE")
        fio_free[b] = t3
        t4 = P.op("vector", lambda v, b=b: v.scalar_tensor_tensor(
            attb[b][:, 0:1024], fio[2][:, :], 1.0, fio[2][:, :],
            ALU.mult, ALU.mult, accum_out=stat[:, 2:3]),
            waits=[t3] + attb_free[b], inc="tDVE")
        t5 = P.op("vector", lambda v: v.tensor_scalar(
            stat[:, 3:4], stat[:, 2:3], 1.0 / D, 1e-5, ALU.mult, ALU.add),
            waits=[t4], inc="tDVE")
        t6 = P.op("scalar", lambda s: s.activation(
            stat[:, 4:5], stat[:, 3:4], AF.Ln), waits=[t5], inc="tACT")
        t7 = P.op("scalar", lambda s: s.activation(
            stat[:, 5:6], stat[:, 4:5], AF.Exp, scale=-0.5),
            waits=[t6], inc="tACT")
        t8 = P.op("vector", lambda v, b=b: v.tensor_scalar(
            attb[b][:, 0:1024], fio[2][:, :], stat[:, 5:6], None, ALU.mult),
            waits=[t7], inc="tDVE")
        fio_free[2] = t8
        td = P.op(pln, lambda g, i=i, b=b: g.dma_start(
            out=xn_sp.ap()[i * 128:(i + 1) * 128, :], in_=attb[b][:, 0:1024]),
            waits=[t8], inc=f"XN{b}", inc_n=16)
        attb_free[b] = [td]
        xn_wtoks.append(td)

    xn_done = xn_wtoks[-2:]
    lrow_gate.extend(xn_done)
    A, B, C, Dv = 0, 1, 2, 3
    for c in range(KC):
        tk = P.op(trq, lambda g, c=c: g.dma_start(
            out=fchunk(A, c), in_=xn_sp.ap()[:, c * 128:(c + 1) * 128],
            transpose=True), waits=xn_done, inc="F0", inc_n=16)
        slot_written(A, c, tk)

    wb_state = {"i": 0}
    wbuf_free = [[], []]
    wstage_free = [None, None]

    def load_W(dram_ap, slot_override=None):
        if slot_override is None:
            b = wb_state["i"] % 2
            wb_state["i"] += 1
        else:
            b = slot_override
        gate = list(wbuf_free[b])
        tcast = None
        for q in range(KC):
            sb_ = q % 2
            tld = P.op(pln, lambda g, q=q, sb_=sb_, a=dram_ap: g.dma_start(
                out=wstage[sb_][:, :],
                in_=a.rearrange("(kc p) n -> p kc n", p=128)[:, q:q + 1, :]),
                waits=[wstage_free[sb_]], inc=f"WS{sb_}", inc_n=16)
            tcast = P.op("gpsimd", lambda g, q=q, sb_=sb_, b=b: g.tensor_copy(
                wbuf[b][:, q * 1024:(q + 1) * 1024], wstage[sb_][:, :]),
                waits=[tld] + (gate if q < 2 else []), inc="tPOOL")
            wstage_free[sb_] = tcast
        wbuf_free[b] = []
        return b, tcast

    def projection(src_slot, dst_slot, wslot, wtok, bias_fn=None,
                   iadd_slot=None, iadd_tok=None, spill=None, spill_sem=None):
        spills = []
        last_ev = None
        dst_gate = list(slot_r[dst_slot])
        TW = min(1024, T)
        NTW = T // TW
        for do in range(KC):
            ev_th = []
            for th in range(NTW):
                pb = (do * NTW + th) % 4
                mmw = [wtok, ps_last[pb]] + dst_gate + [slot_w[src_slot][kc]
                                                        for kc in range(KC)]
                if iadd_slot is not None:
                    mmw += [slot_w[iadd_slot][do], iadd_tok]
                t_mm = None
                for kc in range(KC):
                    for sg in range(TW // SEG):
                        t_mm = P.op("tensor", lambda t, do=do, th=th, kc=kc,
                                    pb=pb, sg=sg, ws=wslot, ss=src_slot:
                                    t.matmul(
                            ps[pb][:, sg * SEG:(sg + 1) * SEG],
                            wtile(ws, kc, do * 128, 128),
                            fchunk(ss, kc, th * TW + sg * SEG, SEG),
                            start=(kc == 0),
                            stop=(kc == KC - 1 and iadd_slot is None)),
                            waits=mmw if (kc == 0 and sg == 0) else [],
                            inc="tPE")
                if iadd_slot is not None:
                    for sg in range(TW // SEG):
                        t_mm = P.op("tensor", lambda t, do=do, th=th, pb=pb,
                                    sg=sg, s2=iadd_slot: t.matmul(
                            ps[pb][:, sg * SEG:(sg + 1) * SEG], iscaled[:, :],
                            fchunk(s2, do, th * TW + sg * SEG, SEG),
                            start=False, stop=True), inc="tPE")
                    slot_read(iadd_slot, t_mm)
                slot_read(src_slot, t_mm)
                eng = "scalar" if (do * NTW + th) % 2 == 0 else "vector"

                def ev_s(s, do=do, th=th, pb=pb, ds=dst_slot, TW=TW):
                    if bias_fn is not None:
                        return s.activation(
                            fchunk(ds, do, th * TW, TW), ps[pb][:, 0:TW],
                            AF.Identity, bias=bias_fn(do))
                    return s.activation(
                        fchunk(ds, do, th * TW, TW), ps[pb][:, 0:TW], AF.Copy)

                def ev_v(v, do=do, th=th, pb=pb, ds=dst_slot, TW=TW):
                    if bias_fn is not None:
                        return v.tensor_scalar(
                            fchunk(ds, do, th * TW, TW), ps[pb][:, 0:TW],
                            bias_fn(do), None, ALU.add)
                    return v.tensor_copy(
                        fchunk(ds, do, th * TW, TW), ps[pb][:, 0:TW])
                if eng == "scalar":
                    t_ev = P.op("scalar", ev_s, waits=[t_mm], inc="tACT")
                else:
                    t_ev = P.op("vector", ev_v, waits=[t_mm], inc="tDVE")
                ps_last[pb] = t_ev
                ev_th.append(t_ev)
                last_ev = t_ev
            slot_written(dst_slot, do, ev_th[-1])
            for t_ in ev_th[:-1]:
                slot_w[dst_slot][do] = ev_th[-1]
            if spill is not None:
                sp = P.op(pln, lambda g, do=do, a=spill, ds=dst_slot: g.dma_start(
                    out=a.ap().rearrange("(c p) t -> c p t", p=128)[do],
                    in_=fchunk(ds, do)), waits=ev_th, inc=spill_sem, inc_n=16)
                slot_read(dst_slot, sp)
                spills.append(sp)
        return last_ev, spills

    def projection_tmajor(src_slot, dst_slot, wslot, wtok, scale_col=None,
                          bias_row=None, bias_tok=None):
        last_ev = None
        dst_gate = list(slot_r[dst_slot])
        for tc in range(TC):
            pb = tc % 4
            mmw = [wtok, ps_last[pb]] + dst_gate + [slot_w[src_slot][kc]
                                                    for kc in range(KC)]
            t_mm = None
            for kc in range(KC):
                for half in range(2):
                    t_mm = P.op("tensor", lambda t, tc=tc, kc=kc, half=half,
                                pb=pb, ws=wslot, ss=src_slot: t.matmul(
                        ps[pb][:, half * 512:(half + 1) * 512],
                        fchunk(ss, kc, tc * 128, 128),
                        wtile(ws, kc, half * 512, 512),
                        start=(kc == 0),
                        stop=(kc == KC - 1 and bias_row is None)),
                        waits=mmw if (kc == 0 and half == 0) else [], inc="tPE")
            if bias_row is not None:
                for half in range(2):
                    t_mm = P.op("tensor", lambda t, half=half, pb=pb: t.matmul(
                        ps[pb][:, half * 512:(half + 1) * 512], ones_bfr[0:1, :],
                        bias_row[0:1, half * 512:(half + 1) * 512],
                        start=False, stop=True),
                        waits=[t_on3, bias_tok], inc="tPE")
            slot_read(src_slot, t_mm)

            def ev(s, tc=tc, pb=pb, ds=dst_slot):
                return s.activation(
                    tchunk(ds, tc), ps[pb][:, 0:1024], AF.Copy,
                    scale=(scale_col if scale_col is not None else 1.0))
            t_ev = P.op("scalar", ev, waits=[t_mm], inc="tACT")
            ps_last[pb] = t_ev
            slot_written(dst_slot, tc % max(KC, TC), t_ev)
            last_ev = t_ev
        return last_ev

    def fill_slot(dst_slot, src_dram, fmajor_src, src_ready):
        gate = list(slot_r[dst_slot]) + list(src_ready)
        for c in range(KC):
            if fmajor_src:
                tk = P.op(pln, lambda g, c=c, a=src_dram, ds=dst_slot: g.dma_start(
                    out=fchunk(ds, c),
                    in_=a.ap().rearrange("(c p) t -> c p t", p=128)[c]),
                    waits=gate if c == 0 else [], inc=f"F{dst_slot}", inc_n=16)
            else:
                tk = P.op(trq, lambda g, c=c, a=src_dram, ds=dst_slot: g.dma_start(
                    out=fchunk(ds, c), in_=a.ap()[:, c * 128:(c + 1) * 128],
                    transpose=True), waits=gate if c == 0 else [],
                    inc=f"F{dst_slot}", inc_n=16)
            slot_written(dst_slot, c, tk)

    def emit_negM(q_ap_fn, k_ap_fn, base, dh, abs_ap, abs_imm, ready,
                  ready_fn=None, out_col=0, sst_eng="vector"):
        prev = None
        for idx, apfn in enumerate((q_ap_fn, k_ap_fn)):
            for ha in range(NHA):
                rd = (list(ready) if ready is not None
                      else list(ready_fn(idx, ha)))
                t_sq = P.op(sst_eng, lambda v, apfn=apfn, ha=ha, base=base,
                            dh=dh: v.tensor_tensor(
                    sq_scr[base:base + dh, 0:NH], apfn(ha), apfn(ha),
                    ALU.mult),
                    waits=rd + [prev] + attb_free[2] + negm_m_last,
                    inc=("tPOOL" if sst_eng == "gpsimd" else "tDVE"))
                t_nm = None
                for sg in range(0, NH, 512):
                    sw = min(512, NH - sg)
                    t_nm = P.op("tensor", lambda t, sg=sg, sw=sw, base=base,
                                dh=dh: t.matmul(
                        ps[3][0:1, sg:sg + sw], ones_bf[base:base + dh, 0:1],
                        sq_scr[base:base + dh, sg:sg + sw],
                        start=True, stop=True),
                        waits=[t_sq, ps_last[3], t_on1] if sg == 0 else [],
                        inc="tPE")
                t_mx = P.op("vector", lambda v, idx=idx, ha=ha: v.reduce_max(
                    tiny[0:1, idx * NHA + ha: idx * NHA + ha + 1],
                    ps[3][0:1, 0:NH], axis=AX.X), waits=[t_nm], inc="tDVE")
                ps_last[3] = t_mx
                attb_free[2] = [t_nm]
                prev = t_mx
        if NHA > 1:
            for idx in range(2):
                prev = P.op("vector", lambda v, idx=idx: v.tensor_tensor(
                    tiny[0:1, idx * NHA: idx * NHA + 1],
                    tiny[0:1, idx * NHA: idx * NHA + 1],
                    tiny[0:1, idx * NHA + 1: idx * NHA + 2], ALU.max),
                    waits=[prev], inc="tDVE")
        t_p = P.op("vector", lambda v: v.tensor_tensor(
            tiny[0:1, 8:9], tiny[0:1, 0:1], tiny[0:1, NHA:NHA + 1], ALU.add),
            waits=[prev], inc="tDVE")
        if abs_ap is not None:
            t_m = P.op("vector", lambda v, a=abs_ap: v.tensor_tensor(
                tiny[0:1, 11:12], tiny[0:1, 8:9], a, ALU.mult),
                waits=[t_p], inc="tDVE")
        else:
            t_m = P.op("vector", lambda v: v.tensor_scalar(
                tiny[0:1, 11:12], tiny[0:1, 8:9], abs_imm, None, ALU.mult),
                waits=[t_p], inc="tDVE")

        negm_m_last[:] = [t_m]

        def emitB():
            t_r = P.op("tensor", lambda t: t.matmul(
                ps[3][:, 0:1], ones_f32r[0:1, :], tiny[0:1, 11:12],
                start=True, stop=True), waits=[t_m, ps_last[3], t_on2],
                inc="tPE")
            t_n = P.op("vector", lambda v, oc=out_col: v.tensor_scalar(
                negM[:, oc:oc + 1], ps[3][:, 0:1], -1.0, None, ALU.mult),
                waits=[t_r], inc="tDVE")
            ps_last[3] = t_n
            return t_n
        return emitB

    dbg_gate = []
    qbuf_gate = []

    def attention_qtile(kf_tile_fn, q_rhs_fn, v_tile_fn, dh, obase, kc_list,
                        diag_off, scale_col, negM_fn, wfold_11, ctx_dst_ap,
                        extra_ready, pending_fin=None, negM_col=0):
        n_kc = len(kc_list)
        t_avl = None
        t_lmm = None
        tp = (0, obase) if obase else None
        ready_q = []
        scores_tok = {}

        def emit_scores(j):
            kc = kc_list[j]
            pb = j % 2
            mmw = [ps_last[pb]] + list(extra_ready)
            t_s = None
            for sg in range(NSEG_P):
                t_s = P.op("tensor", lambda t, kc=kc, sg=sg, pb=pb: t.matmul(
                    ps[pb][:, sg * PSEG:(sg + 1) * PSEG],
                    kf_tile_fn(kc), q_rhs_fn(sg * PSEG, PSEG),
                    start=True, stop=True),
                    waits=mmw if sg == 0 else [], inc="tPE")
            scores_tok[j] = t_s

        def seg_ranges(off):
            out = []
            x = off
            while x < PW:
                nxt = min(PW, (x // PSEG + 1) * PSEG)
                out.append((x, nxt - x))
                x = nxt
            return out

        def emit_lav(j, ab, t_ready, off=0, l_first=False, l_last=False,
                     av_first=False, av_stop=False):
            nonlocal t_lmm, t_avl
            kc = kc_list[j]
            for (o, w) in seg_ranges(off):
                t_lmm = P.op("tensor", lambda t, o=o, w=w, ab=ab,
                             l_first=l_first, l_last=l_last: t.matmul(
                    ps[3][0:1, o:o + w], ones_bf[:, 0:1],
                    attb[ab][:, o:o + w],
                    start=l_first, stop=l_last),
                    waits=[t_ready, t_on1] + ([ps_last[3]] if l_first else []),
                    inc="tPE")
            for (o, w) in seg_ranges(off):
                t_avl = P.op("tensor", lambda t, kc=kc, o=o, w=w, ab=ab, dh=dh,
                             obase=obase, tp=tp, av_first=av_first,
                             av_stop=av_stop: t.matmul(
                    ps[2][obase:obase + dh, o:o + w],
                    v_tile_fn(kc), attb[ab][:, o:o + w],
                    start=av_first, stop=av_stop, tile_position=tp),
                    waits=([ps_last[2]] if av_first else []), inc="tPE")
            attb_free[ab] = [t_avl]

        def pair_a(e0, e1):
            nonlocal t_avl, t_lmm
            j0, a0, r0, _ = e0
            j1, a1, r1, _ = e1
            t_av0 = None
            for (o, w) in seg_ranges(0):
                t_av0 = P.op("tensor", lambda t, kc0=kc_list[j0], o=o, w=w,
                             a0=a0, dh=dh, obase=obase, tp=tp,
                             first=(j0 == 0): t.matmul(
                    ps[2][obase:obase + dh, o:o + w],
                    v_tile_fn(kc0), attb[a0][:, o:o + w],
                    start=first, stop=False, tile_position=tp),
                    waits=[r0] + ([ps_last[2]] if j0 == 0 else []), inc="tPE")
            t_pr = P.op("vector", lambda v, a0=a0, a1=a1: v.tensor_tensor(
                attb[a0][:, 0:PW], attb[a0][:, 0:PW], attb[a1][:, 0:PW],
                ALU.add), waits=[r0, r1, t_av0], inc="tDVE")

            def pair_b():
                nonlocal t_avl, t_lmm
                t_l = None
                for (o, w) in seg_ranges(0):
                    t_l = P.op("tensor", lambda t, o=o, w=w, a0=a0,
                               first=(j0 == 0), last=(j1 == n_kc - 1):
                               t.matmul(
                        ps[3][0:1, o:o + w], ones_bf[:, 0:1],
                        attb[a0][:, o:o + w], start=first, stop=last),
                        waits=[t_pr, t_on1] +
                        ([ps_last[3]] if j0 == 0 else []), inc="tPE")
                t_av2 = None
                for (o, w) in seg_ranges(0):
                    t_av2 = P.op("tensor", lambda t, kc1=kc_list[j1], o=o,
                                 w=w, a1=a1, dh=dh, obase=obase, tp=tp,
                                 last=(j1 == n_kc - 1): t.matmul(
                        ps[2][obase:obase + dh, o:o + w],
                        v_tile_fn(kc1), attb[a1][:, o:o + w],
                        start=False, stop=last, tile_position=tp),
                        waits=[r1], inc="tPE")
                t_lmm = t_l
                attb_free[a0] = [t_l]
                attb_free[a1] = [t_pr, t_av2]
                t_avl = t_av2
            return pair_b

        P.tag = P.tag + "/att"
        pair_mode = (diag_off is None and n_kc % 2 == 0 and n_kc >= 4)
        stash = [None]

        for j in range(min(2, n_kc)):
            emit_scores(j)
        if pending_fin is not None:
            pending_fin()
        t_negM = negM_fn()
        for j, kc in enumerate(kc_list):
            if j >= 2:
                emit_scores(j)
            if stash[0] is not None:
                stash[0]()
                stash[0] = None
            pb = j % 2
            ab = j % (4 if pair_mode else 3)
            off = diag_off.get(kc, 0) if diag_off is not None else 0
            t_x = P.op("scalar", lambda s, pb=pb, ab=ab, nc_=negM_col,
                       off=off: s.activation(
                attb[ab][:, off:PW], ps[pb][:, off:PW], AF.Exp,
                bias=negM[:, nc_:nc_ + 1], scale=scale_col),
                waits=[scores_tok[j], t_negM] + attb_free[ab], inc="tACT")
            ps_last[pb] = t_x
            t_ready = t_x
            if diag_off is not None and kc in diag_off:
                t_ready = P.op("vector", lambda v, ab=ab, off=off: v.tensor_tensor(
                    attb[ab][:, off:off + 128], attb[ab][:, off:off + 128],
                    t128[:, :], ALU.mult), waits=[t_x, t_t128], inc="tDVE")
            ready_q.append((j, ab, t_ready, off))
            if pair_mode and len(ready_q) >= 4:
                stash[0] = pair_a(ready_q.pop(0), ready_q.pop(0))
            elif not pair_mode and len(ready_q) >= 3:
                jj, aa, rr, oo = ready_q.pop(0)
                emit_lav(jj, aa, rr, off=oo, l_first=(jj == 0),
                         l_last=(jj == n_kc - 1), av_first=(jj == 0),
                         av_stop=(jj == n_kc - 1))
        if stash[0] is not None:
            stash[0]()
            stash[0] = None
        while ready_q:
            if pair_mode and len(ready_q) >= 2:
                pair_a(ready_q.pop(0), ready_q.pop(0))()
            else:
                jj, aa, rr, oo = ready_q.pop(0)
                emit_lav(jj, aa, rr, off=oo, l_first=(jj == 0),
                         l_last=(jj == n_kc - 1), av_first=(jj == 0),
                         av_stop=(jj == n_kc - 1))
        t_lmm_f = t_lmm
        t_avl_f = t_avl

        def fin(cbs=()):
            t_le = P.op("scalar", lambda s: s.activation(
                lrow[0:1, 0:PW], ps[3][0:1, 0:PW], AF.Ln),
                waits=[t_lmm_f] + dbg_gate + lrow_gate, inc="tACT")
            ps_last[3] = t_le
            t_rc = P.op("scalar", lambda s, a=wfold_11: s.activation(
                wb_sb[0:1, 0:PW], lrow[0:1, 0:PW], AF.Exp, scale=-1.0,
                bias=(a if a is not None else 0.0)),
                waits=[t_le, ps_last[3], ps_last[2]], inc="tACT")
            t_wb = None
            for sg in range(NSEG_P):
                t_wb = P.op("tensor", lambda t, sg=sg, dh=dh,
                            obase=obase, tp=tp: t.matmul(
                    ps[3][obase:obase + dh, sg * PSEG:(sg + 1) * PSEG],
                    ones_bfr[0:1, 0:dh],
                    wb_sb[0:1, sg * PSEG:(sg + 1) * PSEG],
                    start=True, stop=True, tile_position=tp),
                    waits=[t_rc, ps_last[3], t_on3] if sg == 0 else [],
                    inc="tPE")
            t_we = P.op("vector", lambda v, dh=dh, obase=obase:
                        v.tensor_copy(wb_sb[obase:obase + dh, 0:PW],
                                      ps[3][obase:obase + dh, 0:PW]),
                        waits=[t_wb], inc="tDVE")
            ps_last[3] = t_we
            t_fin = P.op("vector", lambda v, dh=dh, obase=obase,
                         dst=ctx_dst_ap: v.tensor_tensor(
                dst, ps[2][obase:obase + dh, 0:PW],
                wb_sb[obase:obase + dh, 0:PW], ALU.mult),
                waits=[t_we, t_avl_f], inc="tDVE")
            ps_last[2] = t_fin
            for cb in cbs:
                cb(t_fin)
            return t_fin
        return fin

    meta = ffn_dag_meta()
    iscaled_free = []
    lnaw_gate = [None]
    negm_m_last = [None]
    pending = [None]

    def defer_fin(fin, cbs):
        def run():
            return fin(cbs=cbs)
        pending[0] = run

    def take_fin():
        p = pending[0]
        pending[0] = None
        return p

    def drain_fin():
        p = take_fin()
        if p is not None:
            p()
    weq_pre = load_W(W["fWe"].ap()[meta[0]["eq"]], slot_override=0)
    last_spills = {}

    for nd in meta:
        c = nd["c"]
        t_sc1 = P.op("vector", lambda v, c=c: v.tensor_tensor(
            scal[:, 24:25], scal[:, 3 * c:3 * c + 1], scal[:, 3 * c + 1:3 * c + 2],
            ALU.mult), waits=[t_scal], inc="tDVE")
        t_sc2 = P.op("vector", lambda v: v.tensor_scalar(
            scal[:, 24:25], scal[:, 24:25], 1.0 / np.sqrt(128.0), None,
            ALU.mult), waits=[t_sc1], inc="tDVE")
        t_sc3 = P.op("vector", lambda v: v.tensor_tensor(
            scal[:, 25:26], scal[:, 24:25], scal[:, 24:25], ALU.mult),
            waits=[t_sc2], inc="tDVE")
        t_isc = P.op("vector", lambda v, c=c: v.tensor_scalar(
            iscaled[:, :], ident_f32[:, :], scal[:, 3 * c:3 * c + 1], None,
            ALU.mult), waits=[t_scal, t_idf] + iscaled_free, inc="tDVE")
        iscaled_free = []
        t_lnaw = P.op("scalar", lambda s, c=c: s.activation(
            tiny[0:1, 12:13], ewrow[0:1, 18 + c:19 + c], AF.Ln),
            waits=[t_ew, t_aw] + lnaw_gate, inc="tACT")
        lnaw_gate = [t_lnaw]
        t_sf1 = P.op("scalar", lambda s: s.activation(
            tiny[0:1, 13:14], scal[0:1, 25:26], AF.Ln),
            waits=[t_sc3], inc="tACT")
        t_sf2 = P.op("scalar", lambda s: s.activation(
            tiny[0:1, 14:15], tiny[0:1, 13:14], AF.Exp, scale=0.5),
            waits=[t_sf1], inc="tACT")
        t_sf3 = P.op("vector", lambda v: v.tensor_scalar(
            tiny[0:1, 15:16], tiny[0:1, 14:15], 0.51, None, ALU.mult),
            waits=[t_sf2] + negm_m_last, inc="tDVE")
        negm_m_last = [t_sf3]

        P.tag = f"n{c}:pq"
        X1, X2 = c % 2, 1 - c % 2
        wq_b, wq_t = weq_pre
        t_pq, _ = projection(A, B, wq_b, wq_t,
                             bias_fn=lambda do, eq=nd["eq"]:
                             beT[:, eq * 8 + do:eq * 8 + do + 1])
        wbuf_free[wq_b].append(t_pq)

        if c == 0:
            pk_slot = B
        else:
            if nd["in_k"] == -1:
                fill_slot(Dv, xn_sp, False, xn_done)
            else:
                fill_slot(Dv, n_sp[nd["in_k"]], True,
                          last_spills.get(nd["in_k"], []))
            P.tag = f"n{c}:pk"
            wk_b, wk_t = load_W(W["fWe"].ap()[nd["ek"]], slot_override=X2)
            t_pk, _ = projection(Dv, C, wk_b, wk_t,
                                 bias_fn=lambda do, ek=nd["ek"]:
                                 beT[:, ek * 8 + do:ek * 8 + do + 1])
            wbuf_free[wk_b].append(t_pk)
            pk_slot = C

        if c <= 1:
            pv_slot = B
        else:
            fill_slot(A, n_sp[nd["in_v"]], True,
                      last_spills.get(nd["in_v"], []))
            P.tag = f"n{c}:pv"
            wv_b, wv_t = load_W(W["fWe"].ap()[nd["ev"]], slot_override=X1)
            t_pv, _ = projection(A, Dv, wv_b, wv_t,
                                 bias_fn=lambda do, ev=nd["ev"]:
                                 beT[:, ev * 8 + do:ev * 8 + do + 1])
            wbuf_free[wv_b].append(t_pv)
            pv_slot = Dv

        P.tag = f"n{c}:vv"
        wvv_b, wvv_t = load_W(W["fWv"].ap()[c], slot_override=X2)
        if c <= 1:
            gate = list(slot_r[A])
            if gate:
                P.op("tensor", lambda t: t.nop(), waits=gate)
        t_vv = projection_tmajor(pv_slot, A, wvv_b, wvv_t,
                                 scale_col=scal[:, 3 * c + 2:3 * c + 3])
        wbuf_free[wvv_b].append(t_vv)

        wq_b2, wq_t2 = load_W(W["fWq"].ap()[c], slot_override=X1)
        wk_b2, wk_t2 = load_W(W["fWk"].ap()[c], slot_override=X2)
        wo_t = None
        wo_b = None

        for h in range(8):
            P.tag = f"n{c}:h{h}:qk"
            qk_done = []
            TW2 = min(1024, T)
            NTW2 = T // TW2
            for which, (bufap, wsl, wtk, src_sl) in enumerate(
                    [(qbuf, wq_b2, wq_t2, B), (kbuf, wk_b2, wk_t2, pk_slot)]):
                if which == 1:
                    drain_fin()
                for th in range(NTW2):
                    pb = (which * NTW2 + th) % 2
                    mmw = [wtk, ps_last[pb]] + [slot_w[src_sl][kc]
                                                for kc in range(KC)]
                    t_mm = None
                    for kc in range(KC):
                        for sg in range(TW2 // SEG):
                            t_mm = P.op("tensor", lambda t, kc=kc, th=th,
                                        pb=pb, sg=sg, wsl=wsl, h=h,
                                        src_sl=src_sl: t.matmul(
                                ps[pb][:, sg * SEG:(sg + 1) * SEG],
                                wtile(wsl, kc, h * 128, 128),
                                fchunk(src_sl, kc, th * TW2 + sg * SEG, SEG),
                                start=(kc == 0), stop=(kc == KC - 1)),
                                waits=mmw if (kc == 0 and sg == 0) else [],
                                inc="tPE")
                    slot_read(src_sl, t_mm)
                    t_ev = P.op("vector", lambda v, th=th, pb=pb,
                                bufap=bufap, TW2=TW2: v.tensor_copy(
                        bufap[:, th * TW2:(th + 1) * TW2],
                        ps[pb][:, 0:TW2]),
                        waits=[t_mm] + qbuf_gate, inc="tDVE")
                    ps_last[pb] = t_ev
                    qk_done.append(t_ev)
            if debug and c in (0, 1) and h == 0:
                tdq = P.op(pln, lambda g, c=c: g.dma_start(
                    out=dbg[f"q{c}"].ap(), in_=qbuf[:, :]),
                    waits=qk_done, inc="DBG", inc_n=16)
                tdk = P.op(pln, lambda g, c=c: g.dma_start(
                    out=dbg[f"k{c}"].ap(), in_=kbuf[:, :]),
                    waits=qk_done, inc="DBG", inc_n=16)
                qbuf_gate[:] = [tdq, tdk]
            if h == 7:
                wbuf_free[wq_b2].append(qk_done[NSEG_T - 1])
                wbuf_free[wk_b2].append(qk_done[-1])
                wo_b, wo_t = load_W(W["fWo"].ap()[c], slot_override=X1)
                if c < 5:
                    nxt = meta[c + 1]
                    weq_pre = load_W(W["fWe"].ap()[nxt["eq"]],
                                     slot_override=X2)
                else:
                    weq_pre = load_W(W["aWq"].ap(), slot_override=X2)
            drain_fin()
            negM_B = emit_negM(
                lambda ha: qbuf[:, ha * NH:(ha + 1) * NH],
                lambda ha: kbuf[:, ha * NH:(ha + 1) * NH],
                0, 128, tiny[0:1, 15:16], None, None,
                ready_fn=lambda idx, ha, NTW2=NTW2: [qk_done[min(
                    idx * NTW2 + ha, len(qk_done) - 1)]])
            negM_tok = [None]

            def negM_fn(B=negM_B, tok=negM_tok):
                if tok[0] is None:
                    tok[0] = B()
                return tok[0]
            for qt in range(NQ):
                fin = attention_qtile(
                    kf_tile_fn=lambda kc: kbuf[:, kc * 128:(kc + 1) * 128],
                    q_rhs_fn=lambda o, w, qt=qt: qbuf[:, qt * PW + o:
                                                      qt * PW + o + w],
                    v_tile_fn=lambda kc, h=h, A=A: slot[A][
                        :, kc * 1024 + h * 128: kc * 1024 + h * 128 + 128],
                    dh=128, obase=0, kc_list=list(range(TC)), diag_off=None,
                    scale_col=scal[:, 24:25], negM_fn=negM_fn,
                    wfold_11=tiny[0:1, 12:13],
                    ctx_dst_ap=fchunk(Dv, h, qt * PW, PW),
                    extra_ready=qk_done + [t_vv], pending_fin=take_fin())
                defer_fin(fin, [lambda t, A=A: slot_read(A, t),
                                lambda t, h=h: slot_written(Dv, h, t)])
        drain_fin()
        P.tag = f"n{c}:oproj"
        spill_t = n_sp.get(c)
        t_o, sp_toks = projection(Dv, B, wo_b, wo_t, bias_fn=None,
                                  iadd_slot=B, iadd_tok=t_isc, spill=spill_t,
                                  spill_sem=f"SPn{c}")
        if spill_t is not None:
            last_spills[c] = sp_toks
        iscaled_free = [t_o]
        wbuf_free[wo_b].append(t_o)
        A, B = B, A

    n5_spills = []
    for ch in range(KC):
        sp = P.op(pln, lambda g, ch=ch, A=A: g.dma_start(
            out=n5_sp.ap().rearrange("(c p) t -> c p t", p=128)[ch],
            in_=fchunk(A, ch)), waits=[slot_w[A][ch]], inc="SPn5", inc_n=16)
        slot_read(A, sp)
        n5_spills.append(sp)

    P.tag = "L0:qkv"
    fill_slot(B, xn_sp, False, xn_done)
    wq_b, wq_t = weq_pre
    t_q, _ = projection(B, C, wq_b, wq_t, bias_fn=lambda do: vcol("abq", do))
    wbuf_free[wq_b].append(t_q)
    wk_b, wk_t = load_W(W["aWk"].ap(), slot_override=1 - wq_b)
    t_k, _ = projection(B, Dv, wk_b, wk_t, bias_fn=lambda do: vcol("abk", do))
    wbuf_free[wk_b].append(t_k)
    t_bv = P.op(dge, lambda g: g.dma_start(
        out=bvrow[0:1, 0:1024], in_=W["abv"].ap()[None, :]),
        waits=[ps_last[3]], inc="BV", inc_n=16)
    wv_b, wv_t = load_W(W["aWv"].ap(), slot_override=wq_b)
    gate = list(slot_r[A])
    if gate:
        P.op("tensor", lambda t: t.nop(), waits=gate)
    t_v = projection_tmajor(B, A, wv_b, wv_t, bias_row=bvrow, bias_tok=t_bv)
    wbuf_free[wv_b].append(t_v)
    wo_b, wo_t = load_W(W["aWo"].ap(), slot_override=1 - wv_b)

    NDIAG = PW // 128
    l0_negM = {}

    def l0_chain(hh):
        ch_, pr_ = hh // 2, (hh % 2) * 64
        eB = emit_negM(
            lambda ha, ch=ch_, pr=pr_: slot[C][
                pr:pr + 64, ch * T + ha * NH: ch * T + (ha + 1) * NH],
            lambda ha, ch=ch_, pr=pr_: slot[Dv][
                pr:pr + 64, ch * T + ha * NH: ch * T + (ha + 1) * NH],
            pr_, 64, None, 0.06375, [t_q, t_k], out_col=hh)
        l0_negM[hh] = eB()

    l0_chain(0)
    l0_chain(1)
    for h in range(16):
        ch, pr = h // 2, (h % 2) * 64
        P.tag = f"L0:h{h}"
        drain_fin()
        if h + 2 < 16:
            l0_chain(h + 2)

        def negM_fn(h=h):
            return l0_negM[h]
        for qt in range(NQ):
            kc_hi = (qt + 1) * NDIAG
            doff = {kc: kc * 128 - qt * PW for kc in range(qt * NDIAG, kc_hi)}
            fin = attention_qtile(
                kf_tile_fn=lambda kc, ch=ch, pr=pr: slot[Dv][
                    pr:pr + 64, ch * T + kc * 128: ch * T + (kc + 1) * 128],
                q_rhs_fn=lambda o, w, ch=ch, pr=pr, qt=qt: slot[C][
                    pr:pr + 64, ch * T + qt * PW + o: ch * T + qt * PW + o + w],
                v_tile_fn=lambda kc, h=h, A=A: slot[A][
                    :, kc * 1024 + h * 64: kc * 1024 + h * 64 + 64],
                dh=64, obase=pr, kc_list=list(range(kc_hi)), diag_off=doff,
                scale_col=0.125, negM_fn=negM_fn, wfold_11=None,
                ctx_dst_ap=slot[B][pr:pr + 64,
                                   ch * T + qt * PW: ch * T + qt * PW + PW],
                extra_ready=[t_v], pending_fin=take_fin(), negM_col=h)
            defer_fin(fin, [lambda t, A=A: slot_read(A, t),
                            lambda t: slot_read(C, t),
                            lambda t: slot_read(Dv, t),
                            lambda t, ch=ch: slot_written(B, ch, t)])
    drain_fin()
    P.tag = "L0:yproj"
    t_y, y_spills = projection(B, C, wo_b, wo_t,
                               bias_fn=lambda do: vcol("abo", do), spill=y_sp,
                               spill_sem="SPy")
    wbuf_free[wo_b].append(t_y)

    def build_gam(i0, i1, ready):
        toks = []
        for j, i in enumerate((i0, i1)):
            tr = P.op(trq, lambda g, i=i: g.dma_start(
                out=lrow[0:1, 0:1024], in_=W[vps[4 + i]].ap()[None, :]),
                waits=pser() + list(ready) + toks, inc="P", inc_n=16)
            for h2 in range(2):
                tm = P.op("tensor", lambda t, j=j, h2=h2: t.matmul(
                    ps[3][:, h2 * 512:(h2 + 1) * 512], ones_f32r[0:1, :],
                    lrow[0:1, h2 * 512:(h2 + 1) * 512], start=True, stop=True),
                    waits=[tr, ps_last[3], t_on2], inc="tPE")
                te = P.op("vector", lambda v, j=j, h2=h2: v.tensor_copy(
                    gamT[:, j * 1024 + h2 * 512: j * 1024 + (h2 + 1) * 512],
                    ps[3][:, h2 * 512:(h2 + 1) * 512]), waits=[tm], inc="tDVE")
                ps_last[3] = te
                toks.append(te)
        return toks

    def ln_apply_loop(dst_slot, src_sp, src_ready, eps, gam_toks):
        toks = []
        for tc in range(TC):
            b = tc % 2
            tk = P.op(trq, lambda g, tc=tc, b=b, a=src_sp: g.dma_start(
                out=attb[b][:, 0:1024], in_=a.ap()[:, tc * 128:(tc + 1) * 128],
                transpose=True),
                waits=list(src_ready) + attb_free[b], inc=f"T{b}", inc_n=16)
            t1 = P.op("vector", lambda v, b=b: v.reduce_sum(
                stat[:, 0:1], attb[b][:, 0:1024], axis=AX.X),
                waits=[tk] + list(gam_toks), inc="tDVE")
            t2 = P.op("vector", lambda v: v.tensor_scalar(
                stat[:, 1:2], stat[:, 0:1], 1.0 / D, None, ALU.mult),
                waits=[t1], inc="tDVE")
            t3 = P.op("vector", lambda v, b=b: v.tensor_scalar(
                fio[2][:, :], attb[b][:, 0:1024], stat[:, 1:2], None,
                ALU.subtract), waits=[t2, fio_free[2]], inc="tDVE")
            t4 = P.op("vector", lambda v, b=b: v.scalar_tensor_tensor(
                attb[b][:, 0:1024], fio[2][:, :], 1.0, fio[2][:, :],
                ALU.mult, ALU.mult, accum_out=stat[:, 2:3]),
                waits=[t3], inc="tDVE")
            attb_free[b] = [t4]
            t5 = P.op("vector", lambda v, eps=eps: v.tensor_scalar(
                stat[:, 3:4], stat[:, 2:3], 1.0 / D, eps, ALU.mult, ALU.add),
                waits=[t4], inc="tDVE")
            t6 = P.op("scalar", lambda s: s.activation(
                stat[:, 4:5], stat[:, 3:4], AF.Ln), waits=[t5], inc="tACT")
            t7 = P.op("scalar", lambda s: s.activation(
                stat[:, 5:6], stat[:, 4:5], AF.Exp, scale=-0.5),
                waits=[t6], inc="tACT")
            t8 = P.op("vector", lambda v: v.tensor_scalar(
                fio[2][:, :], fio[2][:, :], stat[:, 5:6], None, ALU.mult),
                waits=[t7], inc="tDVE")
            t9 = P.op("vector", lambda v: v.tensor_tensor(
                fio[2][:, :], fio[2][:, :], gamT[:, 0:1024], ALU.mult),
                waits=[t8], inc="tDVE")
            t10 = P.op("vector", lambda v, ds=dst_slot, tc=tc: v.tensor_tensor(
                tchunk(ds, tc), fio[2][:, :], gamT[:, 1024:2048], ALU.add),
                waits=[t9], inc="tDVE")
            fio_free[2] = t10
            slot_written(dst_slot, tc % max(KC, TC), t10)
            toks.append(t10)
        return toks

    fio_free[2] = ps_last[2]
    gam_s = build_gam(0, 1, y_spills)
    s_toks = ln_apply_loop(Dv, y_sp, y_spills, 1e-5, gam_s)
    gam_g = build_gam(2, 3, s_toks)
    g_toks = ln_apply_loop(B, n5_sp, n5_spills, 1e-6, gam_g)
    gs_toks = []
    for tc in range(TC):
        gs_toks.append(P.op("scalar", lambda s, tc=tc: s.activation(
            tchunk(B, tc), tchunk(B, tc), AF.Silu), waits=[g_toks[tc]],
            inc="tACT"))

    out_toks = []
    for tc in range(TC):
        b = tc % 2
        tx = P.op(pln, lambda g, tc=tc, b=b: g.dma_start(
            out=fio[b][:, :], in_=x_in.ap()[tc * 128:(tc + 1) * 128, :]),
            waits=[fio_free[b]], inc=f"X{b}", inc_n=16)
        t1 = P.op("vector", lambda v, tc=tc: v.tensor_tensor(
            fio[2][:, :], tchunk(Dv, tc), tchunk(B, tc), ALU.mult),
            waits=[s_toks[tc], gs_toks[tc], fio_free[2]], inc="tDVE")
        t2 = P.op("vector", lambda v, b=b: v.tensor_tensor(
            fio[2][:, :], fio[2][:, :], fio[b][:, :], ALU.add),
            waits=[tx, t1], inc="tDVE")
        fio_free[b] = t2
        td = P.op(pln, lambda g, tc=tc: g.dma_start(
            out=out_t.ap()[tc * 128:(tc + 1) * 128, :], in_=fio[2][:, :]),
            waits=[t2], inc="OUT", inc_n=16)
        fio_free[2] = td
        out_toks.append(td)

    P.op("sync", lambda g: g.nop(), waits=[out_toks[-1]])

    with nc.Block() as block:
        P.finalize(nc, block, stack)
    stack.close()
    return nc


_NAME_MAP = {
    "aWq": "L0_attn_Wq", "aWk": "L0_attn_Wk", "aWv": "L0_attn_Wv",
    "aWo": "L0_attn_Wo", "abq": "L0_attn_bq", "abk": "L0_attn_bk",
    "abv": "L0_attn_bv", "abo": "L0_attn_bo", "agamma": "L0_attn_gamma",
    "abeta": "L0_attn_beta", "few": "L0_ffn_ew", "fact": "L0_ffn_act_w",
    "fWe": "L0_ffn_We", "fbe": "L0_ffn_be", "fWq": "L0_ffn_Wq",
    "fWk": "L0_ffn_Wk", "fWv": "L0_ffn_Wv", "fWo": "L0_ffn_Wo",
    "fln_g": "L0_ffn_ln_g", "fln_b": "L0_ffn_ln_b",
}


def shard_inputs(inputs, T=2048, nb=None):
    x = np.ascontiguousarray(np.asarray(inputs["L0_attn_x"], dtype=np.float32))
    B = x.shape[0] if nb is None else nb
    shared = {k: np.ascontiguousarray(np.asarray(inputs[v], dtype=np.float32))
              for k, v in _NAME_MAP.items()}
    in_maps = []
    for b in range(B):
        d = dict(shared)
        d["x"] = np.ascontiguousarray(x[b, :T])
        in_maps.append(d)
    return in_maps


def kernel(**inputs):
    T = 2048
    nc = build_nc(T)
    in_maps = shard_inputs(inputs, T)
    res = run_bass_kernel_spmd(nc, in_maps, core_ids=list(range(len(in_maps))))
    out = np.stack([np.asarray(r["out"], dtype=np.float32)
                    for r in res.results], axis=0)
    return out


# version 17
# speedup vs baseline: 1.0038x; 1.0038x over previous
import numpy as np
import ml_dtypes
from contextlib import ExitStack
import concourse.bass as bass
import concourse.mybir as mybir
from concourse.bass_utils import run_bass_kernel_spmd

F32 = mybir.dt.float32
BF16 = mybir.dt.bfloat16
AF = mybir.ActivationFunctionType
ALU = mybir.AluOpType
AX = mybir.AxisListType

D = 1024
KC = 8
NUM_NOD = 6

ENGINES = ("tensor", "scalar", "vector", "gpsimd", "sync")


class Prog:
    def __init__(self):
        self.q = {e: [] for e in ENGINES}
        self.sems = []
        self.cnt = {}
        self.hw = {}
        self.tag = "init"
        self.pe_tags = []

    def _reg(self, name):
        if name not in self.cnt:
            self.cnt[name] = 0
            self.sems.append(name)

    def op(self, eng, fn, waits=(), inc=None, inc_n=None):
        if eng == "tensor":
            self.pe_tags.append(self.tag)
        byname = {}
        for t in waits:
            if t is None:
                continue
            s, v = t
            if v <= 0:
                continue
            if byname.get(s, 0) < v:
                byname[s] = v
        for s, v in byname.items():
            key = (eng, s)
            if self.hw.get(key, 0) >= v:
                continue
            self.hw[key] = v
            self.q[eng].append(("w", s, v))
        tok = None
        if inc is not None:
            self._reg(inc)
            n = inc_n if inc_n is not None else 1
            self.cnt[inc] += n
            tok = (inc, self.cnt[inc])
            self.q[eng].append(("o", fn, inc, n))
        else:
            self.q[eng].append(("o", fn, None, 0))
        return tok

    def finalize(self, nc, block, stack):
        sems = {s: stack.enter_context(nc.semaphore(s)) for s in self.sems}
        for eng_name in ENGINES:
            items = list(self.q[eng_name])

            def make(items):
                def emit(eng):
                    for it in items:
                        if it[0] == "w":
                            eng.wait_ge(sems[it[1]], it[2])
                        else:
                            ins = it[1](eng)
                            if it[2] is not None:
                                ins.then_inc(sems[it[2]], it[3])
                return emit

            getattr(block, eng_name)(make(items))


def ffn_dag_meta():
    meta = []
    lind = 0
    for c in range(NUM_NOD):
        nsrc = min(c + 1, 5)
        snode = c - nsrc
        meta.append(dict(c=c, eq=lind + nsrc - 1, ek=lind, ev=lind + nsrc // 2,
                         in_q=c - 1, in_k=snode, in_v=snode + nsrc // 2))
        lind += nsrc
    return meta


def build_nc(T=2048, debug=False):
    TC = T // 128
    PW = T // 2
    NQ = T // PW
    SEG = min(512, T)
    NSEG_T = T // SEG
    PSEG = min(512, PW)
    NSEG_P = PW // PSEG
    NH = min(1024, T)
    NHA = T // NH

    nc = bass.Bass("TRN2", target_bir_lowering=False)

    x_in = nc.dram_tensor("x", [T, D], F32, kind="ExternalInput")
    out_t = nc.dram_tensor("out", [T, D], F32, kind="ExternalOutput")
    W = {}
    for nm, shp in [
        ("aWq", [D, D]), ("aWk", [D, D]), ("aWv", [D, D]), ("aWo", [D, D]),
        ("abq", [D]), ("abk", [D]), ("abv", [D]), ("abo", [D]),
        ("agamma", [D]), ("abeta", [D]), ("few", [6, 3]), ("fact", [6]),
        ("fWe", [20, D, D]), ("fbe", [20, D]), ("fWq", [6, D, D]),
        ("fWk", [6, D, D]), ("fWv", [6, D, D]), ("fWo", [6, D, D]),
        ("fln_g", [D]), ("fln_b", [D]),
    ]:
        W[nm] = nc.dram_tensor(nm, shp, F32, kind="ExternalInput")

    xn_sp = nc.dram_tensor("xn_sp", [T, D], BF16, kind="Internal")
    n_sp = {c: nc.dram_tensor(f"n{c}_sp", [D, T], BF16, kind="Internal")
            for c in (0, 1, 2)}
    n5_sp = nc.dram_tensor("n5_sp", [D, T], BF16, kind="Internal")
    y_sp = nc.dram_tensor("y_sp", [D, T], BF16, kind="Internal")
    dbg = {}
    if debug:
        for c in (0, 1):
            dbg[f"q{c}"] = nc.dram_tensor(f"dbg_q{c}", [128, T], BF16,
                                          kind="ExternalOutput")
            dbg[f"k{c}"] = nc.dram_tensor(f"dbg_k{c}", [128, T], BF16,
                                          kind="ExternalOutput")
            dbg[f"ctx{c}"] = nc.dram_tensor(f"dbg_ctx{c}", [128, T], BF16,
                                            kind="ExternalOutput")
            dbg[f"w{c}"] = nc.dram_tensor(f"dbg_w{c}", [1, 1024], F32,
                                          kind="ExternalOutput")
            dbg[f"vv{c}"] = nc.dram_tensor(f"dbg_vv{c}", [128, 1024], BF16,
                                           kind="ExternalOutput")

    P = Prog()
    stack = ExitStack()
    e = stack.enter_context

    slot = [e(nc.sbuf_tensor(f"slot{i}", [128, KC * T], BF16)) for i in range(4)]
    wbuf = [e(nc.sbuf_tensor(f"wbuf{i}", [128, KC * 1024], BF16)) for i in range(2)]
    wstage = [e(nc.sbuf_tensor(f"wstage{i}", [128, 1024], F32)) for i in range(2)]
    qbuf = e(nc.sbuf_tensor("qbuf", [128, T], BF16))
    kbuf = e(nc.sbuf_tensor("kbuf", [128, T], BF16))
    attb = [e(nc.sbuf_tensor(f"attb{i}", [128, max(PW, 1024)], BF16))
            for i in range(4)]
    sq_scr = attb[2]
    wb_sb = e(nc.sbuf_tensor("wb_sb", [128, PW], BF16))
    fio = [e(nc.sbuf_tensor(f"fio{i}", [128, 1024], F32)) for i in range(3)]
    ones_bf = e(nc.sbuf_tensor("ones_bf", [128, 1], BF16))
    ones_bfr = e(nc.sbuf_tensor("ones_bfr", [1, 128], BF16))
    ones_f32r = e(nc.sbuf_tensor("ones_f32r", [1, 128], F32))
    ident_f32 = e(nc.sbuf_tensor("ident_f32", [128, 128], F32))
    iscaled = e(nc.sbuf_tensor("iscaled", [128, 128], BF16))
    t128 = e(nc.sbuf_tensor("t128", [128, 128], BF16))
    beT = e(nc.sbuf_tensor("beT", [128, 160], F32))
    vecT = e(nc.sbuf_tensor("vecT", [128, 64], F32))
    vec_stage = e(nc.sbuf_tensor("vec_stage", [128, 128], F32))
    ewrow = e(nc.sbuf_tensor("ewrow", [1, 32], F32))
    scal = e(nc.sbuf_tensor("scal", [128, 32], F32))
    tiny = e(nc.sbuf_tensor("tiny", [1, 16], F32))
    lrow = fio[2]
    stat = e(nc.sbuf_tensor("stat", [128, 8], F32))
    gamT = e(nc.sbuf_tensor("gamT", [128, 2 * 1024], BF16))
    negM = e(nc.sbuf_tensor("negM", [128, 16], F32))

    bvrow = lrow.ap().bitcast(BF16)

    ps = [e(nc.psum_tensor(f"ps{i}", [128, 1024], F32)) for i in range(4)]
    ps_last = {0: None, 1: None, 2: None, 3: None}

    slot_w = [[None] * max(KC, TC) for _ in range(4)]
    slot_r = [[] for _ in range(4)]

    def slot_written(i, ch, tok):
        slot_w[i][ch] = tok
        slot_r[i] = []

    def slot_read(i, tok):
        slot_r[i].append(tok)

    def wtile(b, kc, c0, w):
        return wbuf[b][:, kc * 1024 + c0: kc * 1024 + c0 + w]

    def fchunk(s, c, t0=0, tw=None):
        tw = T if tw is None else tw
        return slot[s][:, c * T + t0: c * T + t0 + tw]

    def tchunk(s, tc, d0=0, dw=1024):
        return slot[s][:, tc * 1024 + d0: tc * 1024 + d0 + dw]

    dge, pln, trq = "gpsimd", "sync", "sync"

    def pser():
        P._reg("P")
        return [("P", P.cnt["P"])]

    t_ew = P.op(trq, lambda g: g.dma_start(
        out=ewrow[0:1, 0:18],
        in_=W["few"].ap().rearrange("a b -> (a b)")[None, :]),
        waits=pser(), inc="P", inc_n=16)
    t_aw = P.op(trq, lambda g: g.dma_start(
        out=ewrow[0:1, 18:24], in_=W["fact"].ap()[None, :]),
        waits=pser(), inc="P", inc_n=16)

    t_io1 = P.op("gpsimd", lambda g: g.iota(
        fio[0][:, 0:128], [[1, 128]], channel_multiplier=-1,
        allow_small_or_imprecise_dtypes=True), inc="tPOOL")
    t_idf = P.op("vector", lambda v: v.tensor_scalar(
        ident_f32[:, :], fio[0][:, 0:128], 0.0, None, ALU.is_equal),
        waits=[t_io1], inc="tDVE")
    t_t128 = P.op("vector", lambda v: v.tensor_scalar(
        t128[:, :], fio[0][:, 0:128], 0.0, None, ALU.is_ge),
        waits=[t_io1], inc="tDVE")
    t_on1 = P.op("gpsimd", lambda g: g.memset(ones_bf[:, :], 1.0), inc="tPOOL")
    t_on2 = P.op("gpsimd", lambda g: g.memset(ones_f32r[:, :], 1.0), inc="tPOOL")
    t_on3 = P.op("gpsimd", lambda g: g.memset(ones_bfr[:, :], 1.0), inc="tPOOL")

    t_be1 = P.op(trq, lambda g: g.dma_start(
        out=vec_stage[:, 0:128],
        in_=W["fbe"].ap().rearrange("a (c p) -> (a c) p", p=128)[0:128, :]),
        waits=pser(), inc="P", inc_n=16)
    tk = P.op("tensor", lambda t: t.transpose(
        ps[0][:, 0:128], vec_stage[:, 0:128], ident_f32[:, :]),
        waits=[t_be1, t_idf], inc="tPE")
    t_ev1 = P.op("vector", lambda v: v.tensor_copy(beT[:, 0:128], ps[0][:, 0:128]),
                 waits=[tk], inc="tDVE")
    ps_last[0] = t_ev1
    t_be2 = P.op(trq, lambda g: g.dma_start(
        out=vec_stage[0:32, 0:128],
        in_=W["fbe"].ap().rearrange("a (c p) -> (a c) p", p=128)[128:160, :]),
        waits=pser() + [tk], inc="P", inc_n=16)
    tk = P.op("tensor", lambda t: t.transpose(
        ps[1][:, 0:32], vec_stage[0:32, 0:128], ident_f32[0:32, 0:32]),
        waits=[t_be2], inc="tPE")
    t_ev2 = P.op("vector", lambda v: v.tensor_copy(beT[:, 128:160], ps[1][:, 0:32]),
                 waits=[tk], inc="tDVE")
    ps_last[1] = t_ev2

    vps = ["abq", "abk", "abv", "abo", "agamma", "abeta", "fln_g", "fln_b"]
    vtoks = []
    for i, nm in enumerate(vps):
        vtoks.append(P.op(trq, lambda g, nm=nm, i=i: g.dma_start(
            out=vec_stage[8 * i:8 * i + 8, 0:128],
            in_=W[nm].ap().rearrange("(c p) -> c p", p=128)),
            waits=pser() + ([t_ev2] if i == 0 else []), inc="P", inc_n=16))
    tk = P.op("tensor", lambda t: t.transpose(
        ps[0][:, 0:64], vec_stage[0:64, 0:128], ident_f32[0:64, 0:64]),
        waits=vtoks + [ps_last[0]], inc="tPE")
    t_vec = P.op("vector", lambda v: v.tensor_copy(vecT[:, 0:64], ps[0][:, 0:64]),
                 waits=[tk], inc="tDVE")
    ps_last[0] = t_vec

    def vcol(name, c):
        i = vps.index(name)
        return vecT[:, i * 8 + c: i * 8 + c + 1]

    tks = []
    for j in range(24):
        tks.append(P.op("tensor", lambda t, j=j: t.matmul(
            ps[1][:, j:j + 1], ones_f32r[0:1, :], ewrow[0:1, j:j + 1],
            start=True, stop=True),
            waits=[t_ew, t_aw, t_on2, ps_last[1]] if j == 0 else [], inc="tPE"))
    t_scal = P.op("vector", lambda v: v.tensor_copy(scal[:, 0:24], ps[1][:, 0:24]),
                  waits=[tks[-1]], inc="tDVE")
    ps_last[1] = t_scal

    xn_wtoks = []
    lrow_gate = []
    fio_free = [t_t128, None, None]
    attb_free = [[], [], [], []]
    for i in range(TC):
        b = i % 2
        tx = P.op(pln, lambda g, i=i, b=b: g.dma_start(
            out=fio[b][:, :], in_=x_in.ap()[i * 128:(i + 1) * 128, :]),
            waits=[fio_free[b]], inc=f"X{b}", inc_n=16)
        t1 = P.op("vector", lambda v, b=b: v.reduce_sum(
            stat[:, 0:1], fio[b][:, :], axis=AX.X), waits=[tx], inc="tDVE")
        t2 = P.op("vector", lambda v: v.tensor_scalar(
            stat[:, 1:2], stat[:, 0:1], 1.0 / D, None, ALU.mult),
            waits=[t1], inc="tDVE")
        t3 = P.op("vector", lambda v, b=b: v.tensor_scalar(
            fio[2][:, :], fio[b][:, :], stat[:, 1:2], None, ALU.subtract),
            waits=[t2, fio_free[2]], inc="tDVE")
        fio_free[b] = t3
        t4 = P.op("vector", lambda v, b=b: v.scalar_tensor_tensor(
            attb[b][:, 0:1024], fio[2][:, :], 1.0, fio[2][:, :],
            ALU.mult, ALU.mult, accum_out=stat[:, 2:3]),
            waits=[t3] + attb_free[b], inc="tDVE")
        t5 = P.op("vector", lambda v: v.tensor_scalar(
            stat[:, 3:4], stat[:, 2:3], 1.0 / D, 1e-5, ALU.mult, ALU.add),
            waits=[t4], inc="tDVE")
        t6 = P.op("scalar", lambda s: s.activation(
            stat[:, 4:5], stat[:, 3:4], AF.Ln), waits=[t5], inc="tACT")
        t7 = P.op("scalar", lambda s: s.activation(
            stat[:, 5:6], stat[:, 4:5], AF.Exp, scale=-0.5),
            waits=[t6], inc="tACT")
        t8 = P.op("vector", lambda v, b=b: v.tensor_scalar(
            attb[b][:, 0:1024], fio[2][:, :], stat[:, 5:6], None, ALU.mult),
            waits=[t7], inc="tDVE")
        fio_free[2] = t8
        td = P.op(pln, lambda g, i=i, b=b: g.dma_start(
            out=xn_sp.ap()[i * 128:(i + 1) * 128, :], in_=attb[b][:, 0:1024]),
            waits=[t8], inc=f"XN{b}", inc_n=16)
        attb_free[b] = [td]
        xn_wtoks.append(td)

    xn_done = xn_wtoks[-2:]
    lrow_gate.extend(xn_done)
    A, B, C, Dv = 0, 1, 2, 3
    for c in range(KC):
        tk = P.op(trq, lambda g, c=c: g.dma_start(
            out=fchunk(A, c), in_=xn_sp.ap()[:, c * 128:(c + 1) * 128],
            transpose=True), waits=xn_done, inc="F0", inc_n=16)
        slot_written(A, c, tk)

    wb_state = {"i": 0}
    wbuf_free = [[], []]
    wstage_free = [None, None]

    def load_W(dram_ap, slot_override=None):
        if slot_override is None:
            b = wb_state["i"] % 2
            wb_state["i"] += 1
        else:
            b = slot_override
        gate = list(wbuf_free[b])
        tcast = None
        for q in range(KC):
            sb_ = q % 2
            tld = P.op(pln, lambda g, q=q, sb_=sb_, a=dram_ap: g.dma_start(
                out=wstage[sb_][:, :],
                in_=a.rearrange("(kc p) n -> p kc n", p=128)[:, q:q + 1, :]),
                waits=[wstage_free[sb_]], inc=f"WS{sb_}", inc_n=16)
            tcast = P.op("gpsimd", lambda g, q=q, sb_=sb_, b=b: g.tensor_copy(
                wbuf[b][:, q * 1024:(q + 1) * 1024], wstage[sb_][:, :]),
                waits=[tld] + (gate if q < 2 else []), inc="tPOOL")
            wstage_free[sb_] = tcast
        wbuf_free[b] = []
        return b, tcast

    def projection(src_slot, dst_slot, wslot, wtok, bias_fn=None,
                   iadd_slot=None, iadd_tok=None, spill=None, spill_sem=None):
        spills = []
        last_ev = None
        dst_gate = list(slot_r[dst_slot])
        TW = min(1024, T)
        NTW = T // TW
        for do in range(KC):
            ev_th = []
            for th in range(NTW):
                pb = (do * NTW + th) % 4
                mmw = [wtok, ps_last[pb]] + dst_gate + [slot_w[src_slot][kc]
                                                        for kc in range(KC)]
                if iadd_slot is not None:
                    mmw += [slot_w[iadd_slot][do], iadd_tok]
                t_mm = None
                for kc in range(KC):
                    for sg in range(TW // SEG):
                        t_mm = P.op("tensor", lambda t, do=do, th=th, kc=kc,
                                    pb=pb, sg=sg, ws=wslot, ss=src_slot:
                                    t.matmul(
                            ps[pb][:, sg * SEG:(sg + 1) * SEG],
                            wtile(ws, kc, do * 128, 128),
                            fchunk(ss, kc, th * TW + sg * SEG, SEG),
                            start=(kc == 0),
                            stop=(kc == KC - 1 and iadd_slot is None)),
                            waits=mmw if (kc == 0 and sg == 0) else [],
                            inc="tPE")
                if iadd_slot is not None:
                    for sg in range(TW // SEG):
                        t_mm = P.op("tensor", lambda t, do=do, th=th, pb=pb,
                                    sg=sg, s2=iadd_slot: t.matmul(
                            ps[pb][:, sg * SEG:(sg + 1) * SEG], iscaled[:, :],
                            fchunk(s2, do, th * TW + sg * SEG, SEG),
                            start=False, stop=True), inc="tPE")
                    slot_read(iadd_slot, t_mm)
                slot_read(src_slot, t_mm)
                eng = "scalar" if (do * NTW + th) % 2 == 0 else "vector"

                def ev_s(s, do=do, th=th, pb=pb, ds=dst_slot, TW=TW):
                    if bias_fn is not None:
                        return s.activation(
                            fchunk(ds, do, th * TW, TW), ps[pb][:, 0:TW],
                            AF.Identity, bias=bias_fn(do))
                    return s.activation(
                        fchunk(ds, do, th * TW, TW), ps[pb][:, 0:TW], AF.Copy)

                def ev_v(v, do=do, th=th, pb=pb, ds=dst_slot, TW=TW):
                    if bias_fn is not None:
                        return v.tensor_scalar(
                            fchunk(ds, do, th * TW, TW), ps[pb][:, 0:TW],
                            bias_fn(do), None, ALU.add)
                    return v.tensor_copy(
                        fchunk(ds, do, th * TW, TW), ps[pb][:, 0:TW])
                if eng == "scalar":
                    t_ev = P.op("scalar", ev_s, waits=[t_mm], inc="tACT")
                else:
                    t_ev = P.op("vector", ev_v, waits=[t_mm], inc="tDVE")
                ps_last[pb] = t_ev
                ev_th.append(t_ev)
                last_ev = t_ev
            slot_written(dst_slot, do, ev_th[-1])
            for t_ in ev_th[:-1]:
                slot_w[dst_slot][do] = ev_th[-1]
            if spill is not None:
                sp = P.op(pln, lambda g, do=do, a=spill, ds=dst_slot: g.dma_start(
                    out=a.ap().rearrange("(c p) t -> c p t", p=128)[do],
                    in_=fchunk(ds, do)), waits=ev_th, inc=spill_sem, inc_n=16)
                slot_read(dst_slot, sp)
                spills.append(sp)
        return last_ev, spills

    def projection_tmajor(src_slot, dst_slot, wslot, wtok, scale_col=None,
                          bias_row=None, bias_tok=None):
        last_ev = None
        dst_gate = list(slot_r[dst_slot])
        for tc in range(TC):
            pb = tc % 4
            mmw = [wtok, ps_last[pb]] + dst_gate + [slot_w[src_slot][kc]
                                                    for kc in range(KC)]
            t_mm = None
            for kc in range(KC):
                for half in range(2):
                    t_mm = P.op("tensor", lambda t, tc=tc, kc=kc, half=half,
                                pb=pb, ws=wslot, ss=src_slot: t.matmul(
                        ps[pb][:, half * 512:(half + 1) * 512],
                        fchunk(ss, kc, tc * 128, 128),
                        wtile(ws, kc, half * 512, 512),
                        start=(kc == 0),
                        stop=(kc == KC - 1 and bias_row is None)),
                        waits=mmw if (kc == 0 and half == 0) else [], inc="tPE")
            if bias_row is not None:
                for half in range(2):
                    t_mm = P.op("tensor", lambda t, half=half, pb=pb: t.matmul(
                        ps[pb][:, half * 512:(half + 1) * 512], ones_bfr[0:1, :],
                        bias_row[0:1, half * 512:(half + 1) * 512],
                        start=False, stop=True),
                        waits=[t_on3, bias_tok], inc="tPE")
            slot_read(src_slot, t_mm)

            def ev(s, tc=tc, pb=pb, ds=dst_slot):
                return s.activation(
                    tchunk(ds, tc), ps[pb][:, 0:1024], AF.Copy,
                    scale=(scale_col if scale_col is not None else 1.0))
            t_ev = P.op("scalar", ev, waits=[t_mm], inc="tACT")
            ps_last[pb] = t_ev
            slot_written(dst_slot, tc % max(KC, TC), t_ev)
            last_ev = t_ev
        return last_ev

    def fill_slot(dst_slot, src_dram, fmajor_src, src_ready):
        gate = list(slot_r[dst_slot]) + list(src_ready)
        for c in range(KC):
            if fmajor_src:
                tk = P.op(pln, lambda g, c=c, a=src_dram, ds=dst_slot: g.dma_start(
                    out=fchunk(ds, c),
                    in_=a.ap().rearrange("(c p) t -> c p t", p=128)[c]),
                    waits=gate if c == 0 else [], inc=f"F{dst_slot}", inc_n=16)
            else:
                tk = P.op(trq, lambda g, c=c, a=src_dram, ds=dst_slot: g.dma_start(
                    out=fchunk(ds, c), in_=a.ap()[:, c * 128:(c + 1) * 128],
                    transpose=True), waits=gate if c == 0 else [],
                    inc=f"F{dst_slot}", inc_n=16)
            slot_written(dst_slot, c, tk)

    def emit_negM(q_ap_fn, k_ap_fn, base, dh, abs_ap, abs_imm, ready,
                  ready_fn=None, out_col=0, sst_eng="vector"):
        prev = None
        for idx, apfn in enumerate((q_ap_fn, k_ap_fn)):
            for ha in range(NHA):
                rd = (list(ready) if ready is not None
                      else list(ready_fn(idx, ha)))
                t_sq = P.op(sst_eng, lambda v, apfn=apfn, ha=ha, base=base,
                            dh=dh: v.tensor_tensor(
                    sq_scr[base:base + dh, 0:NH], apfn(ha), apfn(ha),
                    ALU.mult),
                    waits=rd + [prev] + attb_free[2] + negm_m_last,
                    inc=("tPOOL" if sst_eng == "gpsimd" else "tDVE"))
                t_nm = None
                for sg in range(0, NH, 512):
                    sw = min(512, NH - sg)
                    t_nm = P.op("tensor", lambda t, sg=sg, sw=sw, base=base,
                                dh=dh: t.matmul(
                        ps[3][0:1, sg:sg + sw], ones_bf[base:base + dh, 0:1],
                        sq_scr[base:base + dh, sg:sg + sw],
                        start=True, stop=True),
                        waits=[t_sq, ps_last[3], t_on1] if sg == 0 else [],
                        inc="tPE")
                t_mx = P.op("vector", lambda v, idx=idx, ha=ha: v.reduce_max(
                    tiny[0:1, idx * NHA + ha: idx * NHA + ha + 1],
                    ps[3][0:1, 0:NH], axis=AX.X), waits=[t_nm], inc="tDVE")
                ps_last[3] = t_mx
                attb_free[2] = [t_nm]
                prev = t_mx
        if NHA > 1:
            for idx in range(2):
                prev = P.op("vector", lambda v, idx=idx: v.tensor_tensor(
                    tiny[0:1, idx * NHA: idx * NHA + 1],
                    tiny[0:1, idx * NHA: idx * NHA + 1],
                    tiny[0:1, idx * NHA + 1: idx * NHA + 2], ALU.max),
                    waits=[prev], inc="tDVE")
        t_p = P.op("vector", lambda v: v.tensor_tensor(
            tiny[0:1, 8:9], tiny[0:1, 0:1], tiny[0:1, NHA:NHA + 1], ALU.add),
            waits=[prev], inc="tDVE")
        if abs_ap is not None:
            t_m = P.op("vector", lambda v, a=abs_ap: v.tensor_tensor(
                tiny[0:1, 11:12], tiny[0:1, 8:9], a, ALU.mult),
                waits=[t_p], inc="tDVE")
        else:
            t_m = P.op("vector", lambda v: v.tensor_scalar(
                tiny[0:1, 11:12], tiny[0:1, 8:9], abs_imm, None, ALU.mult),
                waits=[t_p], inc="tDVE")

        negm_m_last[:] = [t_m]

        def emitB():
            t_r = P.op("tensor", lambda t: t.matmul(
                ps[3][:, 0:1], ones_f32r[0:1, :], tiny[0:1, 11:12],
                start=True, stop=True), waits=[t_m, ps_last[3], t_on2],
                inc="tPE")
            t_n = P.op("vector", lambda v, oc=out_col: v.tensor_scalar(
                negM[:, oc:oc + 1], ps[3][:, 0:1], -1.0, None, ALU.mult),
                waits=[t_r], inc="tDVE")
            ps_last[3] = t_n
            return t_n
        return emitB

    dbg_gate = []
    qbuf_gate = []
    wrow_gate = []

    def attention_qtile(kf_tile_fn, q_rhs_fn, v_tile_fn, dh, obase, kc_list,
                        diag_off, scale_col, negM_fn, wfold_11, ctx_dst_ap,
                        extra_ready, pending_fin=None, negM_col=0):
        n_kc = len(kc_list)
        t_avl = None
        t_lmm = None
        tp = (0, obase) if obase else None
        ready_q = []
        scores_tok = {}

        def emit_scores(j):
            kc = kc_list[j]
            pb = j % 2
            mmw = [ps_last[pb]] + list(extra_ready)
            t_s = None
            for sg in range(NSEG_P):
                t_s = P.op("tensor", lambda t, kc=kc, sg=sg, pb=pb: t.matmul(
                    ps[pb][:, sg * PSEG:(sg + 1) * PSEG],
                    kf_tile_fn(kc), q_rhs_fn(sg * PSEG, PSEG),
                    start=True, stop=True),
                    waits=mmw if sg == 0 else [], inc="tPE")
            scores_tok[j] = t_s

        def seg_ranges(off):
            out = []
            x = off
            while x < PW:
                nxt = min(PW, (x // PSEG + 1) * PSEG)
                out.append((x, nxt - x))
                x = nxt
            return out

        def emit_lav(j, ab, t_ready, off=0, l_first=False, l_last=False,
                     av_first=False, av_stop=False):
            nonlocal t_lmm, t_avl
            kc = kc_list[j]
            for (o, w) in seg_ranges(off):
                t_lmm = P.op("tensor", lambda t, o=o, w=w, ab=ab,
                             l_first=l_first, l_last=l_last: t.matmul(
                    ps[3][0:1, o:o + w], ones_bf[:, 0:1],
                    attb[ab][:, o:o + w],
                    start=l_first, stop=l_last),
                    waits=[t_ready, t_on1] + ([ps_last[3]] if l_first else []),
                    inc="tPE")
            for (o, w) in seg_ranges(off):
                t_avl = P.op("tensor", lambda t, kc=kc, o=o, w=w, ab=ab, dh=dh,
                             obase=obase, tp=tp, av_first=av_first,
                             av_stop=av_stop: t.matmul(
                    ps[2][obase:obase + dh, o:o + w],
                    v_tile_fn(kc), attb[ab][:, o:o + w],
                    start=av_first, stop=av_stop, tile_position=tp),
                    waits=([ps_last[2]] if av_first else []), inc="tPE")
            attb_free[ab] = [t_avl]

        def pair_a(e0, e1):
            nonlocal t_avl, t_lmm
            j0, a0, r0, _ = e0
            j1, a1, r1, _ = e1
            t_av0 = None
            for (o, w) in seg_ranges(0):
                t_av0 = P.op("tensor", lambda t, kc0=kc_list[j0], o=o, w=w,
                             a0=a0, dh=dh, obase=obase, tp=tp,
                             first=(j0 == 0): t.matmul(
                    ps[2][obase:obase + dh, o:o + w],
                    v_tile_fn(kc0), attb[a0][:, o:o + w],
                    start=first, stop=False, tile_position=tp),
                    waits=[r0] + ([ps_last[2]] if j0 == 0 else []), inc="tPE")
            t_pr = P.op("vector", lambda v, a0=a0, a1=a1: v.tensor_tensor(
                attb[a0][:, 0:PW], attb[a0][:, 0:PW], attb[a1][:, 0:PW],
                ALU.add), waits=[r0, r1, t_av0], inc="tDVE")

            def pair_b():
                nonlocal t_avl, t_lmm
                t_l = None
                for (o, w) in seg_ranges(0):
                    t_l = P.op("tensor", lambda t, o=o, w=w, a0=a0,
                               first=(j0 == 0), last=(j1 == n_kc - 1):
                               t.matmul(
                        ps[3][0:1, o:o + w], ones_bf[:, 0:1],
                        attb[a0][:, o:o + w], start=first, stop=last),
                        waits=[t_pr, t_on1] +
                        ([ps_last[3]] if j0 == 0 else []), inc="tPE")
                t_av2 = None
                for (o, w) in seg_ranges(0):
                    t_av2 = P.op("tensor", lambda t, kc1=kc_list[j1], o=o,
                                 w=w, a1=a1, dh=dh, obase=obase, tp=tp,
                                 last=(j1 == n_kc - 1): t.matmul(
                        ps[2][obase:obase + dh, o:o + w],
                        v_tile_fn(kc1), attb[a1][:, o:o + w],
                        start=False, stop=last, tile_position=tp),
                        waits=[r1], inc="tPE")
                t_lmm = t_l
                attb_free[a0] = [t_l]
                attb_free[a1] = [t_pr, t_av2]
                t_avl = t_av2
            return pair_b

        P.tag = P.tag + "/att"
        pair_mode = (diag_off is None and n_kc % 2 == 0 and n_kc >= 4)
        stash = [None]

        for j in range(min(2, n_kc)):
            emit_scores(j)
        if pending_fin is not None:
            pending_fin()
        t_negM = negM_fn()
        for j, kc in enumerate(kc_list):
            if j >= 2:
                emit_scores(j)
            if stash[0] is not None:
                stash[0]()
                stash[0] = None
            pb = j % 2
            ab = j % (4 if pair_mode else 3)
            off = diag_off.get(kc, 0) if diag_off is not None else 0
            t_x = P.op("scalar", lambda s, pb=pb, ab=ab, nc_=negM_col,
                       off=off: s.activation(
                attb[ab][:, off:PW], ps[pb][:, off:PW], AF.Exp,
                bias=negM[:, nc_:nc_ + 1], scale=scale_col),
                waits=[scores_tok[j], t_negM] + attb_free[ab], inc="tACT")
            ps_last[pb] = t_x
            t_ready = t_x
            if diag_off is not None and kc in diag_off:
                t_ready = P.op("vector", lambda v, ab=ab, off=off: v.tensor_tensor(
                    attb[ab][:, off:off + 128], attb[ab][:, off:off + 128],
                    t128[:, :], ALU.mult), waits=[t_x, t_t128], inc="tDVE")
            ready_q.append((j, ab, t_ready, off))
            if pair_mode and len(ready_q) >= 4:
                stash[0] = pair_a(ready_q.pop(0), ready_q.pop(0))
            elif not pair_mode and len(ready_q) >= 3:
                jj, aa, rr, oo = ready_q.pop(0)
                emit_lav(jj, aa, rr, off=oo, l_first=(jj == 0),
                         l_last=(jj == n_kc - 1), av_first=(jj == 0),
                         av_stop=(jj == n_kc - 1))
        if stash[0] is not None:
            stash[0]()
            stash[0] = None
        while ready_q:
            if pair_mode and len(ready_q) >= 2:
                pair_a(ready_q.pop(0), ready_q.pop(0))()
            else:
                jj, aa, rr, oo = ready_q.pop(0)
                emit_lav(jj, aa, rr, off=oo, l_first=(jj == 0),
                         l_last=(jj == n_kc - 1), av_first=(jj == 0),
                         av_stop=(jj == n_kc - 1))
        t_lmm_f = t_lmm
        t_avl_f = t_avl
        t_le = P.op("scalar", lambda s: s.activation(
            lrow[0:1, 0:PW], ps[3][0:1, 0:PW], AF.Ln),
            waits=[t_lmm_f] + dbg_gate + lrow_gate + wrow_gate, inc="tACT")
        ps_last[3] = t_le
        t_rc = P.op("scalar", lambda s, a=wfold_11: s.activation(
            wb_sb[0:1, 0:PW], lrow[0:1, 0:PW], AF.Exp, scale=-1.0,
            bias=(a if a is not None else 0.0)),
            waits=[t_le, ps_last[2]] + wrow_gate, inc="tACT")

        def fin(cbs=()):
            t_wb = None
            for sg in range(NSEG_P):
                t_wb = P.op("tensor", lambda t, sg=sg, dh=dh,
                            obase=obase, tp=tp: t.matmul(
                    ps[3][obase:obase + dh, sg * PSEG:(sg + 1) * PSEG],
                    ones_bfr[0:1, 0:dh],
                    wb_sb[0:1, sg * PSEG:(sg + 1) * PSEG],
                    start=True, stop=True, tile_position=tp),
                    waits=[t_rc, ps_last[3], t_on3] if sg == 0 else [],
                    inc="tPE")
            wrow_gate[:] = [t_wb]
            t_we = P.op("vector", lambda v, dh=dh, obase=obase:
                        v.tensor_copy(wb_sb[obase:obase + dh, 0:PW],
                                      ps[3][obase:obase + dh, 0:PW]),
                        waits=[t_wb], inc="tDVE")
            ps_last[3] = t_we
            t_fin = P.op("vector", lambda v, dh=dh, obase=obase,
                         dst=ctx_dst_ap: v.tensor_tensor(
                dst, ps[2][obase:obase + dh, 0:PW],
                wb_sb[obase:obase + dh, 0:PW], ALU.mult),
                waits=[t_we, t_avl_f], inc="tDVE")
            ps_last[2] = t_fin
            for cb in cbs:
                cb(t_fin)
            return t_fin
        return fin

    meta = ffn_dag_meta()
    iscaled_free = []
    lnaw_gate = [None]
    negm_m_last = [None]
    pending = [None]

    def defer_fin(fin, cbs):
        def run():
            return fin(cbs=cbs)
        pending[0] = run

    def take_fin():
        p = pending[0]
        pending[0] = None
        return p

    def drain_fin():
        p = take_fin()
        if p is not None:
            p()
    weq_pre = load_W(W["fWe"].ap()[meta[0]["eq"]], slot_override=0)
    last_spills = {}

    for nd in meta:
        c = nd["c"]
        t_sc1 = P.op("vector", lambda v, c=c: v.tensor_tensor(
            scal[:, 24:25], scal[:, 3 * c:3 * c + 1], scal[:, 3 * c + 1:3 * c + 2],
            ALU.mult), waits=[t_scal], inc="tDVE")
        t_sc2 = P.op("vector", lambda v: v.tensor_scalar(
            scal[:, 24:25], scal[:, 24:25], 1.0 / np.sqrt(128.0), None,
            ALU.mult), waits=[t_sc1], inc="tDVE")
        t_sc3 = P.op("vector", lambda v: v.tensor_tensor(
            scal[:, 25:26], scal[:, 24:25], scal[:, 24:25], ALU.mult),
            waits=[t_sc2], inc="tDVE")
        t_isc = P.op("vector", lambda v, c=c: v.tensor_scalar(
            iscaled[:, :], ident_f32[:, :], scal[:, 3 * c:3 * c + 1], None,
            ALU.mult), waits=[t_scal, t_idf] + iscaled_free, inc="tDVE")
        iscaled_free = []
        t_lnaw = P.op("scalar", lambda s, c=c: s.activation(
            tiny[0:1, 12:13], ewrow[0:1, 18 + c:19 + c], AF.Ln),
            waits=[t_ew, t_aw] + lnaw_gate, inc="tACT")
        lnaw_gate = [t_lnaw]
        t_sf1 = P.op("scalar", lambda s: s.activation(
            tiny[0:1, 13:14], scal[0:1, 25:26], AF.Ln),
            waits=[t_sc3], inc="tACT")
        t_sf2 = P.op("scalar", lambda s: s.activation(
            tiny[0:1, 14:15], tiny[0:1, 13:14], AF.Exp, scale=0.5),
            waits=[t_sf1], inc="tACT")
        t_sf3 = P.op("vector", lambda v: v.tensor_scalar(
            tiny[0:1, 15:16], tiny[0:1, 14:15], 0.51, None, ALU.mult),
            waits=[t_sf2] + negm_m_last, inc="tDVE")
        negm_m_last = [t_sf3]

        P.tag = f"n{c}:pq"
        X1, X2 = c % 2, 1 - c % 2
        wq_b, wq_t = weq_pre
        t_pq, _ = projection(A, B, wq_b, wq_t,
                             bias_fn=lambda do, eq=nd["eq"]:
                             beT[:, eq * 8 + do:eq * 8 + do + 1])
        wbuf_free[wq_b].append(t_pq)

        if c == 0:
            pk_slot = B
        else:
            if nd["in_k"] == -1:
                fill_slot(Dv, xn_sp, False, xn_done)
            else:
                fill_slot(Dv, n_sp[nd["in_k"]], True,
                          last_spills.get(nd["in_k"], []))
            P.tag = f"n{c}:pk"
            wk_b, wk_t = load_W(W["fWe"].ap()[nd["ek"]], slot_override=X2)
            t_pk, _ = projection(Dv, C, wk_b, wk_t,
                                 bias_fn=lambda do, ek=nd["ek"]:
                                 beT[:, ek * 8 + do:ek * 8 + do + 1])
            wbuf_free[wk_b].append(t_pk)
            pk_slot = C

        if c <= 1:
            pv_slot = B
        else:
            fill_slot(A, n_sp[nd["in_v"]], True,
                      last_spills.get(nd["in_v"], []))
            P.tag = f"n{c}:pv"
            wv_b, wv_t = load_W(W["fWe"].ap()[nd["ev"]], slot_override=X1)
            t_pv, _ = projection(A, Dv, wv_b, wv_t,
                                 bias_fn=lambda do, ev=nd["ev"]:
                                 beT[:, ev * 8 + do:ev * 8 + do + 1])
            wbuf_free[wv_b].append(t_pv)
            pv_slot = Dv

        P.tag = f"n{c}:vv"
        wvv_b, wvv_t = load_W(W["fWv"].ap()[c], slot_override=X2)
        if c <= 1:
            gate = list(slot_r[A])
            if gate:
                P.op("tensor", lambda t: t.nop(), waits=gate)
        t_vv = projection_tmajor(pv_slot, A, wvv_b, wvv_t,
                                 scale_col=scal[:, 3 * c + 2:3 * c + 3])
        wbuf_free[wvv_b].append(t_vv)

        wq_b2, wq_t2 = load_W(W["fWq"].ap()[c], slot_override=X1)
        wk_b2, wk_t2 = load_W(W["fWk"].ap()[c], slot_override=X2)
        wo_t = None
        wo_b = None

        for h in range(8):
            P.tag = f"n{c}:h{h}:qk"
            qk_done = []
            TW2 = min(1024, T)
            NTW2 = T // TW2
            for which, (bufap, wsl, wtk, src_sl) in enumerate(
                    [(qbuf, wq_b2, wq_t2, B), (kbuf, wk_b2, wk_t2, pk_slot)]):
                if which == 1:
                    drain_fin()
                for th in range(NTW2):
                    pb = (which * NTW2 + th) % 2
                    mmw = [wtk, ps_last[pb]] + [slot_w[src_sl][kc]
                                                for kc in range(KC)]
                    t_mm = None
                    for kc in range(KC):
                        for sg in range(TW2 // SEG):
                            t_mm = P.op("tensor", lambda t, kc=kc, th=th,
                                        pb=pb, sg=sg, wsl=wsl, h=h,
                                        src_sl=src_sl: t.matmul(
                                ps[pb][:, sg * SEG:(sg + 1) * SEG],
                                wtile(wsl, kc, h * 128, 128),
                                fchunk(src_sl, kc, th * TW2 + sg * SEG, SEG),
                                start=(kc == 0), stop=(kc == KC - 1)),
                                waits=mmw if (kc == 0 and sg == 0) else [],
                                inc="tPE")
                    slot_read(src_sl, t_mm)
                    t_ev = P.op("vector", lambda v, th=th, pb=pb,
                                bufap=bufap, TW2=TW2: v.tensor_copy(
                        bufap[:, th * TW2:(th + 1) * TW2],
                        ps[pb][:, 0:TW2]),
                        waits=[t_mm] + qbuf_gate, inc="tDVE")
                    ps_last[pb] = t_ev
                    qk_done.append(t_ev)
            if debug and c in (0, 1) and h == 0:
                tdq = P.op(pln, lambda g, c=c: g.dma_start(
                    out=dbg[f"q{c}"].ap(), in_=qbuf[:, :]),
                    waits=qk_done, inc="DBG", inc_n=16)
                tdk = P.op(pln, lambda g, c=c: g.dma_start(
                    out=dbg[f"k{c}"].ap(), in_=kbuf[:, :]),
                    waits=qk_done, inc="DBG", inc_n=16)
                qbuf_gate[:] = [tdq, tdk]
            if h == 7:
                wbuf_free[wq_b2].append(qk_done[NSEG_T - 1])
                wbuf_free[wk_b2].append(qk_done[-1])
                wo_b, wo_t = load_W(W["fWo"].ap()[c], slot_override=X1)
                if c < 5:
                    nxt = meta[c + 1]
                    weq_pre = load_W(W["fWe"].ap()[nxt["eq"]],
                                     slot_override=X2)
                else:
                    weq_pre = load_W(W["aWq"].ap(), slot_override=X2)
            drain_fin()
            negM_B = emit_negM(
                lambda ha: qbuf[:, ha * NH:(ha + 1) * NH],
                lambda ha: kbuf[:, ha * NH:(ha + 1) * NH],
                0, 128, tiny[0:1, 15:16], None, None,
                ready_fn=lambda idx, ha, NTW2=NTW2: [qk_done[min(
                    idx * NTW2 + ha, len(qk_done) - 1)]])
            negM_tok = [None]

            def negM_fn(B=negM_B, tok=negM_tok):
                if tok[0] is None:
                    tok[0] = B()
                return tok[0]
            for qt in range(NQ):
                fin = attention_qtile(
                    kf_tile_fn=lambda kc: kbuf[:, kc * 128:(kc + 1) * 128],
                    q_rhs_fn=lambda o, w, qt=qt: qbuf[:, qt * PW + o:
                                                      qt * PW + o + w],
                    v_tile_fn=lambda kc, h=h, A=A: slot[A][
                        :, kc * 1024 + h * 128: kc * 1024 + h * 128 + 128],
                    dh=128, obase=0, kc_list=list(range(TC)), diag_off=None,
                    scale_col=scal[:, 24:25], negM_fn=negM_fn,
                    wfold_11=tiny[0:1, 12:13],
                    ctx_dst_ap=fchunk(Dv, h, qt * PW, PW),
                    extra_ready=qk_done + [t_vv], pending_fin=take_fin())
                defer_fin(fin, [lambda t, A=A: slot_read(A, t),
                                lambda t, h=h: slot_written(Dv, h, t)])
        drain_fin()
        P.tag = f"n{c}:oproj"
        spill_t = n_sp.get(c)
        t_o, sp_toks = projection(Dv, B, wo_b, wo_t, bias_fn=None,
                                  iadd_slot=B, iadd_tok=t_isc, spill=spill_t,
                                  spill_sem=f"SPn{c}")
        if spill_t is not None:
            last_spills[c] = sp_toks
        iscaled_free = [t_o]
        wbuf_free[wo_b].append(t_o)
        A, B = B, A

    n5_spills = []
    for ch in range(KC):
        sp = P.op(pln, lambda g, ch=ch, A=A: g.dma_start(
            out=n5_sp.ap().rearrange("(c p) t -> c p t", p=128)[ch],
            in_=fchunk(A, ch)), waits=[slot_w[A][ch]], inc="SPn5", inc_n=16)
        slot_read(A, sp)
        n5_spills.append(sp)

    P.tag = "L0:qkv"
    fill_slot(B, xn_sp, False, xn_done)
    wq_b, wq_t = weq_pre
    t_q, _ = projection(B, C, wq_b, wq_t, bias_fn=lambda do: vcol("abq", do))
    wbuf_free[wq_b].append(t_q)
    wk_b, wk_t = load_W(W["aWk"].ap(), slot_override=1 - wq_b)
    t_k, _ = projection(B, Dv, wk_b, wk_t, bias_fn=lambda do: vcol("abk", do))
    wbuf_free[wk_b].append(t_k)
    t_bv = P.op(dge, lambda g: g.dma_start(
        out=bvrow[0:1, 0:1024], in_=W["abv"].ap()[None, :]),
        waits=[ps_last[3]], inc="BV", inc_n=16)
    wv_b, wv_t = load_W(W["aWv"].ap(), slot_override=wq_b)
    gate = list(slot_r[A])
    if gate:
        P.op("tensor", lambda t: t.nop(), waits=gate)
    t_v = projection_tmajor(B, A, wv_b, wv_t, bias_row=bvrow, bias_tok=t_bv)
    wbuf_free[wv_b].append(t_v)
    wo_b, wo_t = load_W(W["aWo"].ap(), slot_override=1 - wv_b)

    NDIAG = PW // 128
    l0_negM = {}

    def l0_chain(hh):
        ch_, pr_ = hh // 2, (hh % 2) * 64
        eB = emit_negM(
            lambda ha, ch=ch_, pr=pr_: slot[C][
                pr:pr + 64, ch * T + ha * NH: ch * T + (ha + 1) * NH],
            lambda ha, ch=ch_, pr=pr_: slot[Dv][
                pr:pr + 64, ch * T + ha * NH: ch * T + (ha + 1) * NH],
            pr_, 64, None, 0.06375, [t_q, t_k], out_col=hh)
        l0_negM[hh] = eB()

    l0_chain(0)
    l0_chain(1)
    for h in range(16):
        ch, pr = h // 2, (h % 2) * 64
        P.tag = f"L0:h{h}"
        drain_fin()
        if h + 2 < 16:
            l0_chain(h + 2)

        def negM_fn(h=h):
            return l0_negM[h]
        for qt in range(NQ):
            kc_hi = (qt + 1) * NDIAG
            doff = {kc: kc * 128 - qt * PW for kc in range(qt * NDIAG, kc_hi)}
            fin = attention_qtile(
                kf_tile_fn=lambda kc, ch=ch, pr=pr: slot[Dv][
                    pr:pr + 64, ch * T + kc * 128: ch * T + (kc + 1) * 128],
                q_rhs_fn=lambda o, w, ch=ch, pr=pr, qt=qt: slot[C][
                    pr:pr + 64, ch * T + qt * PW + o: ch * T + qt * PW + o + w],
                v_tile_fn=lambda kc, h=h, A=A: slot[A][
                    :, kc * 1024 + h * 64: kc * 1024 + h * 64 + 64],
                dh=64, obase=pr, kc_list=list(range(kc_hi)), diag_off=doff,
                scale_col=0.125, negM_fn=negM_fn, wfold_11=None,
                ctx_dst_ap=slot[B][pr:pr + 64,
                                   ch * T + qt * PW: ch * T + qt * PW + PW],
                extra_ready=[t_v], pending_fin=take_fin(), negM_col=h)
            defer_fin(fin, [lambda t, A=A: slot_read(A, t),
                            lambda t: slot_read(C, t),
                            lambda t: slot_read(Dv, t),
                            lambda t, ch=ch: slot_written(B, ch, t)])
    drain_fin()
    P.tag = "L0:yproj"
    t_y, y_spills = projection(B, C, wo_b, wo_t,
                               bias_fn=lambda do: vcol("abo", do), spill=y_sp,
                               spill_sem="SPy")
    wbuf_free[wo_b].append(t_y)

    def build_gam(i0, i1, ready):
        toks = []
        for j, i in enumerate((i0, i1)):
            tr = P.op(trq, lambda g, i=i: g.dma_start(
                out=lrow[0:1, 0:1024], in_=W[vps[4 + i]].ap()[None, :]),
                waits=pser() + list(ready) + toks, inc="P", inc_n=16)
            for h2 in range(2):
                tm = P.op("tensor", lambda t, j=j, h2=h2: t.matmul(
                    ps[3][:, h2 * 512:(h2 + 1) * 512], ones_f32r[0:1, :],
                    lrow[0:1, h2 * 512:(h2 + 1) * 512], start=True, stop=True),
                    waits=[tr, ps_last[3], t_on2], inc="tPE")
                te = P.op("vector", lambda v, j=j, h2=h2: v.tensor_copy(
                    gamT[:, j * 1024 + h2 * 512: j * 1024 + (h2 + 1) * 512],
                    ps[3][:, h2 * 512:(h2 + 1) * 512]), waits=[tm], inc="tDVE")
                ps_last[3] = te
                toks.append(te)
        return toks

    def ln_apply_loop(dst_slot, src_sp, src_ready, eps, gam_toks):
        toks = []
        for tc in range(TC):
            b = tc % 2
            tk = P.op(trq, lambda g, tc=tc, b=b, a=src_sp: g.dma_start(
                out=attb[b][:, 0:1024], in_=a.ap()[:, tc * 128:(tc + 1) * 128],
                transpose=True),
                waits=list(src_ready) + attb_free[b], inc=f"T{b}", inc_n=16)
            t1 = P.op("vector", lambda v, b=b: v.reduce_sum(
                stat[:, 0:1], attb[b][:, 0:1024], axis=AX.X),
                waits=[tk] + list(gam_toks), inc="tDVE")
            t2 = P.op("vector", lambda v: v.tensor_scalar(
                stat[:, 1:2], stat[:, 0:1], 1.0 / D, None, ALU.mult),
                waits=[t1], inc="tDVE")
            t3 = P.op("vector", lambda v, b=b: v.tensor_scalar(
                fio[2][:, :], attb[b][:, 0:1024], stat[:, 1:2], None,
                ALU.subtract), waits=[t2, fio_free[2]], inc="tDVE")
            t4 = P.op("vector", lambda v, b=b: v.scalar_tensor_tensor(
                attb[b][:, 0:1024], fio[2][:, :], 1.0, fio[2][:, :],
                ALU.mult, ALU.mult, accum_out=stat[:, 2:3]),
                waits=[t3], inc="tDVE")
            attb_free[b] = [t4]
            t5 = P.op("vector", lambda v, eps=eps: v.tensor_scalar(
                stat[:, 3:4], stat[:, 2:3], 1.0 / D, eps, ALU.mult, ALU.add),
                waits=[t4], inc="tDVE")
            t6 = P.op("scalar", lambda s: s.activation(
                stat[:, 4:5], stat[:, 3:4], AF.Ln), waits=[t5], inc="tACT")
            t7 = P.op("scalar", lambda s: s.activation(
                stat[:, 5:6], stat[:, 4:5], AF.Exp, scale=-0.5),
                waits=[t6], inc="tACT")
            t8 = P.op("vector", lambda v: v.tensor_scalar(
                fio[2][:, :], fio[2][:, :], stat[:, 5:6], None, ALU.mult),
                waits=[t7], inc="tDVE")
            t9 = P.op("vector", lambda v: v.tensor_tensor(
                fio[2][:, :], fio[2][:, :], gamT[:, 0:1024], ALU.mult),
                waits=[t8], inc="tDVE")
            t10 = P.op("vector", lambda v, ds=dst_slot, tc=tc: v.tensor_tensor(
                tchunk(ds, tc), fio[2][:, :], gamT[:, 1024:2048], ALU.add),
                waits=[t9], inc="tDVE")
            fio_free[2] = t10
            slot_written(dst_slot, tc % max(KC, TC), t10)
            toks.append(t10)
        return toks

    fio_free[2] = ps_last[2]
    gam_s = build_gam(0, 1, y_spills)
    s_toks = ln_apply_loop(Dv, y_sp, y_spills, 1e-5, gam_s)
    gam_g = build_gam(2, 3, s_toks)
    g_toks = ln_apply_loop(B, n5_sp, n5_spills, 1e-6, gam_g)
    gs_toks = []
    for tc in range(TC):
        gs_toks.append(P.op("scalar", lambda s, tc=tc: s.activation(
            tchunk(B, tc), tchunk(B, tc), AF.Silu), waits=[g_toks[tc]],
            inc="tACT"))

    out_toks = []
    for tc in range(TC):
        b = tc % 2
        tx = P.op(pln, lambda g, tc=tc, b=b: g.dma_start(
            out=fio[b][:, :], in_=x_in.ap()[tc * 128:(tc + 1) * 128, :]),
            waits=[fio_free[b]], inc=f"X{b}", inc_n=16)
        t1 = P.op("vector", lambda v, tc=tc: v.tensor_tensor(
            fio[2][:, :], tchunk(Dv, tc), tchunk(B, tc), ALU.mult),
            waits=[s_toks[tc], gs_toks[tc], fio_free[2]], inc="tDVE")
        t2 = P.op("vector", lambda v, b=b: v.tensor_tensor(
            fio[2][:, :], fio[2][:, :], fio[b][:, :], ALU.add),
            waits=[tx, t1], inc="tDVE")
        fio_free[b] = t2
        td = P.op(pln, lambda g, tc=tc: g.dma_start(
            out=out_t.ap()[tc * 128:(tc + 1) * 128, :], in_=fio[2][:, :]),
            waits=[t2], inc="OUT", inc_n=16)
        fio_free[2] = td
        out_toks.append(td)

    P.op("sync", lambda g: g.nop(), waits=[out_toks[-1]])

    with nc.Block() as block:
        P.finalize(nc, block, stack)
    stack.close()
    return nc


_NAME_MAP = {
    "aWq": "L0_attn_Wq", "aWk": "L0_attn_Wk", "aWv": "L0_attn_Wv",
    "aWo": "L0_attn_Wo", "abq": "L0_attn_bq", "abk": "L0_attn_bk",
    "abv": "L0_attn_bv", "abo": "L0_attn_bo", "agamma": "L0_attn_gamma",
    "abeta": "L0_attn_beta", "few": "L0_ffn_ew", "fact": "L0_ffn_act_w",
    "fWe": "L0_ffn_We", "fbe": "L0_ffn_be", "fWq": "L0_ffn_Wq",
    "fWk": "L0_ffn_Wk", "fWv": "L0_ffn_Wv", "fWo": "L0_ffn_Wo",
    "fln_g": "L0_ffn_ln_g", "fln_b": "L0_ffn_ln_b",
}


def shard_inputs(inputs, T=2048, nb=None):
    x = np.ascontiguousarray(np.asarray(inputs["L0_attn_x"], dtype=np.float32))
    B = x.shape[0] if nb is None else nb
    shared = {k: np.ascontiguousarray(np.asarray(inputs[v], dtype=np.float32))
              for k, v in _NAME_MAP.items()}
    in_maps = []
    for b in range(B):
        d = dict(shared)
        d["x"] = np.ascontiguousarray(x[b, :T])
        in_maps.append(d)
    return in_maps


def kernel(**inputs):
    T = 2048
    nc = build_nc(T)
    in_maps = shard_inputs(inputs, T)
    res = run_bass_kernel_spmd(nc, in_maps, core_ids=list(range(len(in_maps))))
    out = np.stack([np.asarray(r["out"], dtype=np.float32)
                    for r in res.results], axis=0)
    return out


# version 18
# speedup vs baseline: 1.0116x; 1.0078x over previous
import numpy as np
import ml_dtypes
from contextlib import ExitStack
import concourse.bass as bass
import concourse.mybir as mybir
from concourse.bass_utils import run_bass_kernel_spmd

F32 = mybir.dt.float32
BF16 = mybir.dt.bfloat16
AF = mybir.ActivationFunctionType
ALU = mybir.AluOpType
AX = mybir.AxisListType

D = 1024
KC = 8
NUM_NOD = 6

ENGINES = ("tensor", "scalar", "vector", "gpsimd", "sync")


class Prog:
    def __init__(self):
        self.q = {e: [] for e in ENGINES}
        self.sems = []
        self.cnt = {}
        self.hw = {}
        self.tag = "init"
        self.pe_tags = []

    def _reg(self, name):
        if name not in self.cnt:
            self.cnt[name] = 0
            self.sems.append(name)

    def op(self, eng, fn, waits=(), inc=None, inc_n=None):
        if eng == "tensor":
            self.pe_tags.append(self.tag)
        byname = {}
        for t in waits:
            if t is None:
                continue
            s, v = t
            if v <= 0:
                continue
            if byname.get(s, 0) < v:
                byname[s] = v
        for s, v in byname.items():
            key = (eng, s)
            if self.hw.get(key, 0) >= v:
                continue
            self.hw[key] = v
            self.q[eng].append(("w", s, v))
        tok = None
        if inc is not None:
            self._reg(inc)
            n = inc_n if inc_n is not None else 1
            self.cnt[inc] += n
            tok = (inc, self.cnt[inc])
            self.q[eng].append(("o", fn, inc, n))
        else:
            self.q[eng].append(("o", fn, None, 0))
        return tok

    def finalize(self, nc, block, stack):
        sems = {s: stack.enter_context(nc.semaphore(s)) for s in self.sems}
        for eng_name in ENGINES:
            items = list(self.q[eng_name])

            def make(items):
                def emit(eng):
                    for it in items:
                        if it[0] == "w":
                            eng.wait_ge(sems[it[1]], it[2])
                        else:
                            ins = it[1](eng)
                            if it[2] is not None:
                                ins.then_inc(sems[it[2]], it[3])
                return emit

            getattr(block, eng_name)(make(items))


def ffn_dag_meta():
    meta = []
    lind = 0
    for c in range(NUM_NOD):
        nsrc = min(c + 1, 5)
        snode = c - nsrc
        meta.append(dict(c=c, eq=lind + nsrc - 1, ek=lind, ev=lind + nsrc // 2,
                         in_q=c - 1, in_k=snode, in_v=snode + nsrc // 2))
        lind += nsrc
    return meta


def build_nc(T=2048, debug=False):
    TC = T // 128
    PW = T // 2
    NQ = T // PW
    SEG = min(512, T)
    NSEG_T = T // SEG
    PSEG = min(512, PW)
    NSEG_P = PW // PSEG
    NH = min(1024, T)
    NHA = T // NH

    nc = bass.Bass("TRN2", target_bir_lowering=False)

    x_in = nc.dram_tensor("x", [T, D], F32, kind="ExternalInput")
    out_t = nc.dram_tensor("out", [T, D], F32, kind="ExternalOutput")
    W = {}
    for nm, shp in [
        ("aWq", [D, D]), ("aWk", [D, D]), ("aWv", [D, D]), ("aWo", [D, D]),
        ("abq", [D]), ("abk", [D]), ("abv", [D]), ("abo", [D]),
        ("agamma", [D]), ("abeta", [D]), ("few", [6, 3]), ("fact", [6]),
        ("fWe", [20, D, D]), ("fbe", [20, D]), ("fWq", [6, D, D]),
        ("fWk", [6, D, D]), ("fWv", [6, D, D]), ("fWo", [6, D, D]),
        ("fln_g", [D]), ("fln_b", [D]),
    ]:
        W[nm] = nc.dram_tensor(nm, shp, F32, kind="ExternalInput")

    xn_sp = nc.dram_tensor("xn_sp", [T, D], BF16, kind="Internal")
    n_sp = {c: nc.dram_tensor(f"n{c}_sp", [D, T], BF16, kind="Internal")
            for c in (0, 1, 2)}
    n5_sp = nc.dram_tensor("n5_sp", [D, T], BF16, kind="Internal")
    y_sp = nc.dram_tensor("y_sp", [D, T], BF16, kind="Internal")
    dbg = {}
    if debug:
        for c in (0, 1):
            dbg[f"q{c}"] = nc.dram_tensor(f"dbg_q{c}", [128, T], BF16,
                                          kind="ExternalOutput")
            dbg[f"k{c}"] = nc.dram_tensor(f"dbg_k{c}", [128, T], BF16,
                                          kind="ExternalOutput")
            dbg[f"ctx{c}"] = nc.dram_tensor(f"dbg_ctx{c}", [128, T], BF16,
                                            kind="ExternalOutput")
            dbg[f"w{c}"] = nc.dram_tensor(f"dbg_w{c}", [1, 1024], F32,
                                          kind="ExternalOutput")
            dbg[f"vv{c}"] = nc.dram_tensor(f"dbg_vv{c}", [128, 1024], BF16,
                                           kind="ExternalOutput")

    P = Prog()
    stack = ExitStack()
    e = stack.enter_context

    slot = [e(nc.sbuf_tensor(f"slot{i}", [128, KC * T], BF16)) for i in range(4)]
    wbuf = [e(nc.sbuf_tensor(f"wbuf{i}", [128, KC * 1024], BF16)) for i in range(2)]
    wstage = [e(nc.sbuf_tensor(f"wstage{i}", [128, 1024], F32)) for i in range(2)]
    qbuf = e(nc.sbuf_tensor("qbuf", [128, T], BF16))
    kbuf = e(nc.sbuf_tensor("kbuf", [128, T], BF16))
    attb = [e(nc.sbuf_tensor(f"attb{i}", [128, max(PW, 1024)], BF16))
            for i in range(4)]
    sq_scr = attb[2]
    wb_sb = e(nc.sbuf_tensor("wb_sb", [128, PW], BF16))
    fio = [e(nc.sbuf_tensor(f"fio{i}", [128, 1024], F32)) for i in range(3)]
    ones_bf = e(nc.sbuf_tensor("ones_bf", [128, 1], BF16))
    ones_sq = e(nc.sbuf_tensor("ones_sq", [128, 128], BF16))
    ones_bfr = e(nc.sbuf_tensor("ones_bfr", [1, 128], BF16))
    ones_f32r = e(nc.sbuf_tensor("ones_f32r", [1, 128], F32))
    ident_f32 = e(nc.sbuf_tensor("ident_f32", [128, 128], F32))
    iscaled = e(nc.sbuf_tensor("iscaled", [128, 128], BF16))
    t128 = e(nc.sbuf_tensor("t128", [128, 128], BF16))
    beT = e(nc.sbuf_tensor("beT", [128, 160], F32))
    vecT = e(nc.sbuf_tensor("vecT", [128, 64], F32))
    vec_stage = e(nc.sbuf_tensor("vec_stage", [128, 128], F32))
    ewrow = e(nc.sbuf_tensor("ewrow", [1, 32], F32))
    scal = e(nc.sbuf_tensor("scal", [128, 32], F32))
    tiny = e(nc.sbuf_tensor("tiny", [1, 16], F32))
    lrow = fio[2]
    stat = e(nc.sbuf_tensor("stat", [128, 8], F32))
    gamT = e(nc.sbuf_tensor("gamT", [128, 2 * 1024], BF16))
    negM = e(nc.sbuf_tensor("negM", [128, 16], F32))

    bvrow = lrow.ap().bitcast(BF16)

    ps = [e(nc.psum_tensor(f"ps{i}", [128, 1024], F32)) for i in range(4)]
    ps_last = {0: None, 1: None, 2: None, 3: None}

    slot_w = [[None] * max(KC, TC) for _ in range(4)]
    slot_r = [[] for _ in range(4)]

    def slot_written(i, ch, tok):
        slot_w[i][ch] = tok
        slot_r[i] = []

    def slot_read(i, tok):
        slot_r[i].append(tok)

    def wtile(b, kc, c0, w):
        return wbuf[b][:, kc * 1024 + c0: kc * 1024 + c0 + w]

    def fchunk(s, c, t0=0, tw=None):
        tw = T if tw is None else tw
        return slot[s][:, c * T + t0: c * T + t0 + tw]

    def tchunk(s, tc, d0=0, dw=1024):
        return slot[s][:, tc * 1024 + d0: tc * 1024 + d0 + dw]

    dge, pln, trq = "gpsimd", "sync", "sync"

    def pser():
        P._reg("P")
        return [("P", P.cnt["P"])]

    t_ew = P.op(trq, lambda g: g.dma_start(
        out=ewrow[0:1, 0:18],
        in_=W["few"].ap().rearrange("a b -> (a b)")[None, :]),
        waits=pser(), inc="P", inc_n=16)
    t_aw = P.op(trq, lambda g: g.dma_start(
        out=ewrow[0:1, 18:24], in_=W["fact"].ap()[None, :]),
        waits=pser(), inc="P", inc_n=16)

    t_io1 = P.op("gpsimd", lambda g: g.iota(
        fio[0][:, 0:128], [[1, 128]], channel_multiplier=-1,
        allow_small_or_imprecise_dtypes=True), inc="tPOOL")
    t_idf = P.op("vector", lambda v: v.tensor_scalar(
        ident_f32[:, :], fio[0][:, 0:128], 0.0, None, ALU.is_equal),
        waits=[t_io1], inc="tDVE")
    t_t128 = P.op("vector", lambda v: v.tensor_scalar(
        t128[:, :], fio[0][:, 0:128], 0.0, None, ALU.is_ge),
        waits=[t_io1], inc="tDVE")
    t_on1 = P.op("gpsimd", lambda g: g.memset(ones_bf[:, :], 1.0), inc="tPOOL")
    t_on4 = P.op("gpsimd", lambda g: g.memset(ones_sq[:, :], 1.0), inc="tPOOL")
    t_on2 = P.op("gpsimd", lambda g: g.memset(ones_f32r[:, :], 1.0), inc="tPOOL")
    t_on3 = P.op("gpsimd", lambda g: g.memset(ones_bfr[:, :], 1.0), inc="tPOOL")

    t_be1 = P.op(trq, lambda g: g.dma_start(
        out=vec_stage[:, 0:128],
        in_=W["fbe"].ap().rearrange("a (c p) -> (a c) p", p=128)[0:128, :]),
        waits=pser(), inc="P", inc_n=16)
    tk = P.op("tensor", lambda t: t.transpose(
        ps[0][:, 0:128], vec_stage[:, 0:128], ident_f32[:, :]),
        waits=[t_be1, t_idf], inc="tPE")
    t_ev1 = P.op("vector", lambda v: v.tensor_copy(beT[:, 0:128], ps[0][:, 0:128]),
                 waits=[tk], inc="tDVE")
    ps_last[0] = t_ev1
    t_be2 = P.op(trq, lambda g: g.dma_start(
        out=vec_stage[0:32, 0:128],
        in_=W["fbe"].ap().rearrange("a (c p) -> (a c) p", p=128)[128:160, :]),
        waits=pser() + [tk], inc="P", inc_n=16)
    tk = P.op("tensor", lambda t: t.transpose(
        ps[1][:, 0:32], vec_stage[0:32, 0:128], ident_f32[0:32, 0:32]),
        waits=[t_be2], inc="tPE")
    t_ev2 = P.op("vector", lambda v: v.tensor_copy(beT[:, 128:160], ps[1][:, 0:32]),
                 waits=[tk], inc="tDVE")
    ps_last[1] = t_ev2

    vps = ["abq", "abk", "abv", "abo", "agamma", "abeta", "fln_g", "fln_b"]
    vtoks = []
    for i, nm in enumerate(vps):
        vtoks.append(P.op(trq, lambda g, nm=nm, i=i: g.dma_start(
            out=vec_stage[8 * i:8 * i + 8, 0:128],
            in_=W[nm].ap().rearrange("(c p) -> c p", p=128)),
            waits=pser() + ([t_ev2] if i == 0 else []), inc="P", inc_n=16))
    tk = P.op("tensor", lambda t: t.transpose(
        ps[0][:, 0:64], vec_stage[0:64, 0:128], ident_f32[0:64, 0:64]),
        waits=vtoks + [ps_last[0]], inc="tPE")
    t_vec = P.op("vector", lambda v: v.tensor_copy(vecT[:, 0:64], ps[0][:, 0:64]),
                 waits=[tk], inc="tDVE")
    ps_last[0] = t_vec

    def vcol(name, c):
        i = vps.index(name)
        return vecT[:, i * 8 + c: i * 8 + c + 1]

    tks = []
    for j in range(24):
        tks.append(P.op("tensor", lambda t, j=j: t.matmul(
            ps[1][:, j:j + 1], ones_f32r[0:1, :], ewrow[0:1, j:j + 1],
            start=True, stop=True),
            waits=[t_ew, t_aw, t_on2, ps_last[1]] if j == 0 else [], inc="tPE"))
    t_scal = P.op("vector", lambda v: v.tensor_copy(scal[:, 0:24], ps[1][:, 0:24]),
                  waits=[tks[-1]], inc="tDVE")
    ps_last[1] = t_scal

    xn_wtoks = []
    lrow_gate = []
    fio_free = [t_t128, None, None]
    attb_free = [[], [], [], []]
    for i in range(TC):
        b = i % 2
        tx = P.op(pln, lambda g, i=i, b=b: g.dma_start(
            out=fio[b][:, :], in_=x_in.ap()[i * 128:(i + 1) * 128, :]),
            waits=[fio_free[b]], inc=f"X{b}", inc_n=16)
        t1 = P.op("vector", lambda v, b=b: v.reduce_sum(
            stat[:, 0:1], fio[b][:, :], axis=AX.X), waits=[tx], inc="tDVE")
        t2 = P.op("vector", lambda v: v.tensor_scalar(
            stat[:, 1:2], stat[:, 0:1], 1.0 / D, None, ALU.mult),
            waits=[t1], inc="tDVE")
        t3 = P.op("vector", lambda v, b=b: v.tensor_scalar(
            fio[2][:, :], fio[b][:, :], stat[:, 1:2], None, ALU.subtract),
            waits=[t2, fio_free[2]], inc="tDVE")
        fio_free[b] = t3
        t4 = P.op("vector", lambda v, b=b: v.scalar_tensor_tensor(
            attb[b][:, 0:1024], fio[2][:, :], 1.0, fio[2][:, :],
            ALU.mult, ALU.mult, accum_out=stat[:, 2:3]),
            waits=[t3] + attb_free[b], inc="tDVE")
        t5 = P.op("vector", lambda v: v.tensor_scalar(
            stat[:, 3:4], stat[:, 2:3], 1.0 / D, 1e-5, ALU.mult, ALU.add),
            waits=[t4], inc="tDVE")
        t6 = P.op("scalar", lambda s: s.activation(
            stat[:, 4:5], stat[:, 3:4], AF.Ln), waits=[t5], inc="tACT")
        t7 = P.op("scalar", lambda s: s.activation(
            stat[:, 5:6], stat[:, 4:5], AF.Exp, scale=-0.5),
            waits=[t6], inc="tACT")
        t8 = P.op("vector", lambda v, b=b: v.tensor_scalar(
            attb[b][:, 0:1024], fio[2][:, :], stat[:, 5:6], None, ALU.mult),
            waits=[t7], inc="tDVE")
        fio_free[2] = t8
        td = P.op(pln, lambda g, i=i, b=b: g.dma_start(
            out=xn_sp.ap()[i * 128:(i + 1) * 128, :], in_=attb[b][:, 0:1024]),
            waits=[t8], inc=f"XN{b}", inc_n=16)
        attb_free[b] = [td]
        xn_wtoks.append(td)

    xn_done = xn_wtoks[-2:]
    lrow_gate.extend(xn_done)
    A, B, C, Dv = 0, 1, 2, 3
    for c in range(KC):
        tk = P.op(trq, lambda g, c=c: g.dma_start(
            out=fchunk(A, c), in_=xn_sp.ap()[:, c * 128:(c + 1) * 128],
            transpose=True), waits=xn_done, inc=f"F0c{c}", inc_n=16)
        slot_written(A, c, tk)

    wb_state = {"i": 0}
    wbuf_free = [[], []]
    wstage_free = [None, None]

    def load_W(dram_ap, slot_override=None):
        if slot_override is None:
            b = wb_state["i"] % 2
            wb_state["i"] += 1
        else:
            b = slot_override
        gate = list(wbuf_free[b])
        tcast = None
        for q in range(KC):
            sb_ = q % 2
            tld = P.op(pln, lambda g, q=q, sb_=sb_, a=dram_ap: g.dma_start(
                out=wstage[sb_][:, :],
                in_=a.rearrange("(kc p) n -> p kc n", p=128)[:, q:q + 1, :]),
                waits=[wstage_free[sb_]], inc=f"WS{sb_}", inc_n=16)
            tcast = P.op("gpsimd", lambda g, q=q, sb_=sb_, b=b: g.tensor_copy(
                wbuf[b][:, q * 1024:(q + 1) * 1024], wstage[sb_][:, :]),
                waits=[tld] + (gate if q < 2 else []), inc="tPOOL")
            wstage_free[sb_] = tcast
        wbuf_free[b] = []
        return b, tcast

    def projection(src_slot, dst_slot, wslot, wtok, bias_fn=None,
                   iadd_slot=None, iadd_tok=None, spill=None, spill_sem=None):
        spills = []
        last_ev = None
        dst_gate = list(slot_r[dst_slot])
        TW = min(1024, T)
        NTW = T // TW
        for do in range(KC):
            ev_th = []
            for th in range(NTW):
                pb = (do * NTW + th) % 4
                mmw = [wtok, ps_last[pb]] + dst_gate
                if iadd_slot is not None:
                    mmw += [slot_w[iadd_slot][do], iadd_tok]
                t_mm = None
                for kc in range(KC):
                    for sg in range(TW // SEG):
                        t_mm = P.op("tensor", lambda t, do=do, th=th, kc=kc,
                                    pb=pb, sg=sg, ws=wslot, ss=src_slot:
                                    t.matmul(
                            ps[pb][:, sg * SEG:(sg + 1) * SEG],
                            wtile(ws, kc, do * 128, 128),
                            fchunk(ss, kc, th * TW + sg * SEG, SEG),
                            start=(kc == 0),
                            stop=(kc == KC - 1 and iadd_slot is None)),
                            waits=(mmw if (kc == 0 and sg == 0) else []) +
                            ([slot_w[src_slot][kc]] if sg == 0 else []),
                            inc="tPE")
                if iadd_slot is not None:
                    for sg in range(TW // SEG):
                        t_mm = P.op("tensor", lambda t, do=do, th=th, pb=pb,
                                    sg=sg, s2=iadd_slot: t.matmul(
                            ps[pb][:, sg * SEG:(sg + 1) * SEG], iscaled[:, :],
                            fchunk(s2, do, th * TW + sg * SEG, SEG),
                            start=False, stop=True), inc="tPE")
                    slot_read(iadd_slot, t_mm)
                slot_read(src_slot, t_mm)
                eng = "scalar" if (do * NTW + th) % 2 == 0 else "vector"

                def ev_s(s, do=do, th=th, pb=pb, ds=dst_slot, TW=TW):
                    if bias_fn is not None:
                        return s.activation(
                            fchunk(ds, do, th * TW, TW), ps[pb][:, 0:TW],
                            AF.Identity, bias=bias_fn(do))
                    return s.activation(
                        fchunk(ds, do, th * TW, TW), ps[pb][:, 0:TW], AF.Copy)

                def ev_v(v, do=do, th=th, pb=pb, ds=dst_slot, TW=TW):
                    if bias_fn is not None:
                        return v.tensor_scalar(
                            fchunk(ds, do, th * TW, TW), ps[pb][:, 0:TW],
                            bias_fn(do), None, ALU.add)
                    return v.tensor_copy(
                        fchunk(ds, do, th * TW, TW), ps[pb][:, 0:TW])
                if eng == "scalar":
                    t_ev = P.op("scalar", ev_s, waits=[t_mm], inc="tACT")
                else:
                    t_ev = P.op("vector", ev_v, waits=[t_mm], inc="tDVE")
                ps_last[pb] = t_ev
                ev_th.append(t_ev)
                last_ev = t_ev
            slot_written(dst_slot, do, ev_th[-1])
            for t_ in ev_th[:-1]:
                slot_w[dst_slot][do] = ev_th[-1]
            if spill is not None:
                sp = P.op(pln, lambda g, do=do, a=spill, ds=dst_slot: g.dma_start(
                    out=a.ap().rearrange("(c p) t -> c p t", p=128)[do],
                    in_=fchunk(ds, do)), waits=ev_th, inc=spill_sem, inc_n=16)
                slot_read(dst_slot, sp)
                spills.append(sp)
        return last_ev, spills

    def projection_tmajor(src_slot, dst_slot, wslot, wtok, scale_col=None,
                          bias_row=None, bias_tok=None):
        last_ev = None
        dst_gate = list(slot_r[dst_slot])
        for tc in range(TC):
            pb = tc % 4
            mmw = [wtok, ps_last[pb]] + dst_gate
            t_mm = None
            for kc in range(KC):
                for half in range(2):
                    t_mm = P.op("tensor", lambda t, tc=tc, kc=kc, half=half,
                                pb=pb, ws=wslot, ss=src_slot: t.matmul(
                        ps[pb][:, half * 512:(half + 1) * 512],
                        fchunk(ss, kc, tc * 128, 128),
                        wtile(ws, kc, half * 512, 512),
                        start=(kc == 0),
                        stop=(kc == KC - 1 and bias_row is None)),
                        waits=(mmw if (kc == 0 and half == 0) else []) +
                        ([slot_w[src_slot][kc]] if half == 0 else []),
                        inc="tPE")
            if bias_row is not None:
                for half in range(2):
                    t_mm = P.op("tensor", lambda t, half=half, pb=pb: t.matmul(
                        ps[pb][:, half * 512:(half + 1) * 512], ones_bfr[0:1, :],
                        bias_row[0:1, half * 512:(half + 1) * 512],
                        start=False, stop=True),
                        waits=[t_on3, bias_tok], inc="tPE")
            slot_read(src_slot, t_mm)

            def ev(s, tc=tc, pb=pb, ds=dst_slot):
                return s.activation(
                    tchunk(ds, tc), ps[pb][:, 0:1024], AF.Copy,
                    scale=(scale_col if scale_col is not None else 1.0))
            t_ev = P.op("scalar", ev, waits=[t_mm], inc="tACT")
            ps_last[pb] = t_ev
            slot_written(dst_slot, tc % max(KC, TC), t_ev)
            last_ev = t_ev
        return last_ev

    def fill_slot(dst_slot, src_dram, fmajor_src, src_ready):
        gate = list(slot_r[dst_slot]) + list(src_ready)
        for c in range(KC):
            if fmajor_src:
                tk = P.op(pln, lambda g, c=c, a=src_dram, ds=dst_slot: g.dma_start(
                    out=fchunk(ds, c),
                    in_=a.ap().rearrange("(c p) t -> c p t", p=128)[c]),
                    waits=gate if c == 0 else [], inc=f"F{dst_slot}c{c}",
                    inc_n=16)
            else:
                tk = P.op(trq, lambda g, c=c, a=src_dram, ds=dst_slot: g.dma_start(
                    out=fchunk(ds, c), in_=a.ap()[:, c * 128:(c + 1) * 128],
                    transpose=True), waits=gate if c == 0 else [],
                    inc=f"F{dst_slot}c{c}", inc_n=16)
            slot_written(dst_slot, c, tk)

    def emit_negM(q_ap_fn, k_ap_fn, base, dh, abs_ap, abs_imm, ready,
                  ready_fn=None, out_col=0, sst_eng="vector"):
        prev = None
        for idx, apfn in enumerate((q_ap_fn, k_ap_fn)):
            for ha in range(NHA):
                rd = (list(ready) if ready is not None
                      else list(ready_fn(idx, ha)))
                t_sq = P.op(sst_eng, lambda v, apfn=apfn, ha=ha, base=base,
                            dh=dh: v.tensor_tensor(
                    sq_scr[base:base + dh, 0:NH], apfn(ha), apfn(ha),
                    ALU.mult),
                    waits=rd + [prev] + attb_free[2] + negm_m_last,
                    inc=("tPOOL" if sst_eng == "gpsimd" else "tDVE"))
                t_nm = None
                for sg in range(0, NH, 512):
                    sw = min(512, NH - sg)
                    t_nm = P.op("tensor", lambda t, sg=sg, sw=sw, base=base,
                                dh=dh: t.matmul(
                        ps[3][:, sg:sg + sw], ones_sq[base:base + dh, :],
                        sq_scr[base:base + dh, sg:sg + sw],
                        start=True, stop=True),
                        waits=[t_sq, ps_last[3], t_on4] if sg == 0 else [],
                        inc="tPE")
                dst = 4 + idx if ha == 0 else 7
                t_mx = P.op("vector", lambda v, dst=dst: v.reduce_max(
                    stat[:, dst:dst + 1], ps[3][:, 0:NH], axis=AX.X),
                    waits=[t_nm], inc="tDVE")
                if ha == 1:
                    t_mx = P.op("vector", lambda v, idx=idx: v.tensor_tensor(
                        stat[:, 4 + idx:5 + idx], stat[:, 4 + idx:5 + idx],
                        stat[:, 7:8], ALU.max), waits=[t_mx], inc="tDVE")
                ps_last[3] = t_mx
                attb_free[2] = [t_nm]
                prev = t_mx
        t_p = P.op("vector", lambda v: v.tensor_tensor(
            stat[:, 7:8], stat[:, 4:5], stat[:, 5:6], ALU.add),
            waits=[prev], inc="tDVE")
        gate = [ps_last[0], ps_last[1]] if out_col == 0 else []
        if abs_ap is not None:
            t_m = P.op("vector", lambda v, a=abs_ap, oc=out_col:
                       v.tensor_scalar(
                negM[:, oc:oc + 1], stat[:, 7:8], a, -1.0,
                ALU.mult, ALU.mult), waits=[t_p] + gate, inc="tDVE")
        else:
            t_m = P.op("vector", lambda v, oc=out_col: v.tensor_scalar(
                negM[:, oc:oc + 1], stat[:, 7:8], -abs_imm, None, ALU.mult),
                waits=[t_p] + gate, inc="tDVE")
        negm_m_last[:] = [t_m]

        def emitB():
            return t_m
        return emitB

    dbg_gate = []
    qbuf_gate = []
    wrow_gate = []

    def attention_qtile(kf_tile_fn, q_rhs_fn, v_tile_fn, dh, obase, kc_list,
                        diag_off, scale_col, negM_fn, wfold_11, ctx_dst_ap,
                        extra_ready, pending_fin=None, negM_col=0):
        n_kc = len(kc_list)
        t_avl = None
        t_lmm = None
        tp = (0, obase) if obase else None
        ready_q = []
        scores_tok = {}

        def emit_scores(j):
            kc = kc_list[j]
            pb = j % 2
            mmw = [ps_last[pb]] + list(extra_ready)
            t_s = None
            for sg in range(NSEG_P):
                t_s = P.op("tensor", lambda t, kc=kc, sg=sg, pb=pb: t.matmul(
                    ps[pb][:, sg * PSEG:(sg + 1) * PSEG],
                    kf_tile_fn(kc), q_rhs_fn(sg * PSEG, PSEG),
                    start=True, stop=True),
                    waits=mmw if sg == 0 else [], inc="tPE")
            scores_tok[j] = t_s

        def seg_ranges(off):
            out = []
            x = off
            while x < PW:
                nxt = min(PW, (x // PSEG + 1) * PSEG)
                out.append((x, nxt - x))
                x = nxt
            return out

        def emit_lav(j, ab, t_ready, off=0, l_first=False, l_last=False,
                     av_first=False, av_stop=False):
            nonlocal t_lmm, t_avl
            kc = kc_list[j]
            for (o, w) in seg_ranges(off):
                t_lmm = P.op("tensor", lambda t, o=o, w=w, ab=ab,
                             l_first=l_first, l_last=l_last: t.matmul(
                    ps[3][0:1, o:o + w], ones_bf[:, 0:1],
                    attb[ab][:, o:o + w],
                    start=l_first, stop=l_last),
                    waits=[t_ready, t_on1] + ([ps_last[3]] if l_first else []),
                    inc="tPE")
            for (o, w) in seg_ranges(off):
                t_avl = P.op("tensor", lambda t, kc=kc, o=o, w=w, ab=ab, dh=dh,
                             obase=obase, tp=tp, av_first=av_first,
                             av_stop=av_stop: t.matmul(
                    ps[2][obase:obase + dh, o:o + w],
                    v_tile_fn(kc), attb[ab][:, o:o + w],
                    start=av_first, stop=av_stop, tile_position=tp),
                    waits=([ps_last[2]] if av_first else []), inc="tPE")
            attb_free[ab] = [t_avl]

        def pair_a(e0, e1):
            nonlocal t_avl, t_lmm
            j0, a0, r0, _ = e0
            j1, a1, r1, _ = e1
            t_av0 = None
            for (o, w) in seg_ranges(0):
                t_av0 = P.op("tensor", lambda t, kc0=kc_list[j0], o=o, w=w,
                             a0=a0, dh=dh, obase=obase, tp=tp,
                             first=(j0 == 0): t.matmul(
                    ps[2][obase:obase + dh, o:o + w],
                    v_tile_fn(kc0), attb[a0][:, o:o + w],
                    start=first, stop=False, tile_position=tp),
                    waits=[r0] + ([ps_last[2]] if j0 == 0 else []), inc="tPE")
            t_pr = P.op("vector", lambda v, a0=a0, a1=a1: v.tensor_tensor(
                attb[a0][:, 0:PW], attb[a0][:, 0:PW], attb[a1][:, 0:PW],
                ALU.add), waits=[r0, r1, t_av0], inc="tDVE")

            def pair_b():
                nonlocal t_avl, t_lmm
                t_l = None
                for (o, w) in seg_ranges(0):
                    t_l = P.op("tensor", lambda t, o=o, w=w, a0=a0,
                               first=(j0 == 0), last=(j1 == n_kc - 1):
                               t.matmul(
                        ps[3][0:1, o:o + w], ones_bf[:, 0:1],
                        attb[a0][:, o:o + w], start=first, stop=last),
                        waits=[t_pr, t_on1] +
                        ([ps_last[3]] if j0 == 0 else []), inc="tPE")
                t_av2 = None
                for (o, w) in seg_ranges(0):
                    t_av2 = P.op("tensor", lambda t, kc1=kc_list[j1], o=o,
                                 w=w, a1=a1, dh=dh, obase=obase, tp=tp,
                                 last=(j1 == n_kc - 1): t.matmul(
                        ps[2][obase:obase + dh, o:o + w],
                        v_tile_fn(kc1), attb[a1][:, o:o + w],
                        start=False, stop=last, tile_position=tp),
                        waits=[r1], inc="tPE")
                t_lmm = t_l
                attb_free[a0] = [t_l]
                attb_free[a1] = [t_pr, t_av2]
                t_avl = t_av2
            return pair_b

        P.tag = P.tag + "/att"
        pair_mode = (diag_off is None and n_kc % 2 == 0 and n_kc >= 4)
        stash = [None]

        for j in range(min(2, n_kc)):
            emit_scores(j)
        if pending_fin is not None:
            pending_fin()
        t_negM = negM_fn()
        for j, kc in enumerate(kc_list):
            if j >= 2:
                emit_scores(j)
            if stash[0] is not None:
                stash[0]()
                stash[0] = None
            pb = j % 2
            ab = j % (4 if pair_mode else 3)
            off = diag_off.get(kc, 0) if diag_off is not None else 0
            t_x = P.op("scalar", lambda s, pb=pb, ab=ab, nc_=negM_col,
                       off=off: s.activation(
                attb[ab][:, off:PW], ps[pb][:, off:PW], AF.Exp,
                bias=negM[:, nc_:nc_ + 1], scale=scale_col),
                waits=[scores_tok[j], t_negM] + attb_free[ab], inc="tACT")
            ps_last[pb] = t_x
            t_ready = t_x
            if diag_off is not None and kc in diag_off:
                t_ready = P.op("vector", lambda v, ab=ab, off=off: v.tensor_tensor(
                    attb[ab][:, off:off + 128], attb[ab][:, off:off + 128],
                    t128[:, :], ALU.mult), waits=[t_x, t_t128], inc="tDVE")
            ready_q.append((j, ab, t_ready, off))
            if pair_mode and len(ready_q) >= 4:
                stash[0] = pair_a(ready_q.pop(0), ready_q.pop(0))
            elif not pair_mode and len(ready_q) >= 3:
                jj, aa, rr, oo = ready_q.pop(0)
                emit_lav(jj, aa, rr, off=oo, l_first=(jj == 0),
                         l_last=(jj == n_kc - 1), av_first=(jj == 0),
                         av_stop=(jj == n_kc - 1))
        if stash[0] is not None:
            stash[0]()
            stash[0] = None
        while ready_q:
            if pair_mode and len(ready_q) >= 2:
                pair_a(ready_q.pop(0), ready_q.pop(0))()
            else:
                jj, aa, rr, oo = ready_q.pop(0)
                emit_lav(jj, aa, rr, off=oo, l_first=(jj == 0),
                         l_last=(jj == n_kc - 1), av_first=(jj == 0),
                         av_stop=(jj == n_kc - 1))
        t_lmm_f = t_lmm
        t_avl_f = t_avl
        t_le = P.op("scalar", lambda s: s.activation(
            lrow[0:1, 0:PW], ps[3][0:1, 0:PW], AF.Ln),
            waits=[t_lmm_f] + dbg_gate + lrow_gate + wrow_gate, inc="tACT")
        ps_last[3] = t_le
        t_rc = P.op("scalar", lambda s, a=wfold_11: s.activation(
            wb_sb[0:1, 0:PW], lrow[0:1, 0:PW], AF.Exp, scale=-1.0,
            bias=(a if a is not None else 0.0)),
            waits=[t_le, ps_last[2]] + wrow_gate, inc="tACT")

        def fin(cbs=()):
            t_wb = None
            for sg in range(NSEG_P):
                t_wb = P.op("tensor", lambda t, sg=sg, dh=dh,
                            obase=obase, tp=tp: t.matmul(
                    ps[3][obase:obase + dh, sg * PSEG:(sg + 1) * PSEG],
                    ones_bfr[0:1, 0:dh],
                    wb_sb[0:1, sg * PSEG:(sg + 1) * PSEG],
                    start=True, stop=True, tile_position=tp),
                    waits=[t_rc, ps_last[3], t_on3] if sg == 0 else [],
                    inc="tPE")
            wrow_gate[:] = [t_wb]
            t_we = P.op("vector", lambda v, dh=dh, obase=obase:
                        v.tensor_copy(wb_sb[obase:obase + dh, 0:PW],
                                      ps[3][obase:obase + dh, 0:PW]),
                        waits=[t_wb], inc="tDVE")
            ps_last[3] = t_we
            t_fin = P.op("vector", lambda v, dh=dh, obase=obase,
                         dst=ctx_dst_ap: v.tensor_tensor(
                dst, ps[2][obase:obase + dh, 0:PW],
                wb_sb[obase:obase + dh, 0:PW], ALU.mult),
                waits=[t_we, t_avl_f], inc="tDVE")
            ps_last[2] = t_fin
            for cb in cbs:
                cb(t_fin)
            return t_fin
        return fin

    meta = ffn_dag_meta()
    iscaled_free = []
    lnaw_gate = [None]
    negm_m_last = [None]
    pending = [None]

    def defer_fin(fin, cbs):
        def run():
            return fin(cbs=cbs)
        pending[0] = run

    def take_fin():
        p = pending[0]
        pending[0] = None
        return p

    def drain_fin():
        p = take_fin()
        if p is not None:
            p()
    weq_pre = load_W(W["fWe"].ap()[meta[0]["eq"]], slot_override=0)
    last_spills = {}

    for nd in meta:
        c = nd["c"]
        t_sc1 = P.op("vector", lambda v, c=c: v.tensor_tensor(
            scal[:, 24:25], scal[:, 3 * c:3 * c + 1], scal[:, 3 * c + 1:3 * c + 2],
            ALU.mult), waits=[t_scal], inc="tDVE")
        t_sc2 = P.op("vector", lambda v: v.tensor_scalar(
            scal[:, 24:25], scal[:, 24:25], 1.0 / np.sqrt(128.0), None,
            ALU.mult), waits=[t_sc1], inc="tDVE")
        t_sc3 = P.op("vector", lambda v: v.tensor_tensor(
            scal[:, 25:26], scal[:, 24:25], scal[:, 24:25], ALU.mult),
            waits=[t_sc2], inc="tDVE")
        t_isc = P.op("vector", lambda v, c=c: v.tensor_scalar(
            iscaled[:, :], ident_f32[:, :], scal[:, 3 * c:3 * c + 1], None,
            ALU.mult), waits=[t_scal, t_idf] + iscaled_free, inc="tDVE")
        iscaled_free = []
        t_lnaw = P.op("scalar", lambda s, c=c: s.activation(
            tiny[0:1, 12:13], ewrow[0:1, 18 + c:19 + c], AF.Ln),
            waits=[t_ew, t_aw] + lnaw_gate, inc="tACT")
        lnaw_gate = [t_lnaw]
        t_sf1 = P.op("scalar", lambda s: s.activation(
            scal[:, 27:28], scal[:, 25:26], AF.Ln),
            waits=[t_sc3] + negm_m_last, inc="tACT")
        t_sf2 = P.op("scalar", lambda s: s.activation(
            scal[:, 27:28], scal[:, 27:28], AF.Exp, scale=0.5),
            waits=[t_sf1], inc="tACT")
        t_sf3 = P.op("vector", lambda v: v.tensor_scalar(
            scal[:, 26:27], scal[:, 27:28], 0.51, None, ALU.mult),
            waits=[t_sf2], inc="tDVE")
        negm_m_last = [t_sf3]

        P.tag = f"n{c}:pq"
        X1, X2 = c % 2, 1 - c % 2
        wq_b, wq_t = weq_pre
        t_pq, _ = projection(A, B, wq_b, wq_t,
                             bias_fn=lambda do, eq=nd["eq"]:
                             beT[:, eq * 8 + do:eq * 8 + do + 1])
        wbuf_free[wq_b].append(t_pq)

        if c == 0:
            pk_slot = B
        else:
            if nd["in_k"] == -1:
                fill_slot(Dv, xn_sp, False, xn_done)
            else:
                fill_slot(Dv, n_sp[nd["in_k"]], True,
                          last_spills.get(nd["in_k"], []))
            P.tag = f"n{c}:pk"
            wk_b, wk_t = load_W(W["fWe"].ap()[nd["ek"]], slot_override=X2)
            t_pk, _ = projection(Dv, C, wk_b, wk_t,
                                 bias_fn=lambda do, ek=nd["ek"]:
                                 beT[:, ek * 8 + do:ek * 8 + do + 1])
            wbuf_free[wk_b].append(t_pk)
            pk_slot = C

        if c <= 1:
            pv_slot = B
        else:
            fill_slot(A, n_sp[nd["in_v"]], True,
                      last_spills.get(nd["in_v"], []))
            P.tag = f"n{c}:pv"
            wv_b, wv_t = load_W(W["fWe"].ap()[nd["ev"]], slot_override=X1)
            t_pv, _ = projection(A, Dv, wv_b, wv_t,
                                 bias_fn=lambda do, ev=nd["ev"]:
                                 beT[:, ev * 8 + do:ev * 8 + do + 1])
            wbuf_free[wv_b].append(t_pv)
            pv_slot = Dv

        P.tag = f"n{c}:vv"
        wvv_b, wvv_t = load_W(W["fWv"].ap()[c], slot_override=X2)
        if c <= 1:
            gate = list(slot_r[A])
            if gate:
                P.op("tensor", lambda t: t.nop(), waits=gate)
        t_vv = projection_tmajor(pv_slot, A, wvv_b, wvv_t,
                                 scale_col=scal[:, 3 * c + 2:3 * c + 3])
        wbuf_free[wvv_b].append(t_vv)

        wq_b2, wq_t2 = load_W(W["fWq"].ap()[c], slot_override=X1)
        wk_b2, wk_t2 = load_W(W["fWk"].ap()[c], slot_override=X2)
        wo_t = None
        wo_b = None

        for h in range(8):
            P.tag = f"n{c}:h{h}:qk"
            qk_done = []
            TW2 = min(1024, T)
            NTW2 = T // TW2
            for which, (bufap, wsl, wtk, src_sl) in enumerate(
                    [(qbuf, wq_b2, wq_t2, B), (kbuf, wk_b2, wk_t2, pk_slot)]):
                if which == 1:
                    drain_fin()
                for th in range(NTW2):
                    pb = (which * NTW2 + th) % 2
                    mmw = [wtk, ps_last[pb]]
                    t_mm = None
                    for kc in range(KC):
                        for sg in range(TW2 // SEG):
                            t_mm = P.op("tensor", lambda t, kc=kc, th=th,
                                        pb=pb, sg=sg, wsl=wsl, h=h,
                                        src_sl=src_sl: t.matmul(
                                ps[pb][:, sg * SEG:(sg + 1) * SEG],
                                wtile(wsl, kc, h * 128, 128),
                                fchunk(src_sl, kc, th * TW2 + sg * SEG, SEG),
                                start=(kc == 0), stop=(kc == KC - 1)),
                                waits=(mmw if (kc == 0 and sg == 0) else []) +
                                ([slot_w[src_sl][kc]] if sg == 0 else []),
                                inc="tPE")
                    slot_read(src_sl, t_mm)
                    t_ev = P.op("vector", lambda v, th=th, pb=pb,
                                bufap=bufap, TW2=TW2: v.tensor_copy(
                        bufap[:, th * TW2:(th + 1) * TW2],
                        ps[pb][:, 0:TW2]),
                        waits=[t_mm] + qbuf_gate, inc="tDVE")
                    ps_last[pb] = t_ev
                    qk_done.append(t_ev)
            if debug and c in (0, 1) and h == 0:
                tdq = P.op(pln, lambda g, c=c: g.dma_start(
                    out=dbg[f"q{c}"].ap(), in_=qbuf[:, :]),
                    waits=qk_done, inc="DBG", inc_n=16)
                tdk = P.op(pln, lambda g, c=c: g.dma_start(
                    out=dbg[f"k{c}"].ap(), in_=kbuf[:, :]),
                    waits=qk_done, inc="DBG", inc_n=16)
                qbuf_gate[:] = [tdq, tdk]
            if h == 7:
                wbuf_free[wq_b2].append(qk_done[NSEG_T - 1])
                wbuf_free[wk_b2].append(qk_done[-1])
                wo_b, wo_t = load_W(W["fWo"].ap()[c], slot_override=X1)
                if c < 5:
                    nxt = meta[c + 1]
                    weq_pre = load_W(W["fWe"].ap()[nxt["eq"]],
                                     slot_override=X2)
                else:
                    weq_pre = load_W(W["aWq"].ap(), slot_override=X2)
            drain_fin()
            negM_B = emit_negM(
                lambda ha: qbuf[:, ha * NH:(ha + 1) * NH],
                lambda ha: kbuf[:, ha * NH:(ha + 1) * NH],
                0, 128, scal[:, 26:27], None, None,
                ready_fn=lambda idx, ha, NTW2=NTW2: [qk_done[min(
                    idx * NTW2 + ha, len(qk_done) - 1)]])
            negM_tok = [None]

            def negM_fn(B=negM_B, tok=negM_tok):
                if tok[0] is None:
                    tok[0] = B()
                return tok[0]
            for qt in range(NQ):
                fin = attention_qtile(
                    kf_tile_fn=lambda kc: kbuf[:, kc * 128:(kc + 1) * 128],
                    q_rhs_fn=lambda o, w, qt=qt: qbuf[:, qt * PW + o:
                                                      qt * PW + o + w],
                    v_tile_fn=lambda kc, h=h, A=A: slot[A][
                        :, kc * 1024 + h * 128: kc * 1024 + h * 128 + 128],
                    dh=128, obase=0, kc_list=list(range(TC)), diag_off=None,
                    scale_col=scal[:, 24:25], negM_fn=negM_fn,
                    wfold_11=tiny[0:1, 12:13],
                    ctx_dst_ap=fchunk(Dv, h, qt * PW, PW),
                    extra_ready=qk_done + [t_vv], pending_fin=take_fin())
                defer_fin(fin, [lambda t, A=A: slot_read(A, t),
                                lambda t, h=h: slot_written(Dv, h, t)])
        drain_fin()
        P.tag = f"n{c}:oproj"
        spill_t = n_sp.get(c)
        t_o, sp_toks = projection(Dv, B, wo_b, wo_t, bias_fn=None,
                                  iadd_slot=B, iadd_tok=t_isc, spill=spill_t,
                                  spill_sem=f"SPn{c}")
        if spill_t is not None:
            last_spills[c] = sp_toks
        iscaled_free = [t_o]
        wbuf_free[wo_b].append(t_o)
        A, B = B, A

    n5_spills = []
    for ch in range(KC):
        sp = P.op(pln, lambda g, ch=ch, A=A: g.dma_start(
            out=n5_sp.ap().rearrange("(c p) t -> c p t", p=128)[ch],
            in_=fchunk(A, ch)), waits=[slot_w[A][ch]], inc="SPn5", inc_n=16)
        slot_read(A, sp)
        n5_spills.append(sp)

    P.tag = "L0:qkv"
    fill_slot(B, xn_sp, False, xn_done)
    wq_b, wq_t = weq_pre
    t_q, _ = projection(B, C, wq_b, wq_t, bias_fn=lambda do: vcol("abq", do))
    wbuf_free[wq_b].append(t_q)
    wk_b, wk_t = load_W(W["aWk"].ap(), slot_override=1 - wq_b)
    t_k, _ = projection(B, Dv, wk_b, wk_t, bias_fn=lambda do: vcol("abk", do))
    wbuf_free[wk_b].append(t_k)
    t_bv = P.op(dge, lambda g: g.dma_start(
        out=bvrow[0:1, 0:1024], in_=W["abv"].ap()[None, :]),
        waits=[ps_last[3]], inc="BV", inc_n=16)
    wv_b, wv_t = load_W(W["aWv"].ap(), slot_override=wq_b)
    gate = list(slot_r[A])
    if gate:
        P.op("tensor", lambda t: t.nop(), waits=gate)
    t_v = projection_tmajor(B, A, wv_b, wv_t, bias_row=bvrow, bias_tok=t_bv)
    wbuf_free[wv_b].append(t_v)
    wo_b, wo_t = load_W(W["aWo"].ap(), slot_override=1 - wv_b)

    NDIAG = PW // 128
    l0_negM = {}

    def l0_chain(hh):
        ch_, pr_ = hh // 2, (hh % 2) * 64
        eB = emit_negM(
            lambda ha, ch=ch_, pr=pr_: slot[C][
                pr:pr + 64, ch * T + ha * NH: ch * T + (ha + 1) * NH],
            lambda ha, ch=ch_, pr=pr_: slot[Dv][
                pr:pr + 64, ch * T + ha * NH: ch * T + (ha + 1) * NH],
            pr_, 64, None, 0.06375, [t_q, t_k], out_col=hh)
        l0_negM[hh] = eB()

    l0_chain(0)
    l0_chain(1)
    for h in range(16):
        ch, pr = h // 2, (h % 2) * 64
        P.tag = f"L0:h{h}"
        drain_fin()
        if h + 2 < 16:
            l0_chain(h + 2)

        def negM_fn(h=h):
            return l0_negM[h]
        for qt in range(NQ):
            kc_hi = (qt + 1) * NDIAG
            doff = {kc: kc * 128 - qt * PW for kc in range(qt * NDIAG, kc_hi)}
            fin = attention_qtile(
                kf_tile_fn=lambda kc, ch=ch, pr=pr: slot[Dv][
                    pr:pr + 64, ch * T + kc * 128: ch * T + (kc + 1) * 128],
                q_rhs_fn=lambda o, w, ch=ch, pr=pr, qt=qt: slot[C][
                    pr:pr + 64, ch * T + qt * PW + o: ch * T + qt * PW + o + w],
                v_tile_fn=lambda kc, h=h, A=A: slot[A][
                    :, kc * 1024 + h * 64: kc * 1024 + h * 64 + 64],
                dh=64, obase=pr, kc_list=list(range(kc_hi)), diag_off=doff,
                scale_col=0.125, negM_fn=negM_fn, wfold_11=None,
                ctx_dst_ap=slot[B][pr:pr + 64,
                                   ch * T + qt * PW: ch * T + qt * PW + PW],
                extra_ready=[t_v], pending_fin=take_fin(), negM_col=h)
            defer_fin(fin, [lambda t, A=A: slot_read(A, t),
                            lambda t: slot_read(C, t),
                            lambda t: slot_read(Dv, t),
                            lambda t, ch=ch: slot_written(B, ch, t)])
    drain_fin()
    P.tag = "L0:yproj"
    t_y, y_spills = projection(B, C, wo_b, wo_t,
                               bias_fn=lambda do: vcol("abo", do), spill=y_sp,
                               spill_sem="SPy")
    wbuf_free[wo_b].append(t_y)

    def build_gam(i0, i1, ready):
        toks = []
        for j, i in enumerate((i0, i1)):
            tr = P.op(trq, lambda g, i=i: g.dma_start(
                out=lrow[0:1, 0:1024], in_=W[vps[4 + i]].ap()[None, :]),
                waits=pser() + list(ready) + toks, inc="P", inc_n=16)
            for h2 in range(2):
                tm = P.op("tensor", lambda t, j=j, h2=h2: t.matmul(
                    ps[3][:, h2 * 512:(h2 + 1) * 512], ones_f32r[0:1, :],
                    lrow[0:1, h2 * 512:(h2 + 1) * 512], start=True, stop=True),
                    waits=[tr, ps_last[3], t_on2], inc="tPE")
                te = P.op("vector", lambda v, j=j, h2=h2: v.tensor_copy(
                    gamT[:, j * 1024 + h2 * 512: j * 1024 + (h2 + 1) * 512],
                    ps[3][:, h2 * 512:(h2 + 1) * 512]), waits=[tm], inc="tDVE")
                ps_last[3] = te
                toks.append(te)
        return toks

    def ln_apply_loop(dst_slot, src_sp, src_ready, eps, gam_toks):
        toks = []
        for tc in range(TC):
            b = tc % 2
            tk = P.op(trq, lambda g, tc=tc, b=b, a=src_sp: g.dma_start(
                out=attb[b][:, 0:1024], in_=a.ap()[:, tc * 128:(tc + 1) * 128],
                transpose=True),
                waits=list(src_ready) + attb_free[b], inc=f"T{b}", inc_n=16)
            t1 = P.op("vector", lambda v, b=b: v.reduce_sum(
                stat[:, 0:1], attb[b][:, 0:1024], axis=AX.X),
                waits=[tk] + list(gam_toks), inc="tDVE")
            t2 = P.op("vector", lambda v: v.tensor_scalar(
                stat[:, 1:2], stat[:, 0:1], 1.0 / D, None, ALU.mult),
                waits=[t1], inc="tDVE")
            t3 = P.op("vector", lambda v, b=b: v.tensor_scalar(
                fio[2][:, :], attb[b][:, 0:1024], stat[:, 1:2], None,
                ALU.subtract), waits=[t2, fio_free[2]], inc="tDVE")
            t4 = P.op("vector", lambda v, b=b: v.scalar_tensor_tensor(
                attb[b][:, 0:1024], fio[2][:, :], 1.0, fio[2][:, :],
                ALU.mult, ALU.mult, accum_out=stat[:, 2:3]),
                waits=[t3], inc="tDVE")
            attb_free[b] = [t4]
            t5 = P.op("vector", lambda v, eps=eps: v.tensor_scalar(
                stat[:, 3:4], stat[:, 2:3], 1.0 / D, eps, ALU.mult, ALU.add),
                waits=[t4], inc="tDVE")
            t6 = P.op("scalar", lambda s: s.activation(
                stat[:, 4:5], stat[:, 3:4], AF.Ln), waits=[t5], inc="tACT")
            t7 = P.op("scalar", lambda s: s.activation(
                stat[:, 5:6], stat[:, 4:5], AF.Exp, scale=-0.5),
                waits=[t6], inc="tACT")
            t8 = P.op("vector", lambda v: v.tensor_scalar(
                fio[2][:, :], fio[2][:, :], stat[:, 5:6], None, ALU.mult),
                waits=[t7], inc="tDVE")
            t9 = P.op("vector", lambda v: v.tensor_tensor(
                fio[2][:, :], fio[2][:, :], gamT[:, 0:1024], ALU.mult),
                waits=[t8], inc="tDVE")
            t10 = P.op("vector", lambda v, ds=dst_slot, tc=tc: v.tensor_tensor(
                tchunk(ds, tc), fio[2][:, :], gamT[:, 1024:2048], ALU.add),
                waits=[t9], inc="tDVE")
            fio_free[2] = t10
            slot_written(dst_slot, tc % max(KC, TC), t10)
            toks.append(t10)
        return toks

    fio_free[2] = ps_last[2]
    gam_s = build_gam(0, 1, y_spills)
    s_toks = ln_apply_loop(Dv, y_sp, y_spills, 1e-5, gam_s)
    gam_g = build_gam(2, 3, s_toks)
    g_toks = ln_apply_loop(B, n5_sp, n5_spills, 1e-6, gam_g)
    gs_toks = []
    for tc in range(TC):
        gs_toks.append(P.op("scalar", lambda s, tc=tc: s.activation(
            tchunk(B, tc), tchunk(B, tc), AF.Silu), waits=[g_toks[tc]],
            inc="tACT"))

    out_toks = []
    for tc in range(TC):
        b = tc % 2
        tx = P.op(pln, lambda g, tc=tc, b=b: g.dma_start(
            out=fio[b][:, :], in_=x_in.ap()[tc * 128:(tc + 1) * 128, :]),
            waits=[fio_free[b]], inc=f"X{b}", inc_n=16)
        t1 = P.op("vector", lambda v, tc=tc: v.tensor_tensor(
            fio[2][:, :], tchunk(Dv, tc), tchunk(B, tc), ALU.mult),
            waits=[s_toks[tc], gs_toks[tc], fio_free[2]], inc="tDVE")
        t2 = P.op("vector", lambda v, b=b: v.tensor_tensor(
            fio[2][:, :], fio[2][:, :], fio[b][:, :], ALU.add),
            waits=[tx, t1], inc="tDVE")
        fio_free[b] = t2
        td = P.op(pln, lambda g, tc=tc: g.dma_start(
            out=out_t.ap()[tc * 128:(tc + 1) * 128, :], in_=fio[2][:, :]),
            waits=[t2], inc="OUT", inc_n=16)
        fio_free[2] = td
        out_toks.append(td)

    P.op("sync", lambda g: g.nop(), waits=[out_toks[-1]])

    with nc.Block() as block:
        P.finalize(nc, block, stack)
    stack.close()
    return nc


_NAME_MAP = {
    "aWq": "L0_attn_Wq", "aWk": "L0_attn_Wk", "aWv": "L0_attn_Wv",
    "aWo": "L0_attn_Wo", "abq": "L0_attn_bq", "abk": "L0_attn_bk",
    "abv": "L0_attn_bv", "abo": "L0_attn_bo", "agamma": "L0_attn_gamma",
    "abeta": "L0_attn_beta", "few": "L0_ffn_ew", "fact": "L0_ffn_act_w",
    "fWe": "L0_ffn_We", "fbe": "L0_ffn_be", "fWq": "L0_ffn_Wq",
    "fWk": "L0_ffn_Wk", "fWv": "L0_ffn_Wv", "fWo": "L0_ffn_Wo",
    "fln_g": "L0_ffn_ln_g", "fln_b": "L0_ffn_ln_b",
}


def shard_inputs(inputs, T=2048, nb=None):
    x = np.ascontiguousarray(np.asarray(inputs["L0_attn_x"], dtype=np.float32))
    B = x.shape[0] if nb is None else nb
    shared = {k: np.ascontiguousarray(np.asarray(inputs[v], dtype=np.float32))
              for k, v in _NAME_MAP.items()}
    in_maps = []
    for b in range(B):
        d = dict(shared)
        d["x"] = np.ascontiguousarray(x[b, :T])
        in_maps.append(d)
    return in_maps


def kernel(**inputs):
    T = 2048
    nc = build_nc(T)
    in_maps = shard_inputs(inputs, T)
    res = run_bass_kernel_spmd(nc, in_maps, core_ids=list(range(len(in_maps))))
    out = np.stack([np.asarray(r["out"], dtype=np.float32)
                    for r in res.results], axis=0)
    return out


# version 19
# speedup vs baseline: 1.0131x; 1.0015x over previous
import numpy as np
import ml_dtypes
from contextlib import ExitStack
import concourse.bass as bass
import concourse.mybir as mybir
from concourse.bass_utils import run_bass_kernel_spmd

F32 = mybir.dt.float32
BF16 = mybir.dt.bfloat16
AF = mybir.ActivationFunctionType
ALU = mybir.AluOpType
AX = mybir.AxisListType

D = 1024
KC = 8
NUM_NOD = 6

ENGINES = ("tensor", "scalar", "vector", "gpsimd", "sync")


class Prog:
    def __init__(self):
        self.q = {e: [] for e in ENGINES}
        self.sems = []
        self.cnt = {}
        self.hw = {}
        self.tag = "init"
        self.pe_tags = []

    def _reg(self, name):
        if name not in self.cnt:
            self.cnt[name] = 0
            self.sems.append(name)

    def op(self, eng, fn, waits=(), inc=None, inc_n=None):
        if eng == "tensor":
            self.pe_tags.append(self.tag)
        byname = {}
        for t in waits:
            if t is None:
                continue
            s, v = t
            if v <= 0:
                continue
            if byname.get(s, 0) < v:
                byname[s] = v
        for s, v in byname.items():
            key = (eng, s)
            if self.hw.get(key, 0) >= v:
                continue
            self.hw[key] = v
            self.q[eng].append(("w", s, v))
        tok = None
        if inc is not None:
            self._reg(inc)
            n = inc_n if inc_n is not None else 1
            self.cnt[inc] += n
            tok = (inc, self.cnt[inc])
            self.q[eng].append(("o", fn, inc, n))
        else:
            self.q[eng].append(("o", fn, None, 0))
        return tok

    def finalize(self, nc, block, stack):
        sems = {s: stack.enter_context(nc.semaphore(s)) for s in self.sems}
        for eng_name in ENGINES:
            items = list(self.q[eng_name])

            def make(items):
                def emit(eng):
                    for it in items:
                        if it[0] == "w":
                            eng.wait_ge(sems[it[1]], it[2])
                        else:
                            ins = it[1](eng)
                            if it[2] is not None:
                                ins.then_inc(sems[it[2]], it[3])
                return emit

            getattr(block, eng_name)(make(items))


def ffn_dag_meta():
    meta = []
    lind = 0
    for c in range(NUM_NOD):
        nsrc = min(c + 1, 5)
        snode = c - nsrc
        meta.append(dict(c=c, eq=lind + nsrc - 1, ek=lind, ev=lind + nsrc // 2,
                         in_q=c - 1, in_k=snode, in_v=snode + nsrc // 2))
        lind += nsrc
    return meta


def build_nc(T=2048, debug=False):
    TC = T // 128
    PW = T // 2
    NQ = T // PW
    SEG = min(512, T)
    NSEG_T = T // SEG
    PSEG = min(512, PW)
    NSEG_P = PW // PSEG
    NH = min(1024, T)
    NHA = T // NH

    nc = bass.Bass("TRN2", target_bir_lowering=False)

    x_in = nc.dram_tensor("x", [T, D], F32, kind="ExternalInput")
    out_t = nc.dram_tensor("out", [T, D], F32, kind="ExternalOutput")
    W = {}
    for nm, shp in [
        ("aWq", [D, D]), ("aWk", [D, D]), ("aWv", [D, D]), ("aWo", [D, D]),
        ("abq", [D]), ("abk", [D]), ("abv", [D]), ("abo", [D]),
        ("agamma", [D]), ("abeta", [D]), ("few", [6, 3]), ("fact", [6]),
        ("fWe", [20, D, D]), ("fbe", [20, D]), ("fWq", [6, D, D]),
        ("fWk", [6, D, D]), ("fWv", [6, D, D]), ("fWo", [6, D, D]),
        ("fln_g", [D]), ("fln_b", [D]),
    ]:
        W[nm] = nc.dram_tensor(nm, shp, F32, kind="ExternalInput")

    xn_sp = nc.dram_tensor("xn_sp", [T, D], BF16, kind="Internal")
    n_sp = {c: nc.dram_tensor(f"n{c}_sp", [D, T], BF16, kind="Internal")
            for c in (0, 1, 2)}
    n5_sp = nc.dram_tensor("n5_sp", [D, T], BF16, kind="Internal")
    y_sp = nc.dram_tensor("y_sp", [D, T], BF16, kind="Internal")
    dbg = {}
    if debug:
        for c in (0, 1):
            dbg[f"q{c}"] = nc.dram_tensor(f"dbg_q{c}", [128, T], BF16,
                                          kind="ExternalOutput")
            dbg[f"k{c}"] = nc.dram_tensor(f"dbg_k{c}", [128, T], BF16,
                                          kind="ExternalOutput")
            dbg[f"ctx{c}"] = nc.dram_tensor(f"dbg_ctx{c}", [128, T], BF16,
                                            kind="ExternalOutput")
            dbg[f"w{c}"] = nc.dram_tensor(f"dbg_w{c}", [1, 1024], F32,
                                          kind="ExternalOutput")
            dbg[f"vv{c}"] = nc.dram_tensor(f"dbg_vv{c}", [128, 1024], BF16,
                                           kind="ExternalOutput")

    P = Prog()
    stack = ExitStack()
    e = stack.enter_context

    slot = [e(nc.sbuf_tensor(f"slot{i}", [128, KC * T], BF16)) for i in range(4)]
    wbuf = [e(nc.sbuf_tensor(f"wbuf{i}", [128, KC * 1024], BF16)) for i in range(2)]
    wstage = [e(nc.sbuf_tensor(f"wstage{i}", [128, 1024], F32)) for i in range(2)]
    qbuf = e(nc.sbuf_tensor("qbuf", [128, T], BF16))
    kbuf = e(nc.sbuf_tensor("kbuf", [128, T], BF16))
    attb = [e(nc.sbuf_tensor(f"attb{i}", [128, max(PW, 1024)], BF16))
            for i in range(4)]
    sq_scr = attb[2]
    wb_sb = e(nc.sbuf_tensor("wb_sb", [128, PW], BF16))
    fio = [e(nc.sbuf_tensor(f"fio{i}", [128, 1024], F32)) for i in range(3)]
    ones_bf = e(nc.sbuf_tensor("ones_bf", [128, 1], BF16))
    ones_sq = e(nc.sbuf_tensor("ones_sq", [128, 128], BF16))
    ones_bfr = e(nc.sbuf_tensor("ones_bfr", [1, 128], BF16))
    ones_f32r = e(nc.sbuf_tensor("ones_f32r", [1, 128], F32))
    ident_f32 = e(nc.sbuf_tensor("ident_f32", [128, 128], F32))
    iscaled = e(nc.sbuf_tensor("iscaled", [128, 128], BF16))
    t128 = e(nc.sbuf_tensor("t128", [128, 128], BF16))
    beT = e(nc.sbuf_tensor("beT", [128, 160], F32))
    vecT = e(nc.sbuf_tensor("vecT", [128, 64], F32))
    vec_stage = e(nc.sbuf_tensor("vec_stage", [128, 128], F32))
    ewrow = e(nc.sbuf_tensor("ewrow", [1, 32], F32))
    scal = e(nc.sbuf_tensor("scal", [128, 32], F32))
    tiny = e(nc.sbuf_tensor("tiny", [1, 16], F32))
    lrow = fio[2]
    stat = e(nc.sbuf_tensor("stat", [128, 8], F32))
    gamT = e(nc.sbuf_tensor("gamT", [128, 2 * 1024], BF16))
    negM = e(nc.sbuf_tensor("negM", [128, 16], F32))

    bvrow = lrow.ap().bitcast(BF16)

    ps = [e(nc.psum_tensor(f"ps{i}", [128, 1024], F32)) for i in range(4)]
    ps_last = {0: None, 1: None, 2: None, 3: None}

    slot_w = [[None] * max(KC, TC) for _ in range(4)]
    slot_r = [[] for _ in range(4)]

    def slot_written(i, ch, tok):
        slot_w[i][ch] = tok if isinstance(tok, list) else [tok]
        slot_r[i] = []

    def slot_read(i, tok):
        slot_r[i].append(tok)

    def wtile(b, kc, c0, w):
        return wbuf[b][:, kc * 1024 + c0: kc * 1024 + c0 + w]

    def fchunk(s, c, t0=0, tw=None):
        tw = T if tw is None else tw
        return slot[s][:, c * T + t0: c * T + t0 + tw]

    def tchunk(s, tc, d0=0, dw=1024):
        return slot[s][:, tc * 1024 + d0: tc * 1024 + d0 + dw]

    dge, pln, trq = "gpsimd", "sync", "sync"

    def pser():
        P._reg("P")
        return [("P", P.cnt["P"])]

    t_ew = P.op(trq, lambda g: g.dma_start(
        out=ewrow[0:1, 0:18],
        in_=W["few"].ap().rearrange("a b -> (a b)")[None, :]),
        waits=pser(), inc="P", inc_n=16)
    t_aw = P.op(trq, lambda g: g.dma_start(
        out=ewrow[0:1, 18:24], in_=W["fact"].ap()[None, :]),
        waits=pser(), inc="P", inc_n=16)

    t_io1 = P.op("gpsimd", lambda g: g.iota(
        fio[0][:, 0:128], [[1, 128]], channel_multiplier=-1,
        allow_small_or_imprecise_dtypes=True), inc="tPOOL")
    t_idf = P.op("vector", lambda v: v.tensor_scalar(
        ident_f32[:, :], fio[0][:, 0:128], 0.0, None, ALU.is_equal),
        waits=[t_io1], inc="tDVE")
    t_t128 = P.op("vector", lambda v: v.tensor_scalar(
        t128[:, :], fio[0][:, 0:128], 0.0, None, ALU.is_ge),
        waits=[t_io1], inc="tDVE")
    t_on1 = P.op("gpsimd", lambda g: g.memset(ones_bf[:, :], 1.0), inc="tPOOL")
    t_on4 = P.op("gpsimd", lambda g: g.memset(ones_sq[:, :], 1.0), inc="tPOOL")
    t_on2 = P.op("gpsimd", lambda g: g.memset(ones_f32r[:, :], 1.0), inc="tPOOL")
    t_on3 = P.op("gpsimd", lambda g: g.memset(ones_bfr[:, :], 1.0), inc="tPOOL")

    t_be1 = P.op(trq, lambda g: g.dma_start(
        out=vec_stage[:, 0:128],
        in_=W["fbe"].ap().rearrange("a (c p) -> (a c) p", p=128)[0:128, :]),
        waits=pser(), inc="P", inc_n=16)
    tk = P.op("tensor", lambda t: t.transpose(
        ps[0][:, 0:128], vec_stage[:, 0:128], ident_f32[:, :]),
        waits=[t_be1, t_idf], inc="tPE")
    t_ev1 = P.op("vector", lambda v: v.tensor_copy(beT[:, 0:128], ps[0][:, 0:128]),
                 waits=[tk], inc="tDVE")
    ps_last[0] = t_ev1
    t_be2 = P.op(trq, lambda g: g.dma_start(
        out=vec_stage[0:32, 0:128],
        in_=W["fbe"].ap().rearrange("a (c p) -> (a c) p", p=128)[128:160, :]),
        waits=pser() + [tk], inc="P", inc_n=16)
    tk = P.op("tensor", lambda t: t.transpose(
        ps[1][:, 0:32], vec_stage[0:32, 0:128], ident_f32[0:32, 0:32]),
        waits=[t_be2], inc="tPE")
    t_ev2 = P.op("vector", lambda v: v.tensor_copy(beT[:, 128:160], ps[1][:, 0:32]),
                 waits=[tk], inc="tDVE")
    ps_last[1] = t_ev2

    vps = ["abq", "abk", "abv", "abo", "agamma", "abeta", "fln_g", "fln_b"]
    vtoks = []
    for i, nm in enumerate(vps):
        vtoks.append(P.op(trq, lambda g, nm=nm, i=i: g.dma_start(
            out=vec_stage[8 * i:8 * i + 8, 0:128],
            in_=W[nm].ap().rearrange("(c p) -> c p", p=128)),
            waits=pser() + ([t_ev2] if i == 0 else []), inc="P", inc_n=16))
    tk = P.op("tensor", lambda t: t.transpose(
        ps[0][:, 0:64], vec_stage[0:64, 0:128], ident_f32[0:64, 0:64]),
        waits=vtoks + [ps_last[0]], inc="tPE")
    t_vec = P.op("vector", lambda v: v.tensor_copy(vecT[:, 0:64], ps[0][:, 0:64]),
                 waits=[tk], inc="tDVE")
    ps_last[0] = t_vec

    def vcol(name, c):
        i = vps.index(name)
        return vecT[:, i * 8 + c: i * 8 + c + 1]

    tks = []
    for j in range(24):
        tks.append(P.op("tensor", lambda t, j=j: t.matmul(
            ps[1][:, j:j + 1], ones_f32r[0:1, :], ewrow[0:1, j:j + 1],
            start=True, stop=True),
            waits=[t_ew, t_aw, t_on2, ps_last[1]] if j == 0 else [], inc="tPE"))
    t_scal = P.op("vector", lambda v: v.tensor_copy(scal[:, 0:24], ps[1][:, 0:24]),
                  waits=[tks[-1]], inc="tDVE")
    ps_last[1] = t_scal

    xn_wtoks = []
    lrow_gate = []
    fio_free = [t_t128, None, None]
    attb_free = [[], [], [], []]
    for i in range(TC):
        b = i % 2
        tx = P.op(pln, lambda g, i=i, b=b: g.dma_start(
            out=fio[b][:, :], in_=x_in.ap()[i * 128:(i + 1) * 128, :]),
            waits=[fio_free[b]], inc=f"X{b}", inc_n=16)
        t1 = P.op("vector", lambda v, b=b: v.reduce_sum(
            stat[:, 0:1], fio[b][:, :], axis=AX.X), waits=[tx], inc="tDVE")
        t2 = P.op("vector", lambda v: v.tensor_scalar(
            stat[:, 1:2], stat[:, 0:1], 1.0 / D, None, ALU.mult),
            waits=[t1], inc="tDVE")
        t3 = P.op("vector", lambda v, b=b: v.tensor_scalar(
            fio[2][:, :], fio[b][:, :], stat[:, 1:2], None, ALU.subtract),
            waits=[t2, fio_free[2]], inc="tDVE")
        fio_free[b] = t3
        t4 = P.op("vector", lambda v, b=b: v.scalar_tensor_tensor(
            attb[b][:, 0:1024], fio[2][:, :], 1.0, fio[2][:, :],
            ALU.mult, ALU.mult, accum_out=stat[:, 2:3]),
            waits=[t3] + attb_free[b], inc="tDVE")
        t5 = P.op("vector", lambda v: v.tensor_scalar(
            stat[:, 3:4], stat[:, 2:3], 1.0 / D, 1e-5, ALU.mult, ALU.add),
            waits=[t4], inc="tDVE")
        t6 = P.op("scalar", lambda s: s.activation(
            stat[:, 4:5], stat[:, 3:4], AF.Ln), waits=[t5], inc="tACT")
        t7 = P.op("scalar", lambda s: s.activation(
            stat[:, 5:6], stat[:, 4:5], AF.Exp, scale=-0.5),
            waits=[t6], inc="tACT")
        t8 = P.op("vector", lambda v, b=b: v.tensor_scalar(
            attb[b][:, 0:1024], fio[2][:, :], stat[:, 5:6], None, ALU.mult),
            waits=[t7], inc="tDVE")
        fio_free[2] = t8
        td = P.op(pln, lambda g, i=i, b=b: g.dma_start(
            out=xn_sp.ap()[i * 128:(i + 1) * 128, :], in_=attb[b][:, 0:1024]),
            waits=[t8], inc=f"XN{b}", inc_n=16)
        attb_free[b] = [td]
        xn_wtoks.append(td)

    xn_done = xn_wtoks[-2:]
    lrow_gate.extend(xn_done)
    A, B, C, Dv = 0, 1, 2, 3
    for c in range(KC):
        tk = P.op(trq, lambda g, c=c: g.dma_start(
            out=fchunk(A, c), in_=xn_sp.ap()[:, c * 128:(c + 1) * 128],
            transpose=True), waits=xn_done, inc=f"F0c{c}", inc_n=16)
        slot_written(A, c, tk)

    wb_state = {"i": 0}
    wbuf_free = [[], []]
    wstage_free = [None, None]

    def load_W(dram_ap, slot_override=None):
        if slot_override is None:
            b = wb_state["i"] % 2
            wb_state["i"] += 1
        else:
            b = slot_override
        gate = list(wbuf_free[b])
        tcast = None
        for q in range(KC):
            sb_ = q % 2
            tld = P.op(pln, lambda g, q=q, sb_=sb_, a=dram_ap: g.dma_start(
                out=wstage[sb_][:, :],
                in_=a.rearrange("(kc p) n -> p kc n", p=128)[:, q:q + 1, :]),
                waits=[wstage_free[sb_]], inc=f"WS{sb_}", inc_n=16)
            tcast = P.op("gpsimd", lambda g, q=q, sb_=sb_, b=b: g.tensor_copy(
                wbuf[b][:, q * 1024:(q + 1) * 1024], wstage[sb_][:, :]),
                waits=[tld] + (gate if q < 2 else []), inc="tPOOL")
            wstage_free[sb_] = tcast
        wbuf_free[b] = []
        return b, tcast

    def projection(src_slot, dst_slot, wslot, wtok, bias_fn=None,
                   iadd_slot=None, iadd_tok=None, spill=None, spill_sem=None):
        spills = []
        last_ev = None
        dst_gate = list(slot_r[dst_slot])
        TW = min(1024, T)
        NTW = T // TW
        for do in range(KC):
            ev_th = []
            for th in range(NTW):
                pb = (do * NTW + th) % 4
                mmw = [wtok, ps_last[pb]] + dst_gate
                if iadd_slot is not None:
                    mmw += (slot_w[iadd_slot][do] or []) + [iadd_tok]
                t_mm = None
                for kc in range(KC):
                    for sg in range(TW // SEG):
                        t_mm = P.op("tensor", lambda t, do=do, th=th, kc=kc,
                                    pb=pb, sg=sg, ws=wslot, ss=src_slot:
                                    t.matmul(
                            ps[pb][:, sg * SEG:(sg + 1) * SEG],
                            wtile(ws, kc, do * 128, 128),
                            fchunk(ss, kc, th * TW + sg * SEG, SEG),
                            start=(kc == 0),
                            stop=(kc == KC - 1 and iadd_slot is None)),
                            waits=(mmw if (kc == 0 and sg == 0) else []) +
                            ((slot_w[src_slot][kc] or []) if sg == 0
                             else []),
                            inc="tPE")
                if iadd_slot is not None:
                    for sg in range(TW // SEG):
                        t_mm = P.op("tensor", lambda t, do=do, th=th, pb=pb,
                                    sg=sg, s2=iadd_slot: t.matmul(
                            ps[pb][:, sg * SEG:(sg + 1) * SEG], iscaled[:, :],
                            fchunk(s2, do, th * TW + sg * SEG, SEG),
                            start=False, stop=True), inc="tPE")
                    slot_read(iadd_slot, t_mm)
                slot_read(src_slot, t_mm)
                eng = "scalar" if (do * NTW + th) % 2 == 0 else "vector"

                def ev_s(s, do=do, th=th, pb=pb, ds=dst_slot, TW=TW):
                    if bias_fn is not None:
                        return s.activation(
                            fchunk(ds, do, th * TW, TW), ps[pb][:, 0:TW],
                            AF.Identity, bias=bias_fn(do))
                    return s.activation(
                        fchunk(ds, do, th * TW, TW), ps[pb][:, 0:TW], AF.Copy)

                def ev_v(v, do=do, th=th, pb=pb, ds=dst_slot, TW=TW):
                    if bias_fn is not None:
                        return v.tensor_scalar(
                            fchunk(ds, do, th * TW, TW), ps[pb][:, 0:TW],
                            bias_fn(do), None, ALU.add)
                    return v.tensor_copy(
                        fchunk(ds, do, th * TW, TW), ps[pb][:, 0:TW])
                if eng == "scalar":
                    t_ev = P.op("scalar", ev_s, waits=[t_mm], inc="tACT")
                else:
                    t_ev = P.op("vector", ev_v, waits=[t_mm], inc="tDVE")
                ps_last[pb] = t_ev
                ev_th.append(t_ev)
                last_ev = t_ev
            slot_written(dst_slot, do, list(ev_th))
            if spill is not None:
                sp = P.op(pln, lambda g, do=do, a=spill, ds=dst_slot: g.dma_start(
                    out=a.ap().rearrange("(c p) t -> c p t", p=128)[do],
                    in_=fchunk(ds, do)), waits=ev_th, inc=spill_sem, inc_n=16)
                slot_read(dst_slot, sp)
                spills.append(sp)
        return last_ev, spills

    def projection_tmajor(src_slot, dst_slot, wslot, wtok, scale_col=None,
                          bias_row=None, bias_tok=None):
        last_ev = None
        dst_gate = list(slot_r[dst_slot])
        for tc in range(TC):
            pb = tc % 4
            mmw = [wtok, ps_last[pb]] + dst_gate
            t_mm = None
            for kc in range(KC):
                for half in range(2):
                    t_mm = P.op("tensor", lambda t, tc=tc, kc=kc, half=half,
                                pb=pb, ws=wslot, ss=src_slot: t.matmul(
                        ps[pb][:, half * 512:(half + 1) * 512],
                        fchunk(ss, kc, tc * 128, 128),
                        wtile(ws, kc, half * 512, 512),
                        start=(kc == 0),
                        stop=(kc == KC - 1 and bias_row is None)),
                        waits=(mmw if (kc == 0 and half == 0) else []) +
                        ((slot_w[src_slot][kc] or []) if half == 0 else []),
                        inc="tPE")
            if bias_row is not None:
                for half in range(2):
                    t_mm = P.op("tensor", lambda t, half=half, pb=pb: t.matmul(
                        ps[pb][:, half * 512:(half + 1) * 512], ones_bfr[0:1, :],
                        bias_row[0:1, half * 512:(half + 1) * 512],
                        start=False, stop=True),
                        waits=[t_on3, bias_tok], inc="tPE")
            slot_read(src_slot, t_mm)

            def ev(s, tc=tc, pb=pb, ds=dst_slot):
                return s.activation(
                    tchunk(ds, tc), ps[pb][:, 0:1024], AF.Copy,
                    scale=(scale_col if scale_col is not None else 1.0))
            t_ev = P.op("scalar", ev, waits=[t_mm], inc="tACT")
            ps_last[pb] = t_ev
            slot_written(dst_slot, tc % max(KC, TC), t_ev)
            last_ev = t_ev
        return last_ev

    def fill_slot(dst_slot, src_dram, fmajor_src, src_ready):
        gate = list(slot_r[dst_slot]) + list(src_ready)
        for c in range(KC):
            if fmajor_src:
                tk = P.op(pln, lambda g, c=c, a=src_dram, ds=dst_slot: g.dma_start(
                    out=fchunk(ds, c),
                    in_=a.ap().rearrange("(c p) t -> c p t", p=128)[c]),
                    waits=gate if c == 0 else [], inc=f"F{dst_slot}c{c}",
                    inc_n=16)
            else:
                tk = P.op(trq, lambda g, c=c, a=src_dram, ds=dst_slot: g.dma_start(
                    out=fchunk(ds, c), in_=a.ap()[:, c * 128:(c + 1) * 128],
                    transpose=True), waits=gate if c == 0 else [],
                    inc=f"F{dst_slot}c{c}", inc_n=16)
            slot_written(dst_slot, c, tk)

    def emit_negM(q_ap_fn, k_ap_fn, base, dh, abs_ap, abs_imm, ready,
                  ready_fn=None, out_col=0, sst_eng="vector"):
        prev = None
        for idx, apfn in enumerate((q_ap_fn, k_ap_fn)):
            for ha in range(NHA):
                rd = (list(ready) if ready is not None
                      else list(ready_fn(idx, ha)))
                t_sq = P.op(sst_eng, lambda v, apfn=apfn, ha=ha, base=base,
                            dh=dh: v.tensor_tensor(
                    sq_scr[base:base + dh, 0:NH], apfn(ha), apfn(ha),
                    ALU.mult),
                    waits=rd + [prev] + attb_free[2] + negm_m_last,
                    inc=("tPOOL" if sst_eng == "gpsimd" else "tDVE"))
                t_nm = None
                for sg in range(0, NH, 512):
                    sw = min(512, NH - sg)
                    t_nm = P.op("tensor", lambda t, sg=sg, sw=sw, base=base,
                                dh=dh: t.matmul(
                        ps[3][:, sg:sg + sw], ones_sq[base:base + dh, :],
                        sq_scr[base:base + dh, sg:sg + sw],
                        start=True, stop=True),
                        waits=[t_sq, ps_last[3], t_on4] if sg == 0 else [],
                        inc="tPE")
                dst = 4 + idx if ha == 0 else 7
                t_mx = P.op("vector", lambda v, dst=dst: v.reduce_max(
                    stat[:, dst:dst + 1], ps[3][:, 0:NH], axis=AX.X),
                    waits=[t_nm], inc="tDVE")
                if ha == 1:
                    t_mx = P.op("vector", lambda v, idx=idx: v.tensor_tensor(
                        stat[:, 4 + idx:5 + idx], stat[:, 4 + idx:5 + idx],
                        stat[:, 7:8], ALU.max), waits=[t_mx], inc="tDVE")
                ps_last[3] = t_mx
                attb_free[2] = [t_nm]
                prev = t_mx
        t_p = P.op("vector", lambda v: v.tensor_tensor(
            stat[:, 7:8], stat[:, 4:5], stat[:, 5:6], ALU.add),
            waits=[prev], inc="tDVE")
        gate = [ps_last[0], ps_last[1]] if out_col == 0 else []
        if abs_ap is not None:
            t_m = P.op("vector", lambda v, a=abs_ap, oc=out_col:
                       v.tensor_scalar(
                negM[:, oc:oc + 1], stat[:, 7:8], a, -1.0,
                ALU.mult, ALU.mult), waits=[t_p] + gate, inc="tDVE")
        else:
            t_m = P.op("vector", lambda v, oc=out_col: v.tensor_scalar(
                negM[:, oc:oc + 1], stat[:, 7:8], -abs_imm, None, ALU.mult),
                waits=[t_p] + gate, inc="tDVE")
        negm_m_last[:] = [t_m]

        def emitB():
            return t_m
        return emitB

    dbg_gate = []
    qbuf_gate = []
    wrow_gate = []

    def attention_qtile(kf_tile_fn, q_rhs_fn, v_tile_fn, dh, obase, kc_list,
                        diag_off, scale_col, negM_fn, wfold_11, ctx_dst_ap,
                        extra_ready, pending_fin=None, negM_col=0):
        n_kc = len(kc_list)
        t_avl = None
        t_lmm = None
        tp = (0, obase) if obase else None
        ready_q = []
        scores_tok = {}

        def emit_scores(j):
            kc = kc_list[j]
            pb = j % 2
            mmw = [ps_last[pb]] + list(extra_ready)
            t_s = None
            for sg in range(NSEG_P):
                t_s = P.op("tensor", lambda t, kc=kc, sg=sg, pb=pb: t.matmul(
                    ps[pb][:, sg * PSEG:(sg + 1) * PSEG],
                    kf_tile_fn(kc), q_rhs_fn(sg * PSEG, PSEG),
                    start=True, stop=True),
                    waits=mmw if sg == 0 else [], inc="tPE")
            scores_tok[j] = t_s

        def seg_ranges(off):
            out = []
            x = off
            while x < PW:
                nxt = min(PW, (x // PSEG + 1) * PSEG)
                out.append((x, nxt - x))
                x = nxt
            return out

        def emit_lav(j, ab, t_ready, off=0, l_first=False, l_last=False,
                     av_first=False, av_stop=False):
            nonlocal t_lmm, t_avl
            kc = kc_list[j]
            for (o, w) in seg_ranges(off):
                t_lmm = P.op("tensor", lambda t, o=o, w=w, ab=ab,
                             l_first=l_first, l_last=l_last: t.matmul(
                    ps[3][0:1, o:o + w], ones_bf[:, 0:1],
                    attb[ab][:, o:o + w],
                    start=l_first, stop=l_last),
                    waits=[t_ready, t_on1] + ([ps_last[3]] if l_first else []),
                    inc="tPE")
            for (o, w) in seg_ranges(off):
                t_avl = P.op("tensor", lambda t, kc=kc, o=o, w=w, ab=ab, dh=dh,
                             obase=obase, tp=tp, av_first=av_first,
                             av_stop=av_stop: t.matmul(
                    ps[2][obase:obase + dh, o:o + w],
                    v_tile_fn(kc), attb[ab][:, o:o + w],
                    start=av_first, stop=av_stop, tile_position=tp),
                    waits=([ps_last[2]] if av_first else []), inc="tPE")
            attb_free[ab] = [t_avl]

        def pair_a(e0, e1):
            nonlocal t_avl, t_lmm
            j0, a0, r0, _ = e0
            j1, a1, r1, _ = e1
            t_av0 = None
            for (o, w) in seg_ranges(0):
                t_av0 = P.op("tensor", lambda t, kc0=kc_list[j0], o=o, w=w,
                             a0=a0, dh=dh, obase=obase, tp=tp,
                             first=(j0 == 0): t.matmul(
                    ps[2][obase:obase + dh, o:o + w],
                    v_tile_fn(kc0), attb[a0][:, o:o + w],
                    start=first, stop=False, tile_position=tp),
                    waits=[r0] + ([ps_last[2]] if j0 == 0 else []), inc="tPE")
            t_pr = P.op("vector", lambda v, a0=a0, a1=a1: v.tensor_tensor(
                attb[a0][:, 0:PW], attb[a0][:, 0:PW], attb[a1][:, 0:PW],
                ALU.add), waits=[r0, r1, t_av0], inc="tDVE")

            def pair_b():
                nonlocal t_avl, t_lmm
                t_l = None
                for (o, w) in seg_ranges(0):
                    t_l = P.op("tensor", lambda t, o=o, w=w, a0=a0,
                               first=(j0 == 0), last=(j1 == n_kc - 1):
                               t.matmul(
                        ps[3][0:1, o:o + w], ones_bf[:, 0:1],
                        attb[a0][:, o:o + w], start=first, stop=last),
                        waits=[t_pr, t_on1] +
                        ([ps_last[3]] if j0 == 0 else []), inc="tPE")
                t_av2 = None
                for (o, w) in seg_ranges(0):
                    t_av2 = P.op("tensor", lambda t, kc1=kc_list[j1], o=o,
                                 w=w, a1=a1, dh=dh, obase=obase, tp=tp,
                                 last=(j1 == n_kc - 1): t.matmul(
                        ps[2][obase:obase + dh, o:o + w],
                        v_tile_fn(kc1), attb[a1][:, o:o + w],
                        start=False, stop=last, tile_position=tp),
                        waits=[r1], inc="tPE")
                t_lmm = t_l
                attb_free[a0] = [t_l]
                attb_free[a1] = [t_pr, t_av2]
                t_avl = t_av2
            return pair_b

        P.tag = P.tag + "/att"
        pair_mode = (diag_off is None and n_kc % 2 == 0 and n_kc >= 4)
        stash = [None]

        for j in range(min(2, n_kc)):
            emit_scores(j)
        if pending_fin is not None:
            pending_fin()
        t_negM = negM_fn()
        for j, kc in enumerate(kc_list):
            if j >= 2:
                emit_scores(j)
            if stash[0] is not None:
                stash[0]()
                stash[0] = None
            pb = j % 2
            ab = j % (4 if pair_mode else 3)
            off = diag_off.get(kc, 0) if diag_off is not None else 0
            t_x = P.op("scalar", lambda s, pb=pb, ab=ab, nc_=negM_col,
                       off=off: s.activation(
                attb[ab][:, off:PW], ps[pb][:, off:PW], AF.Exp,
                bias=negM[:, nc_:nc_ + 1], scale=scale_col),
                waits=[scores_tok[j], t_negM] + attb_free[ab], inc="tACT")
            ps_last[pb] = t_x
            t_ready = t_x
            if diag_off is not None and kc in diag_off:
                t_ready = P.op("vector", lambda v, ab=ab, off=off: v.tensor_tensor(
                    attb[ab][:, off:off + 128], attb[ab][:, off:off + 128],
                    t128[:, :], ALU.mult), waits=[t_x, t_t128], inc="tDVE")
            ready_q.append((j, ab, t_ready, off))
            if pair_mode and len(ready_q) >= 4:
                stash[0] = pair_a(ready_q.pop(0), ready_q.pop(0))
            elif not pair_mode and len(ready_q) >= 3:
                jj, aa, rr, oo = ready_q.pop(0)
                emit_lav(jj, aa, rr, off=oo, l_first=(jj == 0),
                         l_last=(jj == n_kc - 1), av_first=(jj == 0),
                         av_stop=(jj == n_kc - 1))
        if stash[0] is not None:
            stash[0]()
            stash[0] = None
        while ready_q:
            if pair_mode and len(ready_q) >= 2:
                pair_a(ready_q.pop(0), ready_q.pop(0))()
            else:
                jj, aa, rr, oo = ready_q.pop(0)
                emit_lav(jj, aa, rr, off=oo, l_first=(jj == 0),
                         l_last=(jj == n_kc - 1), av_first=(jj == 0),
                         av_stop=(jj == n_kc - 1))
        t_lmm_f = t_lmm
        t_avl_f = t_avl
        t_le = P.op("scalar", lambda s: s.activation(
            lrow[0:1, 0:PW], ps[3][0:1, 0:PW], AF.Ln),
            waits=[t_lmm_f] + dbg_gate + lrow_gate + wrow_gate, inc="tACT")
        ps_last[3] = t_le
        t_rc = P.op("scalar", lambda s, a=wfold_11: s.activation(
            wb_sb[0:1, 0:PW], lrow[0:1, 0:PW], AF.Exp, scale=-1.0,
            bias=(a if a is not None else 0.0)),
            waits=[t_le, ps_last[2]] + wrow_gate, inc="tACT")

        def fin(cbs=()):
            t_wb = None
            for sg in range(NSEG_P):
                t_wb = P.op("tensor", lambda t, sg=sg, dh=dh,
                            obase=obase, tp=tp: t.matmul(
                    ps[3][obase:obase + dh, sg * PSEG:(sg + 1) * PSEG],
                    ones_bfr[0:1, 0:dh],
                    wb_sb[0:1, sg * PSEG:(sg + 1) * PSEG],
                    start=True, stop=True, tile_position=tp),
                    waits=[t_rc, ps_last[3], t_on3] if sg == 0 else [],
                    inc="tPE")
            wrow_gate[:] = [t_wb]
            t_we = P.op("vector", lambda v, dh=dh, obase=obase:
                        v.tensor_copy(wb_sb[obase:obase + dh, 0:PW],
                                      ps[3][obase:obase + dh, 0:PW]),
                        waits=[t_wb], inc="tDVE")
            ps_last[3] = t_we
            t_fin = P.op("vector", lambda v, dh=dh, obase=obase,
                         dst=ctx_dst_ap: v.tensor_tensor(
                dst, ps[2][obase:obase + dh, 0:PW],
                wb_sb[obase:obase + dh, 0:PW], ALU.mult),
                waits=[t_we, t_avl_f], inc="tDVE")
            ps_last[2] = t_fin
            for cb in cbs:
                cb(t_fin)
            return t_fin
        return fin

    meta = ffn_dag_meta()
    iscaled_free = []
    lnaw_gate = [None]
    negm_m_last = [None]
    pending = [None]

    def defer_fin(fin, cbs):
        def run():
            return fin(cbs=cbs)
        pending[0] = run

    def take_fin():
        p = pending[0]
        pending[0] = None
        return p

    def drain_fin():
        p = take_fin()
        if p is not None:
            p()
    weq_pre = load_W(W["fWe"].ap()[meta[0]["eq"]], slot_override=0)
    last_spills = {}

    for nd in meta:
        c = nd["c"]
        t_sc1 = P.op("vector", lambda v, c=c: v.tensor_tensor(
            scal[:, 24:25], scal[:, 3 * c:3 * c + 1], scal[:, 3 * c + 1:3 * c + 2],
            ALU.mult), waits=[t_scal], inc="tDVE")
        t_sc2 = P.op("vector", lambda v: v.tensor_scalar(
            scal[:, 24:25], scal[:, 24:25], 1.0 / np.sqrt(128.0), None,
            ALU.mult), waits=[t_sc1], inc="tDVE")
        t_sc3 = P.op("vector", lambda v: v.tensor_tensor(
            scal[:, 25:26], scal[:, 24:25], scal[:, 24:25], ALU.mult),
            waits=[t_sc2], inc="tDVE")
        t_isc = P.op("vector", lambda v, c=c: v.tensor_scalar(
            iscaled[:, :], ident_f32[:, :], scal[:, 3 * c:3 * c + 1], None,
            ALU.mult), waits=[t_scal, t_idf] + iscaled_free, inc="tDVE")
        iscaled_free = []
        t_lnaw = P.op("scalar", lambda s, c=c: s.activation(
            tiny[0:1, 12:13], ewrow[0:1, 18 + c:19 + c], AF.Ln),
            waits=[t_ew, t_aw] + lnaw_gate, inc="tACT")
        lnaw_gate = [t_lnaw]
        t_sf1 = P.op("scalar", lambda s: s.activation(
            scal[:, 27:28], scal[:, 25:26], AF.Ln),
            waits=[t_sc3] + negm_m_last, inc="tACT")
        t_sf2 = P.op("scalar", lambda s: s.activation(
            scal[:, 27:28], scal[:, 27:28], AF.Exp, scale=0.5),
            waits=[t_sf1], inc="tACT")
        t_sf3 = P.op("vector", lambda v: v.tensor_scalar(
            scal[:, 26:27], scal[:, 27:28], 0.51, None, ALU.mult),
            waits=[t_sf2], inc="tDVE")
        negm_m_last = [t_sf3]

        P.tag = f"n{c}:pq"
        X1, X2 = c % 2, 1 - c % 2
        wq_b, wq_t = weq_pre
        t_pq, _ = projection(A, B, wq_b, wq_t,
                             bias_fn=lambda do, eq=nd["eq"]:
                             beT[:, eq * 8 + do:eq * 8 + do + 1])
        wbuf_free[wq_b].append(t_pq)

        if c == 0:
            pk_slot = B
        else:
            if nd["in_k"] == -1:
                fill_slot(Dv, xn_sp, False, xn_done)
            else:
                fill_slot(Dv, n_sp[nd["in_k"]], True,
                          last_spills.get(nd["in_k"], []))
            P.tag = f"n{c}:pk"
            wk_b, wk_t = load_W(W["fWe"].ap()[nd["ek"]], slot_override=X2)
            t_pk, _ = projection(Dv, C, wk_b, wk_t,
                                 bias_fn=lambda do, ek=nd["ek"]:
                                 beT[:, ek * 8 + do:ek * 8 + do + 1])
            wbuf_free[wk_b].append(t_pk)
            pk_slot = C

        if c <= 1:
            pv_slot = B
        else:
            fill_slot(A, n_sp[nd["in_v"]], True,
                      last_spills.get(nd["in_v"], []))
            P.tag = f"n{c}:pv"
            wv_b, wv_t = load_W(W["fWe"].ap()[nd["ev"]], slot_override=X1)
            t_pv, _ = projection(A, Dv, wv_b, wv_t,
                                 bias_fn=lambda do, ev=nd["ev"]:
                                 beT[:, ev * 8 + do:ev * 8 + do + 1])
            wbuf_free[wv_b].append(t_pv)
            pv_slot = Dv

        P.tag = f"n{c}:vv"
        wvv_b, wvv_t = load_W(W["fWv"].ap()[c], slot_override=X2)
        if c <= 1:
            gate = list(slot_r[A])
            if gate:
                P.op("tensor", lambda t: t.nop(), waits=gate)
        t_vv = projection_tmajor(pv_slot, A, wvv_b, wvv_t,
                                 scale_col=scal[:, 3 * c + 2:3 * c + 3])
        wbuf_free[wvv_b].append(t_vv)

        wq_b2, wq_t2 = load_W(W["fWq"].ap()[c], slot_override=X1)
        wk_b2, wk_t2 = load_W(W["fWk"].ap()[c], slot_override=X2)
        wo_t = None
        wo_b = None

        for h in range(8):
            P.tag = f"n{c}:h{h}:qk"
            qk_done = []
            TW2 = min(1024, T)
            NTW2 = T // TW2
            for which, (bufap, wsl, wtk, src_sl) in enumerate(
                    [(qbuf, wq_b2, wq_t2, B), (kbuf, wk_b2, wk_t2, pk_slot)]):
                if which == 1:
                    drain_fin()
                for th in range(NTW2):
                    pb = (which * NTW2 + th) % 2
                    mmw = [wtk, ps_last[pb]]
                    t_mm = None
                    for kc in range(KC):
                        for sg in range(TW2 // SEG):
                            t_mm = P.op("tensor", lambda t, kc=kc, th=th,
                                        pb=pb, sg=sg, wsl=wsl, h=h,
                                        src_sl=src_sl: t.matmul(
                                ps[pb][:, sg * SEG:(sg + 1) * SEG],
                                wtile(wsl, kc, h * 128, 128),
                                fchunk(src_sl, kc, th * TW2 + sg * SEG, SEG),
                                start=(kc == 0), stop=(kc == KC - 1)),
                                waits=(mmw if (kc == 0 and sg == 0) else []) +
                                ((slot_w[src_sl][kc] or []) if sg == 0
                                 else []),
                                inc="tPE")
                    slot_read(src_sl, t_mm)
                    t_ev = P.op("vector", lambda v, th=th, pb=pb,
                                bufap=bufap, TW2=TW2: v.tensor_copy(
                        bufap[:, th * TW2:(th + 1) * TW2],
                        ps[pb][:, 0:TW2]),
                        waits=[t_mm] + qbuf_gate, inc="tDVE")
                    ps_last[pb] = t_ev
                    qk_done.append(t_ev)
            if debug and c in (0, 1) and h == 0:
                tdq = P.op(pln, lambda g, c=c: g.dma_start(
                    out=dbg[f"q{c}"].ap(), in_=qbuf[:, :]),
                    waits=qk_done, inc="DBG", inc_n=16)
                tdk = P.op(pln, lambda g, c=c: g.dma_start(
                    out=dbg[f"k{c}"].ap(), in_=kbuf[:, :]),
                    waits=qk_done, inc="DBG", inc_n=16)
                qbuf_gate[:] = [tdq, tdk]
            if h == 7:
                wbuf_free[wq_b2].append(qk_done[NSEG_T - 1])
                wbuf_free[wk_b2].append(qk_done[-1])
                wo_b, wo_t = load_W(W["fWo"].ap()[c], slot_override=X1)
                if c < 5:
                    nxt = meta[c + 1]
                    weq_pre = load_W(W["fWe"].ap()[nxt["eq"]],
                                     slot_override=X2)
                else:
                    weq_pre = load_W(W["aWq"].ap(), slot_override=X2)
            drain_fin()
            negM_B = emit_negM(
                lambda ha: qbuf[:, ha * NH:(ha + 1) * NH],
                lambda ha: kbuf[:, ha * NH:(ha + 1) * NH],
                0, 128, scal[:, 26:27], None, None,
                ready_fn=lambda idx, ha, NTW2=NTW2: [qk_done[min(
                    idx * NTW2 + ha, len(qk_done) - 1)]])
            negM_tok = [None]

            def negM_fn(B=negM_B, tok=negM_tok):
                if tok[0] is None:
                    tok[0] = B()
                return tok[0]
            for qt in range(NQ):
                fin = attention_qtile(
                    kf_tile_fn=lambda kc: kbuf[:, kc * 128:(kc + 1) * 128],
                    q_rhs_fn=lambda o, w, qt=qt: qbuf[:, qt * PW + o:
                                                      qt * PW + o + w],
                    v_tile_fn=lambda kc, h=h, A=A: slot[A][
                        :, kc * 1024 + h * 128: kc * 1024 + h * 128 + 128],
                    dh=128, obase=0, kc_list=list(range(TC)), diag_off=None,
                    scale_col=scal[:, 24:25], negM_fn=negM_fn,
                    wfold_11=tiny[0:1, 12:13],
                    ctx_dst_ap=fchunk(Dv, h, qt * PW, PW),
                    extra_ready=qk_done + [t_vv], pending_fin=take_fin())
                defer_fin(fin, [lambda t, A=A: slot_read(A, t),
                                lambda t, h=h: slot_written(Dv, h, t)])
        drain_fin()
        P.tag = f"n{c}:oproj"
        spill_t = n_sp.get(c)
        t_o, sp_toks = projection(Dv, B, wo_b, wo_t, bias_fn=None,
                                  iadd_slot=B, iadd_tok=t_isc, spill=spill_t,
                                  spill_sem=f"SPn{c}")
        if spill_t is not None:
            last_spills[c] = sp_toks
        iscaled_free = [t_o]
        wbuf_free[wo_b].append(t_o)
        A, B = B, A

    n5_spills = []
    for ch in range(KC):
        sp = P.op(pln, lambda g, ch=ch, A=A: g.dma_start(
            out=n5_sp.ap().rearrange("(c p) t -> c p t", p=128)[ch],
            in_=fchunk(A, ch)), waits=slot_w[A][ch] or [], inc="SPn5",
            inc_n=16)
        slot_read(A, sp)
        n5_spills.append(sp)

    P.tag = "L0:qkv"
    fill_slot(B, xn_sp, False, xn_done)
    wq_b, wq_t = weq_pre
    t_q, _ = projection(B, C, wq_b, wq_t, bias_fn=lambda do: vcol("abq", do))
    wbuf_free[wq_b].append(t_q)
    wk_b, wk_t = load_W(W["aWk"].ap(), slot_override=1 - wq_b)
    t_k, _ = projection(B, Dv, wk_b, wk_t, bias_fn=lambda do: vcol("abk", do))
    wbuf_free[wk_b].append(t_k)
    t_bv = P.op(dge, lambda g: g.dma_start(
        out=bvrow[0:1, 0:1024], in_=W["abv"].ap()[None, :]),
        waits=[ps_last[3]], inc="BV", inc_n=16)
    wv_b, wv_t = load_W(W["aWv"].ap(), slot_override=wq_b)
    gate = list(slot_r[A])
    if gate:
        P.op("tensor", lambda t: t.nop(), waits=gate)
    t_v = projection_tmajor(B, A, wv_b, wv_t, bias_row=bvrow, bias_tok=t_bv)
    wbuf_free[wv_b].append(t_v)
    wo_b, wo_t = load_W(W["aWo"].ap(), slot_override=1 - wv_b)

    NDIAG = PW // 128
    l0_negM = {}

    def l0_chain(hh):
        ch_, pr_ = hh // 2, (hh % 2) * 64
        eB = emit_negM(
            lambda ha, ch=ch_, pr=pr_: slot[C][
                pr:pr + 64, ch * T + ha * NH: ch * T + (ha + 1) * NH],
            lambda ha, ch=ch_, pr=pr_: slot[Dv][
                pr:pr + 64, ch * T + ha * NH: ch * T + (ha + 1) * NH],
            pr_, 64, None, 0.06375, [t_q, t_k], out_col=hh)
        l0_negM[hh] = eB()

    l0_chain(0)
    l0_chain(1)
    for h in range(16):
        ch, pr = h // 2, (h % 2) * 64
        P.tag = f"L0:h{h}"
        drain_fin()
        if h + 2 < 16:
            l0_chain(h + 2)

        def negM_fn(h=h):
            return l0_negM[h]
        for qt in range(NQ):
            kc_hi = (qt + 1) * NDIAG
            doff = {kc: kc * 128 - qt * PW for kc in range(qt * NDIAG, kc_hi)}
            fin = attention_qtile(
                kf_tile_fn=lambda kc, ch=ch, pr=pr: slot[Dv][
                    pr:pr + 64, ch * T + kc * 128: ch * T + (kc + 1) * 128],
                q_rhs_fn=lambda o, w, ch=ch, pr=pr, qt=qt: slot[C][
                    pr:pr + 64, ch * T + qt * PW + o: ch * T + qt * PW + o + w],
                v_tile_fn=lambda kc, h=h, A=A: slot[A][
                    :, kc * 1024 + h * 64: kc * 1024 + h * 64 + 64],
                dh=64, obase=pr, kc_list=list(range(kc_hi)), diag_off=doff,
                scale_col=0.125, negM_fn=negM_fn, wfold_11=None,
                ctx_dst_ap=slot[B][pr:pr + 64,
                                   ch * T + qt * PW: ch * T + qt * PW + PW],
                extra_ready=[t_v], pending_fin=take_fin(), negM_col=h)
            defer_fin(fin, [lambda t, A=A: slot_read(A, t),
                            lambda t: slot_read(C, t),
                            lambda t: slot_read(Dv, t),
                            lambda t, ch=ch: slot_written(B, ch, t)])
    drain_fin()
    P.tag = "L0:yproj"
    t_y, y_spills = projection(B, C, wo_b, wo_t,
                               bias_fn=lambda do: vcol("abo", do), spill=y_sp,
                               spill_sem="SPy")
    wbuf_free[wo_b].append(t_y)

    def build_gam(i0, i1, ready):
        toks = []
        for j, i in enumerate((i0, i1)):
            tr = P.op(trq, lambda g, i=i: g.dma_start(
                out=lrow[0:1, 0:1024], in_=W[vps[4 + i]].ap()[None, :]),
                waits=pser() + list(ready) + toks, inc="P", inc_n=16)
            for h2 in range(2):
                tm = P.op("tensor", lambda t, j=j, h2=h2: t.matmul(
                    ps[3][:, h2 * 512:(h2 + 1) * 512], ones_f32r[0:1, :],
                    lrow[0:1, h2 * 512:(h2 + 1) * 512], start=True, stop=True),
                    waits=[tr, ps_last[3], t_on2], inc="tPE")
                te = P.op("vector", lambda v, j=j, h2=h2: v.tensor_copy(
                    gamT[:, j * 1024 + h2 * 512: j * 1024 + (h2 + 1) * 512],
                    ps[3][:, h2 * 512:(h2 + 1) * 512]), waits=[tm], inc="tDVE")
                ps_last[3] = te
                toks.append(te)
        return toks

    def ln_apply_loop(dst_slot, src_sp, src_ready, eps, gam_toks):
        toks = []
        for tc in range(TC):
            b = tc % 2
            tk = P.op(trq, lambda g, tc=tc, b=b, a=src_sp: g.dma_start(
                out=attb[b][:, 0:1024], in_=a.ap()[:, tc * 128:(tc + 1) * 128],
                transpose=True),
                waits=list(src_ready) + attb_free[b], inc=f"T{b}", inc_n=16)
            t1 = P.op("vector", lambda v, b=b: v.reduce_sum(
                stat[:, 0:1], attb[b][:, 0:1024], axis=AX.X),
                waits=[tk] + list(gam_toks), inc="tDVE")
            t2 = P.op("vector", lambda v: v.tensor_scalar(
                stat[:, 1:2], stat[:, 0:1], 1.0 / D, None, ALU.mult),
                waits=[t1], inc="tDVE")
            t3 = P.op("vector", lambda v, b=b: v.tensor_scalar(
                fio[2][:, :], attb[b][:, 0:1024], stat[:, 1:2], None,
                ALU.subtract), waits=[t2, fio_free[2]], inc="tDVE")
            t4 = P.op("vector", lambda v, b=b: v.scalar_tensor_tensor(
                attb[b][:, 0:1024], fio[2][:, :], 1.0, fio[2][:, :],
                ALU.mult, ALU.mult, accum_out=stat[:, 2:3]),
                waits=[t3], inc="tDVE")
            attb_free[b] = [t4]
            t5 = P.op("vector", lambda v, eps=eps: v.tensor_scalar(
                stat[:, 3:4], stat[:, 2:3], 1.0 / D, eps, ALU.mult, ALU.add),
                waits=[t4], inc="tDVE")
            t6 = P.op("scalar", lambda s: s.activation(
                stat[:, 4:5], stat[:, 3:4], AF.Ln), waits=[t5], inc="tACT")
            t7 = P.op("scalar", lambda s: s.activation(
                stat[:, 5:6], stat[:, 4:5], AF.Exp, scale=-0.5),
                waits=[t6], inc="tACT")
            t8 = P.op("vector", lambda v: v.tensor_scalar(
                fio[2][:, :], fio[2][:, :], stat[:, 5:6], None, ALU.mult),
                waits=[t7], inc="tDVE")
            t9 = P.op("vector", lambda v: v.tensor_tensor(
                fio[2][:, :], fio[2][:, :], gamT[:, 0:1024], ALU.mult),
                waits=[t8], inc="tDVE")
            t10 = P.op("vector", lambda v, ds=dst_slot, tc=tc: v.tensor_tensor(
                tchunk(ds, tc), fio[2][:, :], gamT[:, 1024:2048], ALU.add),
                waits=[t9], inc="tDVE")
            fio_free[2] = t10
            slot_written(dst_slot, tc % max(KC, TC), t10)
            toks.append(t10)
        return toks

    fio_free[2] = ps_last[2]
    gam_s = build_gam(0, 1, y_spills)
    s_toks = ln_apply_loop(Dv, y_sp, y_spills, 1e-5, gam_s)
    gam_g = build_gam(2, 3, s_toks)
    g_toks = ln_apply_loop(B, n5_sp, n5_spills, 1e-6, gam_g)
    gs_toks = []
    for tc in range(TC):
        gs_toks.append(P.op("scalar", lambda s, tc=tc: s.activation(
            tchunk(B, tc), tchunk(B, tc), AF.Silu), waits=[g_toks[tc]],
            inc="tACT"))

    out_toks = []
    for tc in range(TC):
        b = tc % 2
        tx = P.op(pln, lambda g, tc=tc, b=b: g.dma_start(
            out=fio[b][:, :], in_=x_in.ap()[tc * 128:(tc + 1) * 128, :]),
            waits=[fio_free[b]], inc=f"X{b}", inc_n=16)
        t1 = P.op("vector", lambda v, tc=tc: v.tensor_tensor(
            fio[2][:, :], tchunk(Dv, tc), tchunk(B, tc), ALU.mult),
            waits=[s_toks[tc], gs_toks[tc], fio_free[2]], inc="tDVE")
        t2 = P.op("vector", lambda v, b=b: v.tensor_tensor(
            fio[2][:, :], fio[2][:, :], fio[b][:, :], ALU.add),
            waits=[tx, t1], inc="tDVE")
        fio_free[b] = t2
        td = P.op(pln, lambda g, tc=tc: g.dma_start(
            out=out_t.ap()[tc * 128:(tc + 1) * 128, :], in_=fio[2][:, :]),
            waits=[t2], inc="OUT", inc_n=16)
        fio_free[2] = td
        out_toks.append(td)

    P.op("sync", lambda g: g.nop(), waits=[out_toks[-1]])

    with nc.Block() as block:
        P.finalize(nc, block, stack)
    stack.close()
    return nc


_NAME_MAP = {
    "aWq": "L0_attn_Wq", "aWk": "L0_attn_Wk", "aWv": "L0_attn_Wv",
    "aWo": "L0_attn_Wo", "abq": "L0_attn_bq", "abk": "L0_attn_bk",
    "abv": "L0_attn_bv", "abo": "L0_attn_bo", "agamma": "L0_attn_gamma",
    "abeta": "L0_attn_beta", "few": "L0_ffn_ew", "fact": "L0_ffn_act_w",
    "fWe": "L0_ffn_We", "fbe": "L0_ffn_be", "fWq": "L0_ffn_Wq",
    "fWk": "L0_ffn_Wk", "fWv": "L0_ffn_Wv", "fWo": "L0_ffn_Wo",
    "fln_g": "L0_ffn_ln_g", "fln_b": "L0_ffn_ln_b",
}


def shard_inputs(inputs, T=2048, nb=None):
    x = np.ascontiguousarray(np.asarray(inputs["L0_attn_x"], dtype=np.float32))
    B = x.shape[0] if nb is None else nb
    shared = {k: np.ascontiguousarray(np.asarray(inputs[v], dtype=np.float32))
              for k, v in _NAME_MAP.items()}
    in_maps = []
    for b in range(B):
        d = dict(shared)
        d["x"] = np.ascontiguousarray(x[b, :T])
        in_maps.append(d)
    return in_maps


def kernel(**inputs):
    T = 2048
    nc = build_nc(T)
    in_maps = shard_inputs(inputs, T)
    res = run_bass_kernel_spmd(nc, in_maps, core_ids=list(range(len(in_maps))))
    out = np.stack([np.asarray(r["out"], dtype=np.float32)
                    for r in res.results], axis=0)
    return out
